# version 09
# baseline (speedup 1.0000x reference)
import math
from contextlib import ExitStack

import numpy as np
import ml_dtypes
import concourse.bass as bass
import concourse.mybir as mybir
from concourse.bass_utils import run_bass_kernel_spmd

F32 = mybir.dt.float32
BF16 = mybir.dt.bfloat16
AF = mybir.ActivationFunctionType
ALU = mybir.AluOpType
AX = mybir.AxisListType
BF = ml_dtypes.bfloat16

ENGINES = ("pe", "dve", "act", "pool", "sp")
L1F_DEPTH = 6
L1F_PARTS = ("attn", "ffn")
L1F_STOP = None
SEM_WINDOW = 30000
NEG = -1.0e9



class Op:
    __slots__ = ("idx", "eng", "fn", "deps", "is_dma", "inc_needed", "done_key",
                 "done_val", "tag", "waits", "dma_sem_i")

    def __init__(self, idx, eng, fn, is_dma, tag):
        self.idx = idx
        self.eng = eng
        self.fn = fn
        self.deps = []
        self.is_dma = is_dma
        self.inc_needed = is_dma
        self.done_key = None
        self.done_val = None
        self.tag = tag
        self.waits = []
        self.dma_sem_i = -1


class Prog:
    def __init__(self, nc, n_dma_sems=48):
        self.nc = nc
        self.ops = []
        self.last_write = {}
        self.readers = {}
        self.n_dma_sems = n_dma_sems
        self.dma_rr = 0
        self.out_dma_ops = []

    ALLOWED_FREE = frozenset({"self", "S", "D", "P", "T", "nc", "Th"})

    def _check_fn(self, fn, tag):
        code = getattr(fn, "__code__", None)
        if code is None:
            return
        bad = [v for v in code.co_freevars if v not in self.ALLOWED_FREE]
        if bad:
            raise RuntimeError(
                f"op '{tag}': lambda captures loop variables by reference: {bad}; "
                "bind them as default arguments")

    def emit(self, eng, fn, reads=(), writes=(), tag=""):
        self._check_fn(fn, tag)
        op = Op(len(self.ops), eng, fn, False, tag)
        self._track(op, reads, writes)
        return op

    def dma(self, queue, fn, reads=(), writes=(), tag="", is_output=False):
        self._check_fn(fn, tag)
        op = Op(len(self.ops), queue, fn, True, tag)
        if queue == "pool":
            half = self.n_dma_sems // 2
            op.dma_sem_i = self.dma_rr % half
        else:
            half = self.n_dma_sems // 2
            op.dma_sem_i = half + self.dma_rr % half
        self.dma_rr += 1
        self._track(op, reads, writes)
        if is_output:
            self.out_dma_ops.append(op)
        return op

    ALIASES = {
        "sb2": ("sb2_0", "sb2_1"),
        "sb3": ("sb3_0", "sb3_1"),
        "vtm": ("vtm0", "vtm1"),
        "sc0": ("scS0", "scS1", "scS2", "scS3"),
        "pr2": ("pr2_0", "pr2_1", "pr2_2"),
        "prT": ("prT0", "prT1", "prT2"),
    }

    def _expand(self, keys):
        out = list(keys)
        for k in keys:
            out.extend(self.ALIASES.get(k, ()))
        return out

    def _track(self, op, reads, writes):
        reads = self._expand(reads)
        writes = self._expand(writes)
        lw = self.last_write
        rd = self.readers
        deps = {}
        for r in reads:
            d = lw.get(r)
            if d is not None:
                deps[d.idx] = d
        for w in writes:
            d = lw.get(w)
            if d is not None:
                deps[d.idx] = d
            for d in rd.get(w, ()):
                deps[d.idx] = d
        op.deps = [d for d in deps.values()
                   if not (op.eng == "pe" and d.eng == "pe" and not d.is_dma)]
        for r in reads:
            rd.setdefault(r, []).append(op)
        for w in writes:
            lw[w] = op
            rd[w] = [op]
        self.ops.append(op)

    def materialize_and_run(self, stack):
        nc = self.nc
        for op in self.ops:
            for d in op.deps:
                d.inc_needed = True
        eng_inc = dict.fromkeys(ENGINES, 0)
        dma_targets = [0] * self.n_dma_sems
        for op in self.ops:
            if op.is_dma:
                dma_targets[op.dma_sem_i] += 16
                op.done_key = ("dma", op.dma_sem_i)
                op.done_val = dma_targets[op.dma_sem_i]
            elif op.inc_needed:
                eng_inc[op.eng] += 1
                i = eng_inc[op.eng]
                w = (i - 1) // SEM_WINDOW
                op.done_key = ("eng", op.eng, w)
                op.done_val = i - w * SEM_WINDOW
        self.sems = {}
        for eng in ENGINES:
            n = eng_inc[eng]
            for w in range((n + SEM_WINDOW - 1) // SEM_WINDOW):
                self.sems[("eng", eng, w)] = stack.enter_context(
                    nc.semaphore(f"s_{eng}_{w}"))
        for i in range(self.n_dma_sems):
            if dma_targets[i]:
                self.sems[("dma", i)] = stack.enter_context(nc.semaphore(f"d_{i}"))
        clocks = {eng: {} for eng in ENGINES}
        dma_prev = [0] * self.n_dma_sems
        for op in self.ops:
            clk = clocks[op.eng]
            waits = {}
            if op.is_dma:
                prev = dma_prev[op.dma_sem_i]
                if prev and clk.get(("dma", op.dma_sem_i), 0) < prev:
                    waits[("dma", op.dma_sem_i)] = prev
                dma_prev[op.dma_sem_i] = op.done_val
            for d in op.deps:
                k, v = d.done_key, d.done_val
                if clk.get(k, 0) < v and waits.get(k, 0) < v:
                    waits[k] = v
            for k, v in waits.items():
                clk[k] = max(clk.get(k, 0), v)
            op.waits = sorted(waits.items())
        fin = {}
        for i in range(self.n_dma_sems):
            if dma_targets[i]:
                fin[("dma", i)] = dma_targets[i]
        final_waits = sorted(fin.items())

        block = stack.enter_context(nc.Block())
        per_eng = {e: [] for e in ENGINES}
        for op in self.ops:
            per_eng[op.eng].append(op)
        sems = self.sems

        def runner(ops_list, is_sp):
            def run(eng):
                for op in ops_list:
                    for k, v in op.waits:
                        eng.wait_ge(sems[k], v)
                    inst = op.fn(eng)
                    if op.is_dma:
                        inst.then_inc(sems[("dma", op.dma_sem_i)], 16)
                    elif op.inc_needed:
                        inst.then_inc(sems[op.done_key], 1)
                if is_sp:
                    for k, v in final_waits:
                        eng.wait_ge(sems[k], v)
            return run

        block.tensor(runner(per_eng["pe"], False))
        block.vector(runner(per_eng["dve"], False))
        block.scalar(runner(per_eng["act"], False))
        block.gpsimd(runner(per_eng["pool"], False))
        block.sync(runner(per_eng["sp"], True))



def host_consts(T):
    c = {}
    c["idn"] = np.eye(128, dtype=BF)
    i = np.arange(128)[:, None]
    c["causal"] = np.where(i >= np.arange(128)[None, :], 0.0, NEG).astype(BF)
    jr = np.arange(384)[None, :]
    c["band384"] = np.where((jr >= i) & (jr <= i + 256), 0.0, NEG).astype(BF)
    jr = np.arange(256)[None, :]
    c["band256"] = np.where(jr <= i + 128, 0.0, NEG).astype(BF)
    c["ones_ln"] = np.full((128, 128), 1.0 / 1024.0, dtype=BF)
    bd = np.zeros((128, 128), dtype=np.float32)
    bd[:64, :64] = 1.0
    bd[64:, 64:] = 1.0
    c["blockones"] = bd.astype(BF)
    R = np.zeros((128, 128), dtype=np.float32)
    for h0 in (0, 64):
        for m in range(32):
            R[h0 + m + 32, h0 + m] = -1.0
            R[h0 + m, h0 + m + 32] = 1.0
    c["rot"] = R.astype(BF)
    c["rot32"] = R.astype(np.float32)
    c["blockones32"] = bd.astype(np.float32)
    DIM_HEAD, WINDOW = 64, 256
    half = np.arange(0, DIM_HEAD, 2, dtype=np.float32)
    inv_freq = (1.0 / 10000.0 ** (half / DIM_HEAD)).astype(np.float32)
    t = np.arange(T, dtype=np.float32)
    freqs = t[:, None] * inv_freq[None, :]
    freqs = np.concatenate([freqs, freqs], -1)
    base_scale = ((half + 0.4 * DIM_HEAD) / (1.4 * DIM_HEAD)).astype(np.float32)
    power = (t - T // 2) / (WINDOW // 2)
    scale = base_scale[None, :] ** power[:, None]
    scale = np.concatenate([scale, scale], -1)
    cos, sin = np.cos(freqs), np.sin(freqs)
    qcs = (8.0 * cos * scale).T
    qss = (8.0 * sin * scale).T
    kcs = (cos / scale).T
    kss = (sin / scale).T
    for name, a in (("qcs", qcs), ("qss", qss), ("kcs", kcs), ("kss", kss)):
        c[name] = np.concatenate([a, a], 0).astype(BF)
    return c


def hilbert_runs(M):
    g = int(math.ceil(math.sqrt(M)))
    runs = []
    pos = 0
    r = 0
    while pos < M:
        n = min(g, M - pos)
        full_vals = list(range(r * g, r * g + g))
        if r % 2 == 1:
            full_vals = full_vals[::-1]
        vals = [v for v in full_vals if v < M][:n]
        step = 0 if len(vals) < 2 else vals[1] - vals[0]
        runs.append((vals[0], len(vals), step < 0))
        pos += len(vals)
        r += 1
    return runs



class Core:

    def __init__(self, T, nseq, blocks=(0, 1, 2, 3)):
        self.T = T
        self.nseq = nseq
        self.blocks = blocks
        self.nc = bass.Bass("TRN2", target_bir_lowering=False)
        self.P = Prog(self.nc)
        self.wq_rr = 0
        self.stack = ExitStack()
        self._declare_dram()

    def _declare_dram(self):
        nc, T, nseq = self.nc, self.T, self.nseq
        D = {}

        def inp(name, shape, dt=F32):
            D[name] = nc.dram_tensor(name, shape, dt, kind="ExternalInput").ap()

        inp("x_in", [nseq, 1024, T])
        inp("w0qkvT", [1024, 3072])
        inp("b0qkv", [128, 24])
        inp("w0oT", [1024, 1024])
        inp("b0o", [128, 8])
        inp("w0rT", [1024, 1024])
        inp("w0kT", [1024, 1024])
        inp("w0vT", [1024, 1024])
        inp("mur", [128, 8])
        inp("muk", [128, 8])
        inp("w1qkvT", [1024, 3072])
        inp("w1oT", [1024, 1024])
        inp("fWqkv", [6, 1024, 3072])
        inp("fqs", [128, 6])
        inp("fks", [128, 6])
        inp("fWo", [6, 1024, 1024])
        inp("flng", [6, 128, 8])
        inp("flnb", [6, 128, 8])
        inp("fW1", [6, 1024, 5460])
        inp("fb1a", [6, 128, 22])
        inp("fb1g", [6, 128, 22])
        inp("fW2", [6, 2730, 1024])
        inp("fb2", [6, 128, 8])
        inp("blockones32", [128, 128])
        inp("rot32", [128, 128])
        for n, shape, dt in (
            ("idn", [128, 128], BF16), ("causal", [128, 128], BF16),
            ("band384", [128, 384], BF16), ("band256", [128, 256], BF16),
            ("ones_ln", [128, 128], BF16), ("blockones", [128, 128], BF16),
            ("rot", [128, 128], BF16),
            ("qcs", [128, T], BF16), ("qss", [128, T], BF16),
            ("kcs", [128, T], BF16), ("kss", [128, T], BF16),
        ):
            inp(n, shape, dt)
        D["x_out"] = nc.dram_tensor("x_out", [nseq, 1024, T], F32,
                                    kind="ExternalOutput").ap()
        self.WB = {}
        for n in ("w0qkvT", "w0oT", "w0rT", "w0kT", "w0vT", "w1qkvT", "w1oT",
                  "fWqkv", "fWo", "fW1", "fW2"):
            self.WB[n] = nc.dram_tensor("b_" + n, list(D[n].shape), BF16).ap()
        D["xpark"] = nc.dram_tensor("xpark", [128, 8, T], F32).ap()
        self.D = D

    def alloc(self):
        nc, T = self.nc, self.T
        st = self.stack
        S = {}

        def sb(name, shape, dt):
            S[name] = st.enter_context(nc.sbuf_tensor(name, shape, dt))

        sb("xs", [128, 8, T + 1], F32)
        sb("xb", [128, 8, T], BF16)
        sb("ab", [128, 8, T], BF16)
        sb("sc0", [128, T], F32)
        sb("sc1", [128, T], F32)
        sb("sb0", [128, T], BF16)
        sb("sb1", [128, T], BF16)
        sb("sb2", [128, 2, T], BF16)
        sb("sb3", [128, 2, T], BF16)
        sb("vtm", [128, 2, T // 128, 128], BF16)
        sb("prT", [128, 9, 128], BF16)
        sb("pr2", [128, 3, 384], BF16)
        sb("wk", [128, 2, 8, 128], BF16)
        sb("mx", [128, 8], F32)
        sb("cols", [128, 160], F32)
        for n in ("idn", "causal", "band384", "ones_ln",
                  "blockones", "qcs", "qss", "kcs", "kss"):
            sb("c_" + n, list(self.D[n].shape), BF16)
        for n in ("blockones32", "rot32"):
            sb("c_" + n, list(self.D[n].shape), F32)
        sb("rql", [128, 128], F32)
        sb("rkl", [128, 128], F32)
        self.S = S
        self.pA = st.enter_context(nc.psum_tensor("pA", [128, 512 * 4], F32))
        self.pB = st.enter_context(nc.psum_tensor("pB", [128, 512 * 4], F32))

    def pk(self, which, n0, n1):
        return [f"p{which}{i}" for i in range(n0 // 512, (n1 + 511) // 512)]

    def load_const(self, name):
        P, S, D = self.P, self.S, self.D
        dst = S["c_" + name]
        P.dma("sp", lambda e, d=dst, s=D[name]: e.dma_start(out=d[:, :] if len(d.shape) == 2 else d[:, :, :], in_=s),
              writes=[f"c_{name}"], tag=f"ld_{name}")

    def emit_precast(self):
        P, D = self.P, self.D
        singles = ["w0qkvT", "w0oT", "w0rT", "w0kT", "w0vT", "w1qkvT", "w1oT"]
        for n in singles:
            P.dma("pool", lambda e, s=D[n], d=self.WB[n]: e.dma_start(out=d, in_=s),
                  writes=["b_" + n], tag="precast_" + n)
        for l in range(6):
            for n in ("fWqkv", "fWo", "fW1", "fW2"):
                P.dma("pool", lambda e, s=D[n][l], d=self.WB[n][l]:
                      e.dma_start(out=d, in_=s),
                      writes=[f"b_{n}{l}"], tag=f"precast_{n}{l}")

    def load_cols(self, dst_sl, src_ap, key, tag=""):
        P, S = self.P, self.S
        P.dma("sp", lambda e, s=src_ap, d=S["cols"], sl=dst_sl:
              e.dma_start(out=d[:, sl], in_=s),
              writes=[key], tag=tag or f"cols_{key}")

    def wload(self, buf, w_hbm, rows0, nrows, col0, ncols, tag="", flat=False,
              src_key=None):
        P, S = self.P, self.S
        q = "sp"
        self.wq_rr += 1
        rk = [src_key] if src_key else []
        r0 = rows0
        fv = S["wk"][:, :, :, :].rearrange("p a b c -> p (a b c)") if flat else None
        nfull = 0
        while nfull < len(nrows) and nrows[nfull] == 128:
            nfull += 1
        if nfull > 1 and rows0 == 0:
            src3 = w_hbm[0:128 * nfull, col0:col0 + ncols].rearrange(
                "(k p) o -> p k o", p=128)
            if flat:
                dst3 = bass.AP(fv.tensor, fv.offset,
                               [list(fv.ap[0]), [128, nfull], [1, ncols]])                     if False else None
                dst3 = fv[:, 0:128 * nfull].rearrange("p (k o) -> p k o", o=128)[:, :, 0:ncols]
            else:
                dst3 = S["wk"][:, buf, 0:nfull, 0:ncols]
            P.dma(q, lambda e, s=src3, d=dst3: e.dma_start(out=d, in_=s),
                  reads=rk, writes=(["wk0", "wk1"] if flat else [f"wk{buf}"]),
                  tag=tag or "wload")
            r0 = 128 * nfull
            start_k = nfull
        else:
            start_k = 0
        for k in range(start_k, len(nrows)):
            rc = nrows[k]
            if flat:
                dst = fv[0:rc, 128 * k:128 * k + ncols]
            else:
                dst = S["wk"][0:rc, buf, k, 0:ncols]
            P.dma(q, lambda e, s=w_hbm[r0:r0 + rc, col0:col0 + ncols],
                  d=dst: e.dma_start(out=d, in_=s),
                  reads=rk, writes=(["wk0", "wk1"] if flat else [f"wk{buf}"]),
                  tag=tag or "wload")
            r0 += rc

    def mm(self, psum, which, rhs_of_k, lhsT_of_k, nk, T, rhs_keys, lhsT_keys,
           tag="mm", m=128, n0=0, open_group=True, close_group=True):
        P = self.P
        pkeys = self.pk(which, n0, n0 + T)
        for k in range(nk):
            first = (k == 0) and open_group
            last = (k == nk - 1) and close_group
            for n in range(0, T, 512):
                w = min(512, T - n)
                P.emit("pe", lambda e, a=psum[0:m, n0 + n:n0 + n + w],
                       l=lhsT_of_k(k), r=rhs_of_k(k, n, w), f=first, s=last:
                       e.matmul(a, l, r, start=f, stop=s),
                       reads=list(rhs_keys(k)) + list(lhsT_keys(k)),
                       writes=[pkeys[(n0 + n) // 512 - n0 // 512]], tag=tag)
        return pkeys

    def ln(self, z_of_c, zkey_of_c, T, out_f32_of_c=None, out_bf16_of_c=None,
           outkeys_f32=None, outkeys_bf16=None, g_col=None, b_col=None,
           colkeys=(), tag="ln"):
        P, S = self.P, self.S
        pkA = self.pk("A", 0, T)
        pkB = self.pk("B", 0, T)
        for c in range(8):
            zb = S["sb0" if c % 2 == 0 else "sb1"]
            zq = S["sb2"][:, 0] if c % 2 == 0 else S["sb3"][:, 0]
            zbk = "sb0" if c % 2 == 0 else "sb1"
            zqk = "sb2" if c % 2 == 0 else "sb3"
            P.emit("pool", lambda e, d=zb[:, 0:T], s=z_of_c(c): e.tensor_copy(d, s),
                   reads=[zkey_of_c(c)], writes=[zbk], tag=tag + "_zb")
            P.emit("act", lambda e, d=zq[:, 0:T], s=z_of_c(c):
                   e.activation(d, s, AF.Square),
                   reads=[zkey_of_c(c)], writes=[zqk], tag=tag + "_zq")
            for n in range(0, T, 512):
                w = min(512, T - n)
                P.emit("pe", lambda e, a=self.pA[:, n:n + w], l=S["c_ones_ln"][:, :],
                       r=zb[:, n:n + w], f=(c == 0), s=(c == 7):
                       e.matmul(a, l, r, start=f, stop=s),
                       reads=[zbk, "c_ones_ln"], writes=[pkA[n // 512]], tag=tag + "_sm")
                P.emit("pe", lambda e, a=self.pB[:, n:n + w], l=S["c_ones_ln"][:, :],
                       r=zq[:, n:n + w], f=(c == 0), s=(c == 7):
                       e.matmul(a, l, r, start=f, stop=s),
                       reads=[zqk, "c_ones_ln"], writes=[pkB[n // 512]], tag=tag + "_sq")
        P.emit("act", lambda e: e.activation(S["sc1"][:, 0:T], self.pA[:, 0:T],
                                             AF.Square),
               reads=pkA, writes=["sc1"], tag=tag + "_m2")
        P.emit("dve", lambda e: e.tensor_sub(S["sc1"][:, 0:T], self.pB[:, 0:T],
                                             S["sc1"][:, 0:T]),
               reads=pkB + ["sc1"], writes=["sc1"], tag=tag + "_var")
        P.emit("act", lambda e: e.activation(S["sc1"][:, 0:T], S["sc1"][:, 0:T],
                                             AF.Ln, bias=S["cols"][:, 144:145]),
               reads=["sc1", "colsE"], writes=["sc1"], tag=tag + "_ln")
        P.emit("act", lambda e: e.activation(S["sc1"][:, 0:T], S["sc1"][:, 0:T],
                                             AF.Exp, scale=-0.5),
               reads=["sc1"], writes=["sc1"], tag=tag + "_exp")
        for c in range(8):
            P.emit("dve", lambda e, s=z_of_c(c): e.tensor_sub(S["sc0"][:, 0:T], s,
                                                              self.pA[:, 0:T]),
                   reads=[zkey_of_c(c)] + pkA, writes=["sc0"], tag=tag + "_a1")
            P.emit("dve", lambda e: e.tensor_mul(S["sc0"][:, 0:T], S["sc0"][:, 0:T],
                                                 S["sc1"][:, 0:T]),
                   reads=["sc0", "sc1"], writes=["sc0"], tag=tag + "_a2")
            if out_f32_of_c is not None:
                P.emit("pool", lambda e, d=out_f32_of_c(c):
                       e.tensor_copy(d, S["sc0"][:, 0:T]),
                       reads=["sc0"], writes=[outkeys_f32(c)], tag=tag + "_of")
            if out_bf16_of_c is not None:
                if g_col is not None:
                    P.emit("dve", lambda e, d=out_bf16_of_c(c), g=g_col(c), b=b_col(c):
                           e.tensor_scalar(d, S["sc0"][:, 0:T], g, b, ALU.mult, ALU.add),
                           reads=["sc0"] + list(colkeys), writes=[outkeys_bf16(c)],
                           tag=tag + "_ob")
                else:
                    P.emit("pool", lambda e, d=out_bf16_of_c(c):
                           e.tensor_copy(d, S["sc0"][:, 0:T]),
                           reads=["sc0"], writes=[outkeys_bf16(c)], tag=tag + "_ob")

    def proj(self, w_hbm, wcols, rhs_chunk, rhs_key, evac, nk=8, krows=None,
             piece=128, tag="proj", src_key=None):
        T = self.Tcur
        P, S = self.P, self.S
        if krows is None:
            krows = [128] * nk
        c0, c1 = wcols
        ncols = c1 - c0
        npieces = (ncols + piece - 1) // piece
        for pi in range(npieces):
            pc0 = c0 + pi * piece
            pw = min(piece, c1 - pc0)
            buf = pi % 2
            self.wload(buf, w_hbm, 0, krows, pc0, pw, tag=tag + "_w",
                       src_key=src_key)
            for oc in range(0, pw, 128):
                m = min(128, pw - oc)
                og = (pi * piece + oc) // 128
                which = "A" if og % 2 == 0 else "B"
                ps = self.pA if which == "A" else self.pB
                pkeys = self.mm(
                    ps, which,
                    rhs_of_k=rhs_chunk,
                    lhsT_of_k=lambda k, b=buf, o=oc, mm_=m:
                        S["wk"][0:krows[k], b, k, o:o + mm_],
                    nk=nk, T=T,
                    rhs_keys=rhs_key,
                    lhsT_keys=lambda k, b=buf: [f"wk{b}"],
                    tag=tag, m=m)
                evac(og, ps, which, pkeys, m)

    def l0_attn(self):
        T = self.Tcur
        P, S, D = self.P, self.S, self.D
        nt = T // 128
        for c in range(8):
            P.emit("pool", lambda e, c_=c: e.tensor_copy(
                S["xb"][:, c_, 0:T], S["xs"][:, c_, 1:T + 1]),
                reads=[f"xs{c}"], writes=[f"xb{c}"], tag="l0a_cast")
        rhs = lambda k, n, w: S["xb"][:, k, n:n + w]
        rhsk = lambda k: [f"xb{k}"]

        for j in range(8):
            pj = j % 2
            role_dst = {"q": (S["sb0"][:, 0:T], "sb0"),
                        "k": (S["sb1"][:, 0:T], "sb1"),
                        "v": (S["sb2"][:, pj, 0:T], f"sb2_{pj}")}
            for role, colbase, rname, biascol, rscale in (
                    ("q", 128 * j, "q", 146 + j, 0.125),
                    ("k", 1024 + 128 * j, "k", 8 + j, 1.0),
                    ("v", 2048 + 128 * j, "v", 16 + j, 1.0)):
                buf = 0 if role != "k" else 1
                self.wload(buf, self.WB["w0qkvT"], 0, [128] * 8, colbase, 128,
                           tag="l0a_w" + role, src_key="b_w0qkvT")
                which = "A" if role != "k" else "B"
                ps = self.pA if which == "A" else self.pB
                pkeys = self.mm(ps, which, rhs, lambda k, b=buf: S["wk"][:, b, k, 0:128],
                                nk=8, T=T, rhs_keys=rhsk,
                                lhsT_keys=lambda k, b=buf: [f"wk{b}"], tag="l0a_qkv")
                P.emit("act", lambda e, d=role_dst[rname][0], p=ps[:, 0:T],
                       bc=S["cols"][:, biascol:biascol + 1], rs=rscale:
                       e.activation(d, p, AF.Identity, bias=bc, scale=rs),
                       reads=pkeys + ["colsA"], writes=[role_dst[rname][1]],
                       tag="l0a_ev" + role)
            for t0 in range(0, nt, 4):
                ntile = min(4, nt - t0)
                for i in range(ntile):
                    P.emit("pe", lambda e, i_=i, t_=t0 + i, v2=S["sb2"][:, pj],
                           o=self.pB[:, 0:512].bitcast(BF16)[:, 128 * i:128 * (i + 1)]:
                           e.transpose(o, v2[:, 128 * t_:128 * (t_ + 1)],
                                       S["c_idn"][:, :]),
                           reads=[f"sb2_{pj}", "c_idn"], writes=["pB0"], tag="l0a_vtr")
                P.emit("dve", lambda e, t0_=t0, n_=ntile, vt=S["vtm"][:, pj]:
                       e.tensor_copy(vt[:, t0_:t0_ + n_, :].rearrange("p a b -> p (a b)"),
                                     self.pB[:, 0:512].bitcast(BF16)[:, 0:128 * n_]),
                       reads=["pB0"], writes=[f"vtm{pj}"], tag="l0a_vev")
            units = [(h, qt) for h in (0, 1) for qt in range(nt)]
            nu = len(units)
            rr = [0]

            def stageA(u):
                h, qt = units[u]
                hb = 64 * h
                K = 128 * (qt + 1)
                sck = "sc0" if u % 2 == 0 else "sc1"
                sc = S[sck]
                for n in range(0, K, 512):
                    w = min(512, K - n)
                    b = rr[0] % 4
                    rr[0] += 1
                    pak = f"pA{b}"
                    P.emit("pe", lambda e, n_=n, w_=w, qt_=qt, hb_=hb, b_=b:
                           e.matmul(self.pA[:, 512 * b_:512 * b_ + w_],
                                    S["sb0"][hb_:hb_ + 64, 128 * qt_:128 * (qt_ + 1)],
                                    S["sb1"][hb_:hb_ + 64, n_:n_ + w_],
                                    start=True, stop=True),
                           reads=["sb0", "sb1"], writes=[pak], tag="l0a_sc")
                    last = (n + w == K)
                    if last and w > 128:
                        P.emit("dve", lambda e, n_=n, w_=w, b_=b, d=sc:
                               e.tensor_copy(d[:, n_:n_ + w_ - 128],
                                             self.pA[:, 512 * b_:512 * b_ + w_ - 128]),
                               reads=[pak], writes=[sck], tag="l0a_sev")
                    elif not last:
                        P.emit("dve", lambda e, n_=n, w_=w, b_=b, d=sc:
                               e.tensor_copy(d[:, n_:n_ + w_],
                                             self.pA[:, 512 * b_:512 * b_ + w_]),
                               reads=[pak], writes=[sck], tag="l0a_sev")
                    if last:
                        P.emit("dve", lambda e, K_=K, w_=w, b_=b, d=sc:
                               e.tensor_add(d[:, K_ - 128:K_],
                                            self.pA[:, 512 * b_ + w_ - 128:512 * b_ + w_],
                                            S["c_causal"][:, :]),
                               reads=[pak, "c_causal"], writes=[sck], tag="l0a_sevd")

            def stageBC(u):
                h, qt = units[u]
                K = 128 * (qt + 1)
                ms = u % 4
                sck = "sc0" if u % 2 == 0 else "sc1"
                sc = S[sck]
                prk = f"sb3_{u % 2}"
                pr = S["sb3"][:, u % 2]
                P.emit("dve", lambda e, K_=K, s=sc, ms_=ms:
                       e.tensor_reduce(S["mx"][:, ms_:ms_ + 1], s[:, 0:K_], AX.X,
                                       ALU.max, negate=True),
                       reads=[sck], writes=[f"mx{ms}"], tag="l0a_max")
                P.emit("act", lambda e, K_=K, s=sc, ms_=ms:
                       e.activation(s[:, 0:K_], s[:, 0:K_], AF.Exp,
                                    bias=S["mx"][:, ms_:ms_ + 1],
                                    accum_out=S["mx"][:, 4 + ms_:5 + ms_]),
                       reads=[sck, f"mx{ms}"],
                       writes=[sck, f"mx{4 + ms}"], tag="l0a_exp")
                P.emit("dve", lambda e, ms_=ms:
                       e.reciprocal(S["mx"][:, 4 + ms_:5 + ms_],
                                    S["mx"][:, 4 + ms_:5 + ms_]),
                       reads=[f"mx{4 + ms}"], writes=[f"mx{4 + ms}"], tag="l0a_rc")
                P.emit("dve", lambda e, K_=K, s=sc, p=pr, ms_=ms:
                       e.tensor_scalar_mul(p[:, 0:K_], s[:, 0:K_],
                                           S["mx"][:, 4 + ms_:5 + ms_]),
                       reads=[sck, f"mx{4 + ms}"], writes=[prk], tag="l0a_nrm")

            def stageDE(u):
                h, qt = units[u]
                hb = 64 * h
                nk_ = qt + 1
                prk = "sb3" if u % 2 == 0 else "sb2"
                pr = S["sb3"][:, u % 2]
                pvb = 2 + u % 2
                pvk = f"pB{pvb}"
                pvview = self.pB[0:64, 512 * pvb:512 * pvb + 128]
                for t0 in range(0, nk_, 4):
                    ntile = min(4, nk_ - t0)
                    g = t0 // 4
                    pb = g % 2
                    pbk = f"pB{pb}"
                    pslot = 4 * (g % 2)
                    pbview = self.pB[:, 512 * pb:512 * (pb + 1)].bitcast(BF16)
                    for i in range(ntile):
                        P.emit("pe", lambda e, i_=i, t_=t0 + i, v=pbview, p=pr:
                               e.transpose(v[:, 128 * i_:128 * (i_ + 1)],
                                           p[:, 128 * t_:128 * (t_ + 1)],
                                           S["c_idn"][:, :]),
                               reads=[prk, "c_idn"], writes=[pbk], tag="l0a_ptr")
                    P.emit("dve", lambda e, ps_=pslot, n_=ntile, v=pbview:
                           e.tensor_copy(S["prT"][:, ps_:ps_ + n_, :]
                                         .rearrange("p a b -> p (a b)"),
                                         v[:, 0:128 * n_]),
                           reads=[pbk], writes=[f"prT{g % 2}"], tag="l0a_pev")
                    for i in range(ntile):
                        kt = t0 + i
                        P.emit("pe", lambda e, i_=i, kt_=kt, ps_=pslot, hb_=hb,
                               v=pvview, vt=S["vtm"][:, pj],
                               f=(kt == 0), s=(kt == nk_ - 1):
                               e.matmul(v, vt[:, kt_, hb_:hb_ + 64],
                                        S["prT"][:, ps_ + i_, :], start=f, stop=s),
                               reads=[f"vtm{pj}", f"prT{g % 2}"], writes=[pvk],
                               tag="l0a_pv")
                jj = units[u]
                P.emit("act", lambda e, j_=j, hb_=hb, qt_=qt, v=pvview:
                       e.activation(S["ab"][hb_:hb_ + 64, j_, 128 * qt_:128 * (qt_ + 1)],
                                    v, AF.Copy),
                       reads=[pvk], writes=[f"ab{j}"], tag="l0a_oev")

            for u in range(nu + 1):
                if u < nu:
                    stageA(u)
                    stageBC(u)
                if u >= 1:
                    stageDE(u - 1)

        rhs2 = lambda k, n, w: S["ab"][:, k, n:n + w]
        rhsk2 = lambda k: [f"ab{k}"]

        def evac(og, ps, which, pkeys, m):
            P.emit("dve", lambda e, og_=og, p=ps:
                   e.scalar_tensor_tensor(S["xs"][:, og_, 1:T + 1], p[:, 0:T],
                                          S["cols"][:, 24 + og_:25 + og_],
                                          S["xs"][:, og_, 1:T + 1],
                                          ALU.add, ALU.add),
                   reads=pkeys + [f"xs{og}", "colsA"], writes=[f"xs{og}"],
                   tag="l0a_wo_ev")

        self.proj(self.WB["w0oT"], (0, 1024), rhs2, rhsk2, evac, tag="l0a_wo",
                  src_key="b_w0oT")
        self.ln(lambda c: S["xs"][:, c, 1:T + 1], lambda c: f"xs{c}", T,
                out_f32_of_c=lambda c: S["xs"][:, c, 1:T + 1],
                outkeys_f32=lambda c: f"xs{c}", tag="l0a_ln")

    def l0_ffn(self):
        T = self.Tcur
        P, S, D = self.P, self.S, self.D

        def build_mix(mu_off, om_off, tag):
            for c in range(8):
                P.emit("dve", lambda e, c_=c, o_=om_off: e.tensor_scalar_mul(
                    S["sc0"][:, 0:T], S["xs"][:, c_, 0:T],
                    S["cols"][:, o_ + c_:o_ + c_ + 1]),
                    reads=[f"xs{c}", "colsA"], writes=["sc0"], tag=tag + "_p1")
                P.emit("dve", lambda e, c_=c, m_=mu_off: e.scalar_tensor_tensor(
                    S["xb"][:, c_, 0:T], S["xs"][:, c_, 1:T + 1],
                    S["cols"][:, m_ + c_:m_ + c_ + 1],
                    S["sc0"][:, 0:T], ALU.mult, ALU.add),
                    reads=[f"xs{c}", "colsA", "sc0"], writes=[f"xb{c}"],
                    tag=tag + "_p2")

        rhs = lambda k, n, w: S["xb"][:, k, n:n + w]
        rhsk = lambda k: [f"xb{k}"]

        build_mix(40, 56, "l0f_xk")

        def evac_k(og, ps, which, pkeys, m):
            P.emit("dve", lambda e, p=ps: e.tensor_scalar_max(
                S["sc1"][:, 0:T], p[:, 0:T], 0.0),
                reads=pkeys, writes=["sc1"], tag="l0f_kre")
            P.emit("act", lambda e, og_=og: e.activation(
                S["ab"][:, og_, 0:T], S["sc1"][:, 0:T], AF.Square),
                reads=["sc1"], writes=[f"ab{og}"], tag="l0f_ksq")

        self.proj(self.WB["w0kT"], (0, 1024), rhs, rhsk, evac_k, tag="l0f_k",
                  src_key="b_w0kT")

        build_mix(32, 48, "l0f_xr")
        rhs2 = lambda k, n, w: S["ab"][:, k, n:n + w]
        rhsk2 = lambda k: [f"ab{k}"]
        for og in range(8):
            buf = og % 2
            self.wload(buf, self.WB["w0rT"], 0, [128] * 8, 128 * og, 128,
                       tag="l0f_wr", src_key="b_w0rT")
            pk1 = self.mm(self.pA, "A", rhs,
                          lambda k, b=buf: S["wk"][:, b, k, 0:128], nk=8, T=T,
                          rhs_keys=rhsk, lhsT_keys=lambda k, b=buf: [f"wk{b}"],
                          tag="l0f_r")
            P.emit("act", lambda e: e.activation(S["sc0"][:, 0:T],
                                                 self.pA[:, 0:T], AF.Sigmoid),
                   reads=pk1, writes=["sc0"], tag="l0f_sig")
            self.wload(buf, self.WB["w0vT"], 0, [128] * 8, 128 * og, 128,
                       tag="l0f_wv", src_key="b_w0vT")
            pk2 = self.mm(self.pB, "B", rhs2,
                          lambda k, b=buf: S["wk"][:, b, k, 0:128], nk=8, T=T,
                          rhs_keys=rhsk2, lhsT_keys=lambda k, b=buf: [f"wk{b}"],
                          tag="l0f_v")
            P.emit("dve", lambda e: e.tensor_mul(S["sc1"][:, 0:T],
                                                 self.pB[:, 0:T], S["sc0"][:, 0:T]),
                   reads=pk2 + ["sc0"], writes=["sc1"], tag="l0f_rv")
            P.emit("dve", lambda e, og_=og: e.tensor_add(
                S["xs"][:, og_, 1:T + 1], S["xs"][:, og_, 1:T + 1], S["sc1"][:, 0:T]),
                reads=["sc1", f"xs{og}"], writes=[f"xs{og}"], tag="l0f_res")
        self.ln(lambda c: S["xs"][:, c, 1:T + 1], lambda c: f"xs{c}", T,
                out_f32_of_c=lambda c: S["xs"][:, c, 1:T + 1],
                outkeys_f32=lambda c: f"xs{c}", tag="l0f_ln")

    def l1_attn(self):
        T = self.Tcur
        P, S, D = self.P, self.S, self.D
        ns = T // 128
        g = int(math.ceil(math.sqrt(T)))
        nfull = T // g
        tail = T - nfull * g
        assert nfull % 2 == 0, "tail block must be even-indexed"
        pstep_xs = 8 * (T + 1)
        pstep_xb = 8 * T
        neven = nfull // 2
        for c in range(8):
            base_s = c * (T + 1) + 1
            base_d = c * T
            P.emit("pool", lambda e, bs=base_s, bd=base_d, g_=g, ne=neven,
                   pb=pstep_xb, px=pstep_xs:
                   e.tensor_copy(bass.AP(S["xb"], bd, [[pb, 128], [2 * g_, ne], [1, g_]]),
                                 bass.AP(S["xs"], bs, [[px, 128], [2 * g_, ne], [1, g_]])),
                   reads=[f"xs{c}"], writes=[f"xb{c}"], tag="l1a_pme")
            P.emit("pool", lambda e, bs=base_s, bd=base_d, g_=g, ne=neven,
                   pb=pstep_xb, px=pstep_xs:
                   e.tensor_copy(bass.AP(S["xb"], bd + g_, [[pb, 128], [2 * g_, ne], [1, g_]]),
                                 bass.AP(S["xs"], bs + 2 * g_ - 1, [[px, 128], [2 * g_, ne], [-1, g_]])),
                   reads=[f"xs{c}"], writes=[f"xb{c}"], tag="l1a_pmo")
            if tail:
                P.emit("pool", lambda e, c_=c, g_=g, nf=nfull, tl=tail:
                       e.tensor_copy(S["xb"][:, c_, nf * g_:nf * g_ + tl],
                                     S["xs"][:, c_, 1 + nf * g_:1 + nf * g_ + tl]),
                       reads=[f"xs{c}"], writes=[f"xb{c}"], tag="l1a_pmt")
        rhs = lambda k, n, w: S["xb"][:, k, n:n + w]
        rhsk = lambda k: [f"xb{k}"]

        for j in range(8):
            pj = j % 2
            role_dst = {"sb0": (S["sb0"][:, 0:T], "sb0"),
                        "sb1": (S["sb1"][:, 0:T], "sb1"),
                        "sb2": (S["sb2"][:, pj, 0:T], f"sb2_{pj}")}
            for role, colbase, dstname in (("q", 128 * j, "sb0"),
                                           ("k", 1024 + 128 * j, "sb1"),
                                           ("v", 2048 + 128 * j, "sb2")):
                buf = 0 if role != "k" else 1
                self.wload(buf, self.WB["w1qkvT"], 0, [128] * 8, colbase, 128,
                           tag="l1a_w" + role, src_key="b_w1qkvT")
                which = "A" if role != "k" else "B"
                ps = self.pA if which == "A" else self.pB
                pkeys = self.mm(ps, which, rhs, lambda k, b=buf: S["wk"][:, b, k, 0:128],
                                nk=8, T=T, rhs_keys=rhsk,
                                lhsT_keys=lambda k, b=buf: [f"wk{b}"], tag="l1a_qkv")
                P.emit("act", lambda e, d=role_dst[dstname][0], p=ps[:, 0:T],
                       rs=(0.125 if role == "q" else 1.0):
                       e.activation(d, p, AF.Identity, scale=rs),
                       reads=pkeys, writes=[role_dst[dstname][1]],
                       tag="l1a_ev" + role)
            for s4 in range(0, ns, 4):
                nseg = min(4, ns - s4)
                for i in range(nseg):
                    P.emit("pe", lambda e, i_=i, s_=s4 + i, v2=S["sb2"][:, pj]:
                           e.transpose(self.pA[:, 0:512].bitcast(BF16)[0:64, 128 * i_:128 * (i_ + 1)],
                                       v2[:, 128 * s_:128 * (s_ + 1):2],
                                       S["c_idn"][:, :]),
                           reads=[f"sb2_{pj}", "c_idn"], writes=["pA0"], tag="l1a_vtr")
                P.emit("dve", lambda e, s4_=s4, n_=nseg, vt=S["vtm"][:, pj]:
                       e.tensor_copy(vt[0:64, s4_:s4_ + n_, :].rearrange("p a b -> p (a b)"),
                                     self.pA[:, 0:512].bitcast(BF16)[0:64, 0:128 * n_]),
                       reads=["pA0"], writes=[f"vtm{pj}"], tag="l1a_vev")
            units = [(h, s) for h in (0, 1) for s in range(ns)]
            nu = len(units)

            def stageA(u):
                h, s = units[u]
                hb = 64 * h
                sbank = 512 * (u % 4)
                pak = f"pB{u % 4}"
                P.emit("pe", lambda e, s_=s, hb_=hb, n_=sbank:
                       e.matmul(self.pB[:, n_:n_ + 64],
                                S["sb0"][hb_:hb_ + 64, 128 * s_:128 * (s_ + 1)],
                                S["sb1"][hb_:hb_ + 64, 128 * s_:128 * (s_ + 1):2],
                                start=True, stop=True),
                       reads=["sb0", "sb1"], writes=[pak], tag="l1a_sc")

            def stageBC(u):
                ms = u % 4
                u3 = u % 3
                sbank = 512 * ms
                pak = f"pB{ms}"
                prk = f"pr2_{u3}"
                P.emit("dve", lambda e, n_=sbank, ms_=ms:
                       e.tensor_reduce(S["mx"][:, ms_:ms_ + 1],
                                       self.pB[:, n_:n_ + 64], AX.X,
                                       ALU.max, negate=True),
                       reads=[pak], writes=[f"mx{ms}"], tag="l1a_max")
                P.emit("act", lambda e, n_=sbank, ms_=ms, b_=u3:
                       e.activation(S["pr2"][:, b_, 0:64], self.pB[:, n_:n_ + 64],
                                    AF.Exp, bias=S["mx"][:, ms_:ms_ + 1],
                                    accum_out=S["mx"][:, 4 + ms_:5 + ms_]),
                       reads=[pak, f"mx{ms}"],
                       writes=[prk, f"mx{4 + ms}"], tag="l1a_exp")
                P.emit("dve", lambda e, ms_=ms:
                       e.reciprocal(S["mx"][:, 4 + ms_:5 + ms_],
                                    S["mx"][:, 4 + ms_:5 + ms_]),
                       reads=[f"mx{4 + ms}"], writes=[f"mx{4 + ms}"], tag="l1a_rc")
                P.emit("dve", lambda e, ms_=ms, b_=u3:
                       e.tensor_scalar_mul(S["pr2"][:, b_, 0:64],
                                           S["pr2"][:, b_, 0:64],
                                           S["mx"][:, 4 + ms_:5 + ms_]),
                       reads=[prk, f"mx{4 + ms}"], writes=[prk], tag="l1a_nrm")

            def stageD(u):
                ms = u % 4
                u3 = u % 3
                prk = f"pr2_{u3}"
                pak = f"pB{ms}"
                pbview = self.pB[:, 512 * ms:512 * (ms + 1)].bitcast(BF16)
                P.emit("pe", lambda e, b_=u3, v=pbview:
                       e.transpose(v[0:64, 0:128],
                                   S["pr2"][:, b_, 0:64], S["c_idn"][:, :]),
                       reads=[prk, "c_idn"], writes=[pak], tag="l1a_ptr")
                P.emit("dve", lambda e, b_=u3, v=pbview:
                       e.tensor_copy(S["prT"][0:64, b_, :], v[0:64, 0:128]),
                       reads=[pak], writes=[f"prT{u3}"], tag="l1a_pev")

            def stageE(u):
                h, s = units[u]
                hb = 64 * h
                ms = u % 4
                u3 = u % 3
                pak = f"pB{ms}"
                pvview = self.pB[0:64, 512 * ms + 384:512 * ms + 512]
                P.emit("pe", lambda e, s_=s, hb_=hb, b_=u3, v=pvview,
                       vt=S["vtm"][:, pj]:
                       e.matmul(v, vt[0:64, s_, hb_:hb_ + 64],
                                S["prT"][0:64, b_, :], start=True, stop=True),
                       reads=[f"vtm{pj}", f"prT{u3}"], writes=[pak], tag="l1a_pv")
                P.emit("act", lambda e, j_=j, hb_=hb, s_=s, v=pvview:
                       e.activation(S["ab"][hb_:hb_ + 64, j_, 128 * s_:128 * (s_ + 1)],
                                    v, AF.Copy),
                       reads=[pak], writes=[f"ab{j}"], tag="l1a_oev")

            for u in range(nu + 4):
                if u >= 4:
                    stageE(u - 4)
                if u < nu:
                    stageA(u)
                    stageBC(u)
                if u >= 2 and u - 2 < nu:
                    stageD(u - 2)
        rhs2 = lambda k, n, w: S["ab"][:, k, n:n + w]
        rhsk2 = lambda k: [f"ab{k}"]

        def evac(og, ps, which, pkeys, m):
            P.emit("dve", lambda e, og_=og, p=ps:
                   e.tensor_add(S["xs"][:, og_, 1:T + 1], S["xs"][:, og_, 1:T + 1],
                                p[:, 0:T]),
                   reads=pkeys + [f"xs{og}"], writes=[f"xs{og}"], tag="l1a_wo_ev")

        self.proj(self.WB["w1oT"], (0, 1024), rhs2, rhsk2, evac, tag="l1a_wo",
                  src_key="b_w1oT")
        self.ln(lambda c: S["xs"][:, c, 1:T + 1], lambda c: f"xs{c}", T,
                out_f32_of_c=lambda c: S["xs"][:, c, 1:T + 1],
                outkeys_f32=lambda c: f"xs{c}", tag="l1a_ln")

    def l1_ffn(self):
        T = self.Tcur
        P, S, D = self.P, self.S, self.D
        nt = T // 128
        P.dma("sp", lambda e: e.dma_start(out=D["xpark"][:, :, 0:T],
                                          in_=S["xs"][:, :, 1:T + 1]),
              reads=[f"xs{c}" for c in range(8)], writes=["xpark"], tag="park")
        for l in range(L1F_DEPTH):
            self._l1f_layer(l)
        if L1F_STOP is not None:
            return
        for c in range(8):
            P.dma("sp", lambda e, c_=c: e.dma_start(out=S["sc0"][:, 0:T],
                                                    in_=D["xpark"][:, c_, 0:T]),
                  reads=["xpark"], writes=["sc0"], tag="unpark")
            P.emit("dve", lambda e, c_=c: e.tensor_add(
                S["xs"][:, c_, 1:T + 1], S["xs"][:, c_, 1:T + 1], S["sc0"][:, 0:T]),
                reads=["sc0", f"xs{c}"], writes=[f"xs{c}"], tag="unpark_add")
        self.ln(lambda c: S["xs"][:, c, 1:T + 1], lambda c: f"xs{c}", T,
                out_f32_of_c=lambda c: S["xs"][:, c, 1:T + 1],
                outkeys_f32=lambda c: f"xs{c}", tag="l1f_lnF")

    def _l1f_layer(self, l):
        T = self.Tcur
        P, S, D = self.P, self.S, self.D
        nt = T // 128
        P.dma("sp", lambda e, l_=l: e.dma_start(out=S["cols"][:, 76:84], in_=D["flng"][l_]),
              writes=["colsL"], tag="ld_lng")
        P.dma("sp", lambda e, l_=l: e.dma_start(out=S["cols"][:, 84:92], in_=D["flnb"][l_]),
              writes=["colsL"], tag="ld_lnb")
        P.dma("sp", lambda e, l_=l: e.dma_start(out=S["cols"][:, 92:100], in_=D["fb2"][l_]),
              writes=["colsL"], tag="ld_b2")
        P.dma("sp", lambda e, l_=l: e.dma_start(out=S["cols"][:, 100:122], in_=D["fb1a"][l_]),
              writes=["colsB"], tag="ld_b1a")
        P.dma("sp", lambda e, l_=l: e.dma_start(out=S["cols"][:, 122:144], in_=D["fb1g"][l_]),
              writes=["colsB"], tag="ld_b1g")
        P.emit("dve", lambda e, l_=l: e.tensor_scalar_mul(
            S["rql"][:, :], S["c_rot32"][:, :], S["cols"][:, 64 + l_:65 + l_]),
            reads=["c_rot32", "colsA"], writes=["rql"], tag="mk_rql")
        P.emit("dve", lambda e, l_=l: e.tensor_scalar_mul(
            S["rkl"][:, :], S["c_rot32"][:, :], S["cols"][:, 70 + l_:71 + l_]),
            reads=["c_rot32", "colsA"], writes=["rkl"], tag="mk_rkl")
        if "attn" not in L1F_PARTS:
            self._l1f_ffn_part(l)
            return
        for c in range(8):
            src = (lambda c_: S["xs"][:, c_, 1:T + 1]) if c < 4 else \
                  (lambda c_: S["xs"][:, c_, 0:T])
            P.emit("pool", lambda e, c_=c, s=src(c): e.tensor_copy(
                S["xb"][:, c_, 0:T], s),
                reads=[f"xs{c}"], writes=[f"xb{c}"], tag="l1f_sh")
        rhs = lambda k, n, w: S["xb"][:, k, n:n + w]
        rhsk = lambda k: [f"xb{k}"]
        wq = self.WB["fWqkv"][l]
        wqk = f"b_fWqkv{l}"

        def prep_gen(j):
            pj = j % 2
            for role in ("q", "k", "v"):
                colbase = {"q": 128 * j, "k": 1024 + 128 * j,
                           "v": 2048 + 128 * j}[role]
                buf = 0 if role in ("q", "v") else 1
                self.wload(buf, wq, 0, [128] * 8, colbase, 128,
                           tag="l1f_w" + role, src_key=wqk)
                yield
                pkeys = self.mm(self.pA, "A", rhs,
                                lambda k, b=buf: S["wk"][:, b, k, 0:128],
                                nk=8, T=T, rhs_keys=rhsk,
                                lhsT_keys=lambda k, b=buf: [f"wk{b}"], tag="l1f_qkv")
                yield
                if role == "v":
                    P.emit("act", lambda e: e.activation(S["sb0"][:, 0:T],
                                                         self.pA[:, 0:T], AF.Copy),
                           reads=pkeys, writes=["sb0"], tag="l1f_evv")
                    yield
                    for s4 in range(0, nt, 4):
                        nseg = min(4, nt - s4)
                        for i in range(nseg):
                            P.emit("pe", lambda e, i_=i, t_=s4 + i:
                                   e.transpose(self.pA[:, 0:512].bitcast(BF16)
                                               [:, 128 * i_:128 * (i_ + 1)],
                                               S["sb0"][:, 128 * t_:128 * (t_ + 1)],
                                               S["c_idn"][:, :]),
                                   reads=["sb0", "c_idn"], writes=["pA0"],
                                   tag="l1f_vtr")
                        P.emit("dve", lambda e, s4_=s4, n_=nseg, vt=S["vtm"][:, pj]:
                               e.tensor_copy(vt[:, s4_:s4_ + n_, :]
                                             .rearrange("p a b -> p (a b)"),
                                             self.pA[:, 0:512].bitcast(BF16)[:, 0:128 * n_]),
                               reads=["pA0"], writes=[f"vtm{pj}"], tag="l1f_vev")
                        yield
                    continue
                pkA = self.pk("A", 0, T)
                P.emit("act", lambda e: e.activation(S["sc0"][:, 0:T],
                                                     self.pA[:, 0:T], AF.Copy),
                       reads=pkeys, writes=["sc0"], tag="l1f_evqk")
                yield
                rkey = "rql" if role == "q" else "rkl"
                for n in range(0, T, 512):
                    w = min(512, T - n)
                    P.emit("pe", lambda e, n_=n, w_=w, rm=S[rkey]:
                           e.matmul(self.pA[:, n_:n_ + w_], rm[:, :],
                                    S["sc0"][:, n_:n_ + w_], start=True, stop=True),
                           reads=["sc0", rkey], writes=[pkA[n // 512]],
                           tag="l1f_rot")
                yield
                scol = S["cols"][:, 64 + l:65 + l] if role == "q" else \
                    S["cols"][:, 70 + l:71 + l]
                sstab = S["c_qss"] if role == "q" else S["c_kss"]
                cstab = S["c_qcs"] if role == "q" else S["c_kcs"]
                sskey = "c_qss" if role == "q" else "c_kss"
                cskey = "c_qcs" if role == "q" else "c_kcs"
                P.emit("dve", lambda e, st=sstab: e.tensor_mul(
                    S["sc1"][:, 0:T], self.pA[:, 0:T], st[:, 0:T]),
                    reads=pkA + [sskey], writes=["sc1"], tag="l1f_t5")
                yield
                P.emit("act", lambda e: e.activation(S["sb0"][:, 0:T],
                                                     S["sc0"][:, 0:T], AF.Square),
                       reads=["sc0"], writes=["sb0"], tag="l1f_sq")
                yield
                P.emit("dve", lambda e, sc=scol, ct=cstab: e.scalar_tensor_tensor(
                    S["sc0"][:, 0:T], S["sc0"][:, 0:T], sc, ct[:, 0:T],
                    ALU.mult, ALU.mult),
                    reads=["sc0", "colsA", cskey], writes=["sc0"], tag="l1f_t6")
                yield
                P.emit("pool", lambda e: e.tensor_add(
                    S["sc0"][:, 0:T], S["sc0"][:, 0:T], S["sc1"][:, 0:T]),
                    reads=["sc0", "sc1"], writes=["sc0"], tag="l1f_u")
                yield
                for n in range(0, T, 512):
                    w = min(512, T - n)
                    P.emit("pe", lambda e, n_=n, w_=w:
                           e.matmul(self.pA[:, n_:n_ + w_], S["c_blockones"][:, :],
                                    S["sb0"][:, n_:n_ + w_], start=True, stop=True),
                           reads=["sb0", "c_blockones"], writes=[pkA[n // 512]],
                           tag="l1f_nsum")
                yield
                P.emit("act", lambda e: e.activation(S["sc1"][:, 0:T],
                                                     self.pA[:, 0:T], AF.Ln,
                                                     bias=S["cols"][:, 145:146]),
                       reads=pkA + ["colsE"], writes=["sc1"], tag="l1f_nln")
                P.emit("act", lambda e: e.activation(S["sc1"][:, 0:T],
                                                     S["sc1"][:, 0:T], AF.Exp,
                                                     scale=-0.5),
                       reads=["sc1"], writes=["sc1"], tag="l1f_nexp")
                yield
                dst = "sb2" if role == "q" else "sb3"
                P.emit("dve", lambda e, d=S[dst][:, pj]: e.tensor_mul(
                    d[:, 0:T], S["sc0"][:, 0:T], S["sc1"][:, 0:T]),
                    reads=["sc0", "sc1"], writes=[f"{dst}_{pj}"], tag="l1f_" + dst)
                yield

        def attention(j, gen):
            pj = j % 2
            units = [(h, qt) for h in (0, 1) for qt in range(nt)]
            nu = len(units)
            qb = S["sb2"][:, pj]
            kb = S["sb3"][:, pj]
            qbk, kbk = f"sb2_{pj}", f"sb3_{pj}"
            vt = S["vtm"][:, pj]
            vtk = f"vtm{pj}"

            def stageA(u):
                h, qt = units[u]
                hb = 64 * h
                k0 = max(0, 128 * qt - 256)
                kw = 128 * (qt + 1) - k0
                sbank = 512 * (u % 4)
                pak = f"pB{u % 4}"
                mask = {128: S["c_causal"], 256: S["c_band384"][:, 128:384],
                        384: S["c_band384"]}[kw]
                maskk = {128: "c_causal", 256: "c_band384", 384: "c_band384"}[kw]
                P.emit("pe", lambda e, qt_=qt, hb_=hb, n_=sbank, k0_=k0, kw_=kw,
                       q_=qb, k_=kb:
                       e.matmul(self.pB[:, n_:n_ + kw_],
                                q_[hb_:hb_ + 64, 128 * qt_:128 * (qt_ + 1)],
                                k_[hb_:hb_ + 64, k0_:k0_ + kw_],
                                start=True, stop=False),
                       reads=[qbk, kbk], writes=[pak], tag="l1f_sc")
                P.emit("pe", lambda e, n_=sbank, kw_=kw, m=mask:
                       e.matmul(self.pB[:, n_:n_ + kw_], S["c_idn"][:, :],
                                m[:, 0:kw_], start=False, stop=True),
                       reads=["c_idn", maskk], writes=[pak], tag="l1f_scm")

            def stageBC(u):
                h, qt = units[u]
                k0 = max(0, 128 * qt - 256)
                kw = 128 * (qt + 1) - k0
                ms = u % 4
                u3 = u % 3
                sbank = 512 * ms
                pak = f"pB{ms}"
                prk = f"pr2_{u3}"
                P.emit("dve", lambda e, n_=sbank, kw_=kw, ms_=ms:
                       e.tensor_reduce(S["mx"][:, ms_:ms_ + 1],
                                       self.pB[:, n_:n_ + kw_], AX.X,
                                       ALU.max, negate=True),
                       reads=[pak], writes=[f"mx{ms}"], tag="l1f_max")
                P.emit("act", lambda e, n_=sbank, kw_=kw, b_=u3, ms_=ms:
                       e.activation(S["pr2"][:, b_, 0:kw_],
                                    self.pB[:, n_:n_ + kw_], AF.Exp,
                                    bias=S["mx"][:, ms_:ms_ + 1],
                                    accum_out=S["mx"][:, 4 + ms_:5 + ms_]),
                       reads=[pak, f"mx{ms}"],
                       writes=[prk, f"mx{4 + ms}"], tag="l1f_exp")
                P.emit("dve", lambda e, ms_=ms:
                       e.reciprocal(S["mx"][:, 4 + ms_:5 + ms_],
                                    S["mx"][:, 4 + ms_:5 + ms_]),
                       reads=[f"mx{4 + ms}"], writes=[f"mx{4 + ms}"], tag="l1f_rc")
                P.emit("dve", lambda e, kw_=kw, ms_=ms, b_=u3:
                       e.tensor_scalar_mul(S["pr2"][:, b_, 0:kw_],
                                           S["pr2"][:, b_, 0:kw_],
                                           S["mx"][:, 4 + ms_:5 + ms_]),
                       reads=[prk, f"mx{4 + ms}"], writes=[prk], tag="l1f_nrm")

            def stageD(u):
                h, qt = units[u]
                k0 = max(0, 128 * qt - 256)
                kw = 128 * (qt + 1) - k0
                nkc = kw // 128
                u4 = u % 4
                u3 = u % 3
                prk = f"pr2_{u3}"
                pak = f"pB{u4}"
                pbview = self.pB[:, 512 * u4:512 * (u4 + 1)].bitcast(BF16)
                for i in range(nkc):
                    P.emit("pe", lambda e, i_=i, b_=u3, v=pbview:
                           e.transpose(v[:, 128 * i_:128 * (i_ + 1)],
                                       S["pr2"][:, b_, 128 * i_:128 * (i_ + 1)],
                                       S["c_idn"][:, :]),
                           reads=[prk, "c_idn"], writes=[pak], tag="l1f_ptr")
                P.emit("dve", lambda e, n_=nkc, b_=u3, v=pbview:
                       e.tensor_copy(S["prT"][:, 3 * b_:3 * b_ + n_, :]
                                     .rearrange("p a b -> p (a b)"),
                                     v[:, 0:128 * n_]),
                       reads=[pak], writes=[f"prT{u3}"], tag="l1f_pev")

            def stageE(u):
                h, qt = units[u]
                hb = 64 * h
                k0 = max(0, 128 * qt - 256)
                kw = 128 * (qt + 1) - k0
                nkc = kw // 128
                u4 = u % 4
                u3 = u % 3
                pak = f"pB{u4}"
                pvview = self.pB[0:64, 512 * u4 + 384:512 * u4 + 512]
                for i in range(nkc):
                    ktg = k0 // 128 + i
                    P.emit("pe", lambda e, i_=i, ktg_=ktg, hb_=hb, b_=u3,
                           v=pvview, vt_=vt, f=(i == 0), s=(i == nkc - 1):
                           e.matmul(v, vt_[:, ktg_, hb_:hb_ + 64],
                                    S["prT"][:, 3 * b_ + i_, :], start=f, stop=s),
                           reads=[vtk, f"prT{u3}"], writes=[pak], tag="l1f_pv")
                P.emit("act", lambda e, j_=j, hb_=hb, qt_=qt, v=pvview:
                       e.activation(S["ab"][hb_:hb_ + 64, j_, 128 * qt_:128 * (qt_ + 1)],
                                    v, AF.Copy),
                       reads=[pak], writes=[f"ab{j}"], tag="l1f_oev")

            for u in range(nu + 4):
                if u >= 4:
                    stageE(u - 4)
                if u < nu:
                    stageA(u)
                    stageBC(u)
                if u >= 2 and u - 2 < nu:
                    stageD(u - 2)
                next(gen, None)
                next(gen, None)
            for _ in gen:
                pass

        for _ in prep_gen(0):
            pass
        if L1F_STOP == "qk":
            return
        for j in range(8):
            gen = prep_gen(j + 1) if j < 7 else iter(())
            attention(j, gen)

        rhs2 = lambda k, n, w: S["ab"][:, k, n:n + w]
        rhsk2 = lambda k: [f"ab{k}"]

        def evac(og, ps, which, pkeys, m):
            P.emit("dve", lambda e, og_=og, p=ps:
                   e.tensor_add(S["xs"][:, og_, 1:T + 1], S["xs"][:, og_, 1:T + 1],
                                p[:, 0:T]),
                   reads=pkeys + [f"xs{og}"], writes=[f"xs{og}"], tag="l1f_wo_ev")

        self.proj(self.WB["fWo"][l], (0, 1024), rhs2, rhsk2, evac, tag="l1f_wo",
                  src_key=f"b_fWo{l}")

        if "ffn" not in L1F_PARTS:
            return
        self._l1f_ffn_part(l)

    def _l1f_ffn_part(self, l):
        T = self.Tcur
        P, S, D = self.P, self.S, self.D
        for th in (1, 0):
            t0 = th * (T // 2)
            Th = T // 2
            zs = lambda c: (S["xs"][:, c, 1 + t0:1 + t0 + Th] if c < 4
                            else S["xs"][:, c, t0:t0 + Th])
            self.ln(zs, lambda c: f"xs{c}", Th,
                    out_bf16_of_c=lambda c: S["xb"][:, c, 0:Th],
                    outkeys_bf16=lambda c: f"xb{c}",
                    g_col=lambda c: S["cols"][:, 76 + c:77 + c],
                    b_col=lambda c: S["cols"][:, 84 + c:85 + c],
                    colkeys=["colsL"], tag="l1f_ln")
            if L1F_STOP == "xn":
                return
            rhs3 = lambda k, n, w: S["xb"][:, k, n:n + w]
            rhsk3 = lambda k: [f"xb{k}"]

            def hslot(i):
                if i < 16:
                    return S["ab"][:, i // 2, (i % 2) * Th:(i % 2) * Th + Th], f"ab{i // 2}"
                c = i - 16
                return S["xb"][:, c, T - Th:T], f"xb{c}"

            for i in range(22):
                m = 128 if i < 21 else 42
                self.wload(0, self.WB["fW1"][l], 0, [128] * 8, 128 * i, m,
                           tag="l1f_w1a", src_key=f"b_fW1{l}")
                self.wload(1, self.WB["fW1"][l], 0, [128] * 8, 2730 + 128 * i, m,
                           tag="l1f_w1g", src_key=f"b_fW1{l}")
                pp = self.pA if i % 2 == 0 else self.pB
                which = "A" if i % 2 == 0 else "B"
                gk = "sc0" if i % 2 == 0 else "sc1"
                pkA = self.mm(pp, which, rhs3,
                              lambda k: S["wk"][:, 0, k, 0:m], nk=8, T=Th,
                              rhs_keys=rhsk3, lhsT_keys=lambda k: ["wk0"],
                              tag="l1f_a", m=m, n0=0)
                pkB = self.mm(pp, which, rhs3,
                              lambda k: S["wk"][:, 1, k, 0:m], nk=8, T=Th,
                              rhs_keys=rhsk3, lhsT_keys=lambda k: ["wk1"],
                              tag="l1f_g", m=m, n0=Th)
                P.emit("act", lambda e, m_=m, i_=i, p=pp, g=S[gk]:
                       e.activation(g[0:m_, 0:Th], p[0:m_, Th:2 * Th],
                                    AF.Gelu, bias=S["cols"][0:m_, 122 + i_:123 + i_]),
                       reads=pkB + ["colsB"], writes=[gk], tag="l1f_gelu")
                hap, hkey = hslot(i)
                P.emit("dve", lambda e, m_=m, i_=i, d=hap, p=pp, g=S[gk]:
                       e.scalar_tensor_tensor(d[0:m_, :], p[0:m_, 0:Th],
                                              S["cols"][0:m_, 100 + i_:101 + i_],
                                              g[0:m_, 0:Th],
                                              ALU.add, ALU.mult),
                       reads=pkA + ["colsB", gk], writes=[hkey], tag="l1f_h")
            if L1F_STOP == "h":
                return
            for og in range(8):
                which = "A" if og % 2 == 0 else "B"
                ps = self.pA if which == "A" else self.pB

                def rhs4(k, n, w):
                    hap, _ = hslot(k)
                    rows = 128 if k < 21 else 42
                    return hap[0:rows, n:n + w]

                def rhsk4(k):
                    return [hslot(k)[1]]

                w2fv = S["wk"][:, :, :, :].rearrange("p a b c -> p (a b c)")
                self.wload(0, self.WB["fW2"][l], 0, [128] * 11, 128 * og, 128,
                           tag="l1f_w2", flat=True, src_key=f"b_fW2{l}")
                pkeys = self.mm(ps, which,
                                lambda k, n, w: rhs4(k, n, w),
                                lambda k: w2fv[0:128, 128 * k:128 * k + 128],
                                nk=11, T=Th, rhs_keys=rhsk4,
                                lhsT_keys=lambda k: ["wk0", "wk1"],
                                tag="l1f_w2mm", close_group=False)
                self.wload(0, self.WB["fW2"][l], 1408, [128] * 10 + [42],
                           128 * og, 128, tag="l1f_w2b", flat=True,
                           src_key=f"b_fW2{l}")
                pkeys = self.mm(ps, which,
                                lambda k, n, w: rhs4(k + 11, n, w),
                                lambda k: w2fv[0:(128 if k < 10 else 42),
                                               128 * k:128 * k + 128],
                                nk=11, T=Th, rhs_keys=lambda k: rhsk4(k + 11),
                                lhsT_keys=lambda k: ["wk0", "wk1"],
                                tag="l1f_w2mm2", open_group=False)
                P.emit("dve", lambda e, og_=og, p=ps, t0_=t0:
                       e.scalar_tensor_tensor(S["xs"][:, og_, 1 + t0_:1 + t0_ + Th],
                                              p[:, 0:Th],
                                              S["cols"][:, 92 + og_:93 + og_],
                                              S["xs"][:, og_, 1 + t0_:1 + t0_ + Th],
                                              ALU.add, ALU.add),
                       reads=pkeys + [f"xs{og}", "colsL"], writes=[f"xs{og}"],
                       tag="l1f_w2ev")
            if L1F_STOP == "half":
                return

    def build(self):
        self.alloc()
        P, S, D = self.P, self.S, self.D
        T = self.T
        self.Tcur = T
        self.emit_precast()
        for n in ("idn", "causal", "band384", "ones_ln",
                  "blockones", "qcs", "qss", "kcs", "kss", "blockones32",
                  "rot32"):
            self.load_const(n)
        for src, sl in (("b0qkv", (0, 24)), ("b0o", (24, 32)), ("mur", (32, 40)),
                        ("muk", (40, 48)), ("fqs", (64, 70)), ("fks", (70, 76))):
            P.dma("sp", lambda e, s=src, a=sl: e.dma_start(
                out=S["cols"][:, a[0]:a[1]], in_=D[s]),
                writes=["colsA"], tag=f"ld_{src}")
        P.emit("dve", lambda e: e.tensor_scalar_mul(S["cols"][:, 146:154],
                                                    S["cols"][:, 0:8], 0.125),
               reads=["colsA"], writes=["colsA"], tag="qbias8")
        P.emit("dve", lambda e: e.memset(S["cols"][:, 144:145], 1e-5),
               writes=["colsE"], tag="eps1")
        P.emit("dve", lambda e: e.memset(S["cols"][:, 145:146], 1e-24),
               writes=["colsE"], tag="eps2")
        P.emit("dve", lambda e: e.tensor_scalar(S["cols"][:, 48:56],
                                                S["cols"][:, 32:40], -1.0, 1.0,
                                                ALU.mult, ALU.add),
               reads=["colsA"], writes=["colsA"], tag="omr")
        P.emit("dve", lambda e: e.tensor_scalar(S["cols"][:, 56:64],
                                                S["cols"][:, 40:48], -1.0, 1.0,
                                                ALU.mult, ALU.add),
               reads=["colsA"], writes=["colsA"], tag="omk")
        P.emit("dve", lambda e: e.memset(S["xs"][:, :, 0:1], 0.0),
               writes=[f"xs{c}" for c in range(8)], tag="zerocol")
        for s in range(self.nseq):
            P.dma("sp", lambda e, s_=s: e.dma_start(
                out=S["xs"][:, :, 1:T + 1],
                in_=D["x_in"][s_].rearrange("(c p) t -> p c t", p=128)),
                reads=[], writes=[f"xs{c}" for c in range(8)], tag="ldx")
            for b in self.blocks:
                (self.l0_attn, self.l0_ffn, self.l1_attn, self.l1_ffn)[b]()
            P.dma("sp", lambda e, s_=s: e.dma_start(
                out=D["x_out"][s_].rearrange("(c p) t -> p c t", p=128),
                in_=S["xs"][:, :, 1:T + 1]),
                reads=[f"xs{c}" for c in range(8)], writes=[], tag="stx",
                is_output=True)
        self.P.materialize_and_run(self.stack)
        self.stack.close()
        return self.nc



def host_inputs(full, T=2048):
    h = {}
    h["w0qkvT"] = np.ascontiguousarray(full["L0_attn_W_qkv"].T)
    h["b0qkv"] = np.ascontiguousarray(full["L0_attn_b_qkv"].reshape(24, 128).T)
    h["w0oT"] = np.ascontiguousarray(full["L0_attn_W_o"].T)
    h["b0o"] = np.ascontiguousarray(full["L0_attn_b_o"].reshape(8, 128).T)
    h["w0rT"] = np.ascontiguousarray(full["L0_ffn_W_r"].T)
    h["w0kT"] = np.ascontiguousarray(full["L0_ffn_W_k"].T)
    h["w0vT"] = np.ascontiguousarray(full["L0_ffn_W_v"].T)
    h["mur"] = np.ascontiguousarray(full["L0_ffn_mu_r"].reshape(8, 128).T)
    h["muk"] = np.ascontiguousarray(full["L0_ffn_mu_k"].reshape(8, 128).T)
    h["w1qkvT"] = np.ascontiguousarray(full["L1_attn_w_qkv"].T)
    h["w1oT"] = np.ascontiguousarray(full["L1_attn_w_out"].T)
    h["fWqkv"] = np.ascontiguousarray(full["L1_ffn_Wqkv"])
    h["fqs"] = np.ascontiguousarray(
        np.concatenate([full["L1_ffn_q_scale"].T, full["L1_ffn_q_scale"].T], 0))
    h["fks"] = np.ascontiguousarray(
        np.concatenate([full["L1_ffn_k_scale"].T, full["L1_ffn_k_scale"].T], 0))
    h["fWo"] = np.ascontiguousarray(full["L1_ffn_Wo"])
    h["flng"] = np.ascontiguousarray(
        full["L1_ffn_ln_g"].reshape(6, 8, 128).transpose(0, 2, 1))
    h["flnb"] = np.ascontiguousarray(
        full["L1_ffn_ln_b"].reshape(6, 8, 128).transpose(0, 2, 1))
    h["fW1"] = np.ascontiguousarray(full["L1_ffn_W1"])
    b1 = full["L1_ffn_b1"]

    def colize22(v):
        out = np.zeros((6, 128, 22), dtype=np.float32)
        for l in range(6):
            x = np.zeros(22 * 128, dtype=np.float32)
            x[:2730] = v[l]
            out[l] = x.reshape(22, 128).T
        return out

    h["fb1a"] = colize22(b1[:, :2730])
    h["fb1g"] = colize22(b1[:, 2730:])
    h["fW2"] = np.ascontiguousarray(full["L1_ffn_W2"])
    h["fb2"] = np.ascontiguousarray(
        full["L1_ffn_b2"].reshape(6, 8, 128).transpose(0, 2, 1))
    for k, v in host_consts(T).items():
        h[k] = v
    for k in h:
        if h[k].dtype == np.float64:
            raise ValueError(k)
    return h


def kernel(**inputs):
    T, nseq, ncores = 2048, 4, 8
    shared = host_inputs(inputs, T)
    x = np.ascontiguousarray(
        np.asarray(inputs["L0_attn_x"]).transpose(0, 2, 1))
    core = Core(T, nseq, blocks=(0, 1, 2, 3))
    nc = core.build()
    in_maps = []
    for c in range(ncores):
        m = dict(shared)
        m["x_in"] = x[c * nseq:(c + 1) * nseq]
        in_maps.append(m)
    res = run_bass_kernel_spmd(nc, in_maps, core_ids=list(range(ncores)))
    out = np.empty((32, T, 1024), dtype=np.float32)
    for c in range(ncores):
        o = np.asarray(res.results[c]["x_out"])
        out[c * nseq:(c + 1) * nseq] = o.transpose(0, 2, 1)
    return out


# version 11
# speedup vs baseline: 1.1162x; 1.1162x over previous
import math
from contextlib import ExitStack

import numpy as np
import ml_dtypes
import concourse.bass as bass
import concourse.mybir as mybir
from concourse.bass_utils import run_bass_kernel_spmd

F32 = mybir.dt.float32
BF16 = mybir.dt.bfloat16
AF = mybir.ActivationFunctionType
ALU = mybir.AluOpType
AX = mybir.AxisListType
BF = ml_dtypes.bfloat16

ENGINES = ("pe", "dve", "act", "pool", "sp")
L1F_DEPTH = 6
L1F_PARTS = ("attn", "ffn")
L1F_STOP = None
SEM_WINDOW = 30000
NEG = -1.0e9



class Op:
    __slots__ = ("idx", "eng", "fn", "deps", "is_dma", "inc_needed", "done_key",
                 "done_val", "tag", "waits", "dma_sem_i")

    def __init__(self, idx, eng, fn, is_dma, tag):
        self.idx = idx
        self.eng = eng
        self.fn = fn
        self.deps = []
        self.is_dma = is_dma
        self.inc_needed = is_dma
        self.done_key = None
        self.done_val = None
        self.tag = tag
        self.waits = []
        self.dma_sem_i = -1


class Prog:
    def __init__(self, nc, n_dma_sems=48):
        self.nc = nc
        self.ops = []
        self.last_write = {}
        self.readers = {}
        self.n_dma_sems = n_dma_sems
        self.dma_rr = 0
        self.out_dma_ops = []

    ALLOWED_FREE = frozenset({"self", "S", "D", "P", "T", "nc", "Th"})

    def _check_fn(self, fn, tag):
        code = getattr(fn, "__code__", None)
        if code is None:
            return
        bad = [v for v in code.co_freevars if v not in self.ALLOWED_FREE]
        if bad:
            raise RuntimeError(
                f"op '{tag}': lambda captures loop variables by reference: {bad}; "
                "bind them as default arguments")

    def emit(self, eng, fn, reads=(), writes=(), tag=""):
        self._check_fn(fn, tag)
        op = Op(len(self.ops), eng, fn, False, tag)
        self._track(op, reads, writes)
        return op

    def dma(self, queue, fn, reads=(), writes=(), tag="", is_output=False):
        self._check_fn(fn, tag)
        op = Op(len(self.ops), queue, fn, True, tag)
        if queue == "pool":
            half = self.n_dma_sems // 2
            op.dma_sem_i = self.dma_rr % half
        else:
            half = self.n_dma_sems // 2
            op.dma_sem_i = half + self.dma_rr % half
        self.dma_rr += 1
        self._track(op, reads, writes)
        if is_output:
            self.out_dma_ops.append(op)
        return op

    ALIASES = {
        "sb2": ("sb2_0", "sb2_1"),
        "sb3": ("sb3_0", "sb3_1"),
        "vtm": ("vtm0", "vtm1"),
        "sc0": ("scS0", "scS1", "scS2", "scS3"),
        "pr2": ("pr2_0", "pr2_1", "pr2_2"),
        "prT": ("prT0", "prT1", "prT2"),
    }

    def _expand(self, keys):
        out = list(keys)
        for k in keys:
            out.extend(self.ALIASES.get(k, ()))
        return out

    def _track(self, op, reads, writes):
        reads = self._expand(reads)
        writes = self._expand(writes)
        lw = self.last_write
        rd = self.readers
        deps = {}
        for r in reads:
            d = lw.get(r)
            if d is not None:
                deps[d.idx] = d
        for w in writes:
            d = lw.get(w)
            if d is not None:
                deps[d.idx] = d
            for d in rd.get(w, ()):
                deps[d.idx] = d
        op.deps = [d for d in deps.values()
                   if not (op.eng == "pe" and d.eng == "pe" and not d.is_dma)]
        for r in reads:
            rd.setdefault(r, []).append(op)
        for w in writes:
            lw[w] = op
            rd[w] = [op]
        self.ops.append(op)

    def materialize_and_run(self, stack):
        nc = self.nc
        for op in self.ops:
            for d in op.deps:
                d.inc_needed = True
        eng_inc = dict.fromkeys(ENGINES, 0)
        dma_targets = [0] * self.n_dma_sems
        for op in self.ops:
            if op.is_dma:
                dma_targets[op.dma_sem_i] += 16
                op.done_key = ("dma", op.dma_sem_i)
                op.done_val = dma_targets[op.dma_sem_i]
            elif op.inc_needed:
                eng_inc[op.eng] += 1
                i = eng_inc[op.eng]
                w = (i - 1) // SEM_WINDOW
                op.done_key = ("eng", op.eng, w)
                op.done_val = i - w * SEM_WINDOW
        self.sems = {}
        for eng in ENGINES:
            n = eng_inc[eng]
            for w in range((n + SEM_WINDOW - 1) // SEM_WINDOW):
                self.sems[("eng", eng, w)] = stack.enter_context(
                    nc.semaphore(f"s_{eng}_{w}"))
        for i in range(self.n_dma_sems):
            if dma_targets[i]:
                self.sems[("dma", i)] = stack.enter_context(nc.semaphore(f"d_{i}"))
        clocks = {eng: {} for eng in ENGINES}
        dma_prev = [0] * self.n_dma_sems
        for op in self.ops:
            clk = clocks[op.eng]
            waits = {}
            if op.is_dma:
                prev = dma_prev[op.dma_sem_i]
                if prev and clk.get(("dma", op.dma_sem_i), 0) < prev:
                    waits[("dma", op.dma_sem_i)] = prev
                dma_prev[op.dma_sem_i] = op.done_val
            for d in op.deps:
                k, v = d.done_key, d.done_val
                if clk.get(k, 0) < v and waits.get(k, 0) < v:
                    waits[k] = v
            for k, v in waits.items():
                clk[k] = max(clk.get(k, 0), v)
            op.waits = sorted(waits.items())
        fin = {}
        for i in range(self.n_dma_sems):
            if dma_targets[i]:
                fin[("dma", i)] = dma_targets[i]
        final_waits = sorted(fin.items())

        block = stack.enter_context(nc.Block())
        per_eng = {e: [] for e in ENGINES}
        for op in self.ops:
            per_eng[op.eng].append(op)
        sems = self.sems

        def runner(ops_list, is_sp):
            def run(eng):
                for op in ops_list:
                    for k, v in op.waits:
                        eng.wait_ge(sems[k], v)
                    inst = op.fn(eng)
                    if op.is_dma:
                        inst.then_inc(sems[("dma", op.dma_sem_i)], 16)
                    elif op.inc_needed:
                        inst.then_inc(sems[op.done_key], 1)
                if is_sp:
                    for k, v in final_waits:
                        eng.wait_ge(sems[k], v)
            return run

        block.tensor(runner(per_eng["pe"], False))
        block.vector(runner(per_eng["dve"], False))
        block.scalar(runner(per_eng["act"], False))
        block.gpsimd(runner(per_eng["pool"], False))
        block.sync(runner(per_eng["sp"], True))



def host_consts(T):
    c = {}
    c["idn"] = np.eye(128, dtype=BF)
    i = np.arange(128)[:, None]
    c["causal"] = np.where(i >= np.arange(128)[None, :], 0.0, NEG).astype(BF)
    jr = np.arange(384)[None, :]
    c["band384"] = np.where((jr >= i) & (jr <= i + 256), 0.0, NEG).astype(BF)
    jr = np.arange(256)[None, :]
    c["band256"] = np.where(jr <= i + 128, 0.0, NEG).astype(BF)
    c["ones_ln"] = np.full((128, 128), 1.0 / 1024.0, dtype=BF)
    bd = np.zeros((128, 128), dtype=np.float32)
    bd[:64, :64] = 1.0
    bd[64:, 64:] = 1.0
    c["blockones"] = bd.astype(BF)
    R = np.zeros((128, 128), dtype=np.float32)
    for h0 in (0, 64):
        for m in range(32):
            R[h0 + m + 32, h0 + m] = -1.0
            R[h0 + m, h0 + m + 32] = 1.0
    c["rot"] = R.astype(BF)
    c["rot32"] = R.astype(np.float32)
    c["blockones32"] = bd.astype(np.float32)
    DIM_HEAD, WINDOW = 64, 256
    half = np.arange(0, DIM_HEAD, 2, dtype=np.float32)
    inv_freq = (1.0 / 10000.0 ** (half / DIM_HEAD)).astype(np.float32)
    t = np.arange(T, dtype=np.float32)
    freqs = t[:, None] * inv_freq[None, :]
    freqs = np.concatenate([freqs, freqs], -1)
    base_scale = ((half + 0.4 * DIM_HEAD) / (1.4 * DIM_HEAD)).astype(np.float32)
    power = (t - T // 2) / (WINDOW // 2)
    scale = base_scale[None, :] ** power[:, None]
    scale = np.concatenate([scale, scale], -1)
    cos, sin = np.cos(freqs), np.sin(freqs)
    qcs = (8.0 * cos * scale).T
    qss = (8.0 * sin * scale).T
    kcs = (cos / scale).T
    kss = (sin / scale).T
    for name, a in (("qcs", qcs), ("qss", qss), ("kcs", kcs), ("kss", kss)):
        c[name] = np.concatenate([a, a], 0).astype(BF)
    return c


def hilbert_runs(M):
    g = int(math.ceil(math.sqrt(M)))
    runs = []
    pos = 0
    r = 0
    while pos < M:
        n = min(g, M - pos)
        full_vals = list(range(r * g, r * g + g))
        if r % 2 == 1:
            full_vals = full_vals[::-1]
        vals = [v for v in full_vals if v < M][:n]
        step = 0 if len(vals) < 2 else vals[1] - vals[0]
        runs.append((vals[0], len(vals), step < 0))
        pos += len(vals)
        r += 1
    return runs



class Core:

    def __init__(self, T, nseq, blocks=(0, 1, 2, 3)):
        self.T = T
        self.nseq = nseq
        self.blocks = blocks
        self.nc = bass.Bass("TRN2", target_bir_lowering=False)
        self.P = Prog(self.nc)
        self.wq_rr = 0
        self.stack = ExitStack()
        self._declare_dram()

    def _declare_dram(self):
        nc, T, nseq = self.nc, self.T, self.nseq
        D = {}

        def inp(name, shape, dt=F32):
            D[name] = nc.dram_tensor(name, shape, dt, kind="ExternalInput").ap()

        inp("x_in", [nseq, 1024, T])
        inp("w0qkvT", [1024, 3072])
        inp("b0qkv", [128, 24])
        inp("w0oT", [1024, 1024])
        inp("b0o", [128, 8])
        inp("w0rT", [1024, 1024])
        inp("w0kT", [1024, 1024])
        inp("w0vT", [1024, 1024])
        inp("mur", [128, 8])
        inp("muk", [128, 8])
        inp("w1qkvT", [1024, 3072])
        inp("w1oT", [1024, 1024])
        inp("fWqkv", [6, 1024, 3072])
        inp("fqs", [128, 6])
        inp("fks", [128, 6])
        inp("fWo", [6, 1024, 1024])
        inp("flng", [6, 128, 8])
        inp("flnb", [6, 128, 8])
        inp("fW1", [6, 1024, 5460])
        inp("fb1a", [6, 128, 22])
        inp("fb1g", [6, 128, 22])
        inp("fW2", [6, 2730, 1024])
        inp("fb2", [6, 128, 8])
        inp("blockones32", [128, 128])
        inp("rot32", [128, 128])
        for n, shape, dt in (
            ("idn", [128, 128], BF16), ("causal", [128, 128], BF16),
            ("band384", [128, 384], BF16), ("band256", [128, 256], BF16),
            ("ones_ln", [128, 128], BF16), ("blockones", [128, 128], BF16),
            ("rot", [128, 128], BF16),
            ("qcs", [128, T], BF16), ("qss", [128, T], BF16),
            ("kcs", [128, T], BF16), ("kss", [128, T], BF16),
        ):
            inp(n, shape, dt)
        D["x_out"] = nc.dram_tensor("x_out", [nseq, 1024, T], F32,
                                    kind="ExternalOutput").ap()
        self.WB = {}
        for n in ("w0qkvT", "w0oT", "w0rT", "w0kT", "w0vT", "w1qkvT", "w1oT",
                  "fWqkv", "fWo", "fW1", "fW2"):
            self.WB[n] = nc.dram_tensor("b_" + n, list(D[n].shape), BF16).ap()
        D["xpark"] = nc.dram_tensor("xpark", [128, 8, T], F32).ap()
        self.D = D

    def alloc(self):
        nc, T = self.nc, self.T
        st = self.stack
        S = {}

        def sb(name, shape, dt):
            S[name] = st.enter_context(nc.sbuf_tensor(name, shape, dt))

        sb("xs", [128, 8, T + 1], F32)
        sb("xb", [128, 8, T], BF16)
        sb("ab", [128, 8, T], BF16)
        sb("sc0", [128, T], F32)
        sb("sc1", [128, T], F32)
        sb("sb0", [128, T], BF16)
        sb("sb1", [128, T], BF16)
        sb("sb2", [128, 2, T], BF16)
        sb("sb3", [128, 2, T], BF16)
        sb("vtm", [128, 2, T // 128, 128], BF16)
        sb("prT", [128, 9, 128], BF16)
        sb("pr2", [128, 3, 384], BF16)
        sb("wk", [128, 2, 8, 128], BF16)
        sb("mx", [128, 8], F32)
        sb("cols", [128, 160], F32)
        for n in ("idn", "causal", "band384", "ones_ln",
                  "blockones", "qcs", "qss", "kcs", "kss"):
            sb("c_" + n, list(self.D[n].shape), BF16)
        for n in ("blockones32", "rot32"):
            sb("c_" + n, list(self.D[n].shape), F32)
        self.S = S
        self.pA = st.enter_context(nc.psum_tensor("pA", [128, 512 * 4], F32))
        self.pB = st.enter_context(nc.psum_tensor("pB", [128, 512 * 4], F32))

    def pk(self, which, n0, n1):
        return [f"p{which}{i}" for i in range(n0 // 512, (n1 + 511) // 512)]

    def load_const(self, name):
        P, S, D = self.P, self.S, self.D
        dst = S["c_" + name]
        P.dma("sp", lambda e, d=dst, s=D[name]: e.dma_start(out=d[:, :] if len(d.shape) == 2 else d[:, :, :], in_=s),
              writes=[f"c_{name}"], tag=f"ld_{name}")

    def emit_precast(self):
        P, D = self.P, self.D
        singles = ["w0qkvT", "w0oT", "w0rT", "w0kT", "w0vT", "w1qkvT", "w1oT"]
        for n in singles:
            P.dma("pool", lambda e, s=D[n], d=self.WB[n]: e.dma_start(out=d, in_=s),
                  writes=["b_" + n], tag="precast_" + n)
        for l in range(6):
            for n in ("fWqkv", "fWo", "fW1", "fW2"):
                P.dma("pool", lambda e, s=D[n][l], d=self.WB[n][l]:
                      e.dma_start(out=d, in_=s),
                      writes=[f"b_{n}{l}"], tag=f"precast_{n}{l}")

    def load_cols(self, dst_sl, src_ap, key, tag=""):
        P, S = self.P, self.S
        P.dma("sp", lambda e, s=src_ap, d=S["cols"], sl=dst_sl:
              e.dma_start(out=d[:, sl], in_=s),
              writes=[key], tag=tag or f"cols_{key}")

    def wload(self, buf, w_hbm, rows0, nrows, col0, ncols, tag="", flat=False,
              src_key=None):
        P, S = self.P, self.S
        q = "sp"
        self.wq_rr += 1
        rk = [src_key] if src_key else []
        r0 = rows0
        fv = S["wk"][:, :, :, :].rearrange("p a b c -> p (a b c)") if flat else None
        nfull = 0
        while nfull < len(nrows) and nrows[nfull] == 128:
            nfull += 1
        if nfull > 1 and rows0 % 128 == 0:
            src3 = w_hbm[rows0:rows0 + 128 * nfull, col0:col0 + ncols].rearrange(
                "(k p) o -> p k o", p=128)
            if flat:
                dst3 = bass.AP(fv.tensor, fv.offset,
                               [list(fv.ap[0]), [128, nfull], [1, ncols]])                     if False else None
                dst3 = fv[:, 0:128 * nfull].rearrange("p (k o) -> p k o", o=128)[:, :, 0:ncols]
            else:
                dst3 = S["wk"][:, buf, 0:nfull, 0:ncols]
            P.dma(q, lambda e, s=src3, d=dst3: e.dma_start(out=d, in_=s),
                  reads=rk, writes=(["wk0", "wk1"] if flat else [f"wk{buf}"]),
                  tag=tag or "wload")
            r0 = rows0 + 128 * nfull
            start_k = nfull
        else:
            start_k = 0
        for k in range(start_k, len(nrows)):
            rc = nrows[k]
            if flat:
                dst = fv[0:rc, 128 * k:128 * k + ncols]
            else:
                dst = S["wk"][0:rc, buf, k, 0:ncols]
            P.dma(q, lambda e, s=w_hbm[r0:r0 + rc, col0:col0 + ncols],
                  d=dst: e.dma_start(out=d, in_=s),
                  reads=rk, writes=(["wk0", "wk1"] if flat else [f"wk{buf}"]),
                  tag=tag or "wload")
            r0 += rc

    def mm(self, psum, which, rhs_of_k, lhsT_of_k, nk, T, rhs_keys, lhsT_keys,
           tag="mm", m=128, n0=0, open_group=True, close_group=True):
        P = self.P
        pkeys = self.pk(which, n0, n0 + T)
        for k in range(nk):
            first = (k == 0) and open_group
            last = (k == nk - 1) and close_group
            for n in range(0, T, 512):
                w = min(512, T - n)
                P.emit("pe", lambda e, a=psum[0:m, n0 + n:n0 + n + w],
                       l=lhsT_of_k(k), r=rhs_of_k(k, n, w), f=first, s=last:
                       e.matmul(a, l, r, start=f, stop=s),
                       reads=list(rhs_keys(k)) + list(lhsT_keys(k)),
                       writes=[pkeys[(n0 + n) // 512 - n0 // 512]], tag=tag)
        return pkeys

    def ln(self, z_of_c, zkey_of_c, T, out_f32_of_c=None, out_bf16_of_c=None,
           outkeys_f32=None, outkeys_bf16=None, g_col=None, b_col=None,
           colkeys=(), tag="ln"):
        P, S = self.P, self.S
        pkA = self.pk("A", 0, T)
        pkB = self.pk("B", 0, T)
        for c in range(8):
            zb = S["sb0" if c % 2 == 0 else "sb1"]
            zq = S["sb2"][:, 0] if c % 2 == 0 else S["sb3"][:, 0]
            zbk = "sb0" if c % 2 == 0 else "sb1"
            zqk = "sb2" if c % 2 == 0 else "sb3"
            P.emit("pool", lambda e, d=zb[:, 0:T], s=z_of_c(c): e.tensor_copy(d, s),
                   reads=[zkey_of_c(c)], writes=[zbk], tag=tag + "_zb")
            P.emit("act", lambda e, d=zq[:, 0:T], s=z_of_c(c):
                   e.activation(d, s, AF.Square),
                   reads=[zkey_of_c(c)], writes=[zqk], tag=tag + "_zq")
            for n in range(0, T, 512):
                w = min(512, T - n)
                P.emit("pe", lambda e, a=self.pA[:, n:n + w], l=S["c_ones_ln"][:, :],
                       r=zb[:, n:n + w], f=(c == 0), s=(c == 7):
                       e.matmul(a, l, r, start=f, stop=s),
                       reads=[zbk, "c_ones_ln"], writes=[pkA[n // 512]], tag=tag + "_sm")
                P.emit("pe", lambda e, a=self.pB[:, n:n + w], l=S["c_ones_ln"][:, :],
                       r=zq[:, n:n + w], f=(c == 0), s=(c == 7):
                       e.matmul(a, l, r, start=f, stop=s),
                       reads=[zqk, "c_ones_ln"], writes=[pkB[n // 512]], tag=tag + "_sq")
        P.emit("act", lambda e: e.activation(S["sc1"][:, 0:T], self.pA[:, 0:T],
                                             AF.Square),
               reads=pkA, writes=["sc1"], tag=tag + "_m2")
        P.emit("dve", lambda e: e.tensor_sub(S["sc1"][:, 0:T], self.pB[:, 0:T],
                                             S["sc1"][:, 0:T]),
               reads=pkB + ["sc1"], writes=["sc1"], tag=tag + "_var")
        P.emit("act", lambda e: e.activation(S["sc1"][:, 0:T], S["sc1"][:, 0:T],
                                             AF.Ln, bias=S["cols"][:, 144:145]),
               reads=["sc1", "colsE"], writes=["sc1"], tag=tag + "_ln")
        P.emit("act", lambda e: e.activation(S["sc1"][:, 0:T], S["sc1"][:, 0:T],
                                             AF.Exp, scale=-0.5),
               reads=["sc1"], writes=["sc1"], tag=tag + "_exp")
        for c in range(8):
            P.emit("dve", lambda e, s=z_of_c(c): e.tensor_sub(S["sc0"][:, 0:T], s,
                                                              self.pA[:, 0:T]),
                   reads=[zkey_of_c(c)] + pkA, writes=["sc0"], tag=tag + "_a1")
            P.emit("dve", lambda e: e.tensor_mul(S["sc0"][:, 0:T], S["sc0"][:, 0:T],
                                                 S["sc1"][:, 0:T]),
                   reads=["sc0", "sc1"], writes=["sc0"], tag=tag + "_a2")
            if out_f32_of_c is not None:
                P.emit("pool", lambda e, d=out_f32_of_c(c):
                       e.tensor_copy(d, S["sc0"][:, 0:T]),
                       reads=["sc0"], writes=[outkeys_f32(c)], tag=tag + "_of")
            if out_bf16_of_c is not None:
                if g_col is not None:
                    P.emit("dve", lambda e, d=out_bf16_of_c(c), g=g_col(c), b=b_col(c):
                           e.tensor_scalar(d, S["sc0"][:, 0:T], g, b, ALU.mult, ALU.add),
                           reads=["sc0"] + list(colkeys), writes=[outkeys_bf16(c)],
                           tag=tag + "_ob")
                else:
                    P.emit("pool", lambda e, d=out_bf16_of_c(c):
                           e.tensor_copy(d, S["sc0"][:, 0:T]),
                           reads=["sc0"], writes=[outkeys_bf16(c)], tag=tag + "_ob")

    def proj(self, w_hbm, wcols, rhs_chunk, rhs_key, evac, nk=8, krows=None,
             piece=128, tag="proj", src_key=None):
        T = self.Tcur
        P, S = self.P, self.S
        if krows is None:
            krows = [128] * nk
        c0, c1 = wcols
        ncols = c1 - c0
        npieces = (ncols + piece - 1) // piece
        for pi in range(npieces):
            pc0 = c0 + pi * piece
            pw = min(piece, c1 - pc0)
            buf = pi % 2
            self.wload(buf, w_hbm, 0, krows, pc0, pw, tag=tag + "_w",
                       src_key=src_key)
            for oc in range(0, pw, 128):
                m = min(128, pw - oc)
                og = (pi * piece + oc) // 128
                which = "A" if og % 2 == 0 else "B"
                ps = self.pA if which == "A" else self.pB
                pkeys = self.mm(
                    ps, which,
                    rhs_of_k=rhs_chunk,
                    lhsT_of_k=lambda k, b=buf, o=oc, mm_=m:
                        S["wk"][0:krows[k], b, k, o:o + mm_],
                    nk=nk, T=T,
                    rhs_keys=rhs_key,
                    lhsT_keys=lambda k, b=buf: [f"wk{b}"],
                    tag=tag, m=m)
                evac(og, ps, which, pkeys, m)

    def l0_attn(self):
        T = self.Tcur
        P, S, D = self.P, self.S, self.D
        nt = T // 128
        for c in range(8):
            P.emit("pool", lambda e, c_=c: e.tensor_copy(
                S["xb"][:, c_, 0:T], S["xs"][:, c_, 1:T + 1]),
                reads=[f"xs{c}"], writes=[f"xb{c}"], tag="l0a_cast")
        rhs = lambda k, n, w: S["xb"][:, k, n:n + w]
        rhsk = lambda k: [f"xb{k}"]

        for j in range(8):
            pj = j % 2
            role_dst = {"q": (S["sb0"][:, 0:T], "sb0"),
                        "k": (S["sb1"][:, 0:T], "sb1"),
                        "v": (S["sb2"][:, pj, 0:T], f"sb2_{pj}")}
            for role, colbase, rname, biascol, rscale in (
                    ("q", 128 * j, "q", 146 + j, 0.125),
                    ("k", 1024 + 128 * j, "k", 8 + j, 1.0),
                    ("v", 2048 + 128 * j, "v", 16 + j, 1.0)):
                buf = 0 if role != "k" else 1
                self.wload(buf, self.WB["w0qkvT"], 0, [128] * 8, colbase, 128,
                           tag="l0a_w" + role, src_key="b_w0qkvT")
                which = "A" if role != "k" else "B"
                ps = self.pA if which == "A" else self.pB
                pkeys = self.mm(ps, which, rhs, lambda k, b=buf: S["wk"][:, b, k, 0:128],
                                nk=8, T=T, rhs_keys=rhsk,
                                lhsT_keys=lambda k, b=buf: [f"wk{b}"], tag="l0a_qkv")
                P.emit("act", lambda e, d=role_dst[rname][0], p=ps[:, 0:T],
                       bc=S["cols"][:, biascol:biascol + 1], rs=rscale:
                       e.activation(d, p, AF.Identity, bias=bc, scale=rs),
                       reads=pkeys + ["colsA"], writes=[role_dst[rname][1]],
                       tag="l0a_ev" + role)
            for t0 in range(0, nt, 4):
                ntile = min(4, nt - t0)
                for i in range(ntile):
                    P.emit("pe", lambda e, i_=i, t_=t0 + i, v2=S["sb2"][:, pj],
                           o=self.pB[:, 0:512].bitcast(BF16)[:, 128 * i:128 * (i + 1)]:
                           e.transpose(o, v2[:, 128 * t_:128 * (t_ + 1)],
                                       S["c_idn"][:, :]),
                           reads=[f"sb2_{pj}", "c_idn"], writes=["pB0"], tag="l0a_vtr")
                P.emit("dve", lambda e, t0_=t0, n_=ntile, vt=S["vtm"][:, pj]:
                       e.tensor_copy(vt[:, t0_:t0_ + n_, :].rearrange("p a b -> p (a b)"),
                                     self.pB[:, 0:512].bitcast(BF16)[:, 0:128 * n_]),
                       reads=["pB0"], writes=[f"vtm{pj}"], tag="l0a_vev")
            units = [(h, qt) for h in (0, 1) for qt in range(nt)]
            nu = len(units)
            rr = [0]

            def stageA(u):
                h, qt = units[u]
                hb = 64 * h
                K = 128 * (qt + 1)
                sck = "sc0" if u % 2 == 0 else "sc1"
                sc = S[sck]
                for n in range(0, K, 512):
                    w = min(512, K - n)
                    b = rr[0] % 4
                    rr[0] += 1
                    pak = f"pA{b}"
                    P.emit("pe", lambda e, n_=n, w_=w, qt_=qt, hb_=hb, b_=b:
                           e.matmul(self.pA[:, 512 * b_:512 * b_ + w_],
                                    S["sb0"][hb_:hb_ + 64, 128 * qt_:128 * (qt_ + 1)],
                                    S["sb1"][hb_:hb_ + 64, n_:n_ + w_],
                                    start=True, stop=True),
                           reads=["sb0", "sb1"], writes=[pak], tag="l0a_sc")
                    last = (n + w == K)
                    if last and w > 128:
                        P.emit("dve", lambda e, n_=n, w_=w, b_=b, d=sc:
                               e.tensor_copy(d[:, n_:n_ + w_ - 128],
                                             self.pA[:, 512 * b_:512 * b_ + w_ - 128]),
                               reads=[pak], writes=[sck], tag="l0a_sev")
                    elif not last:
                        P.emit("dve", lambda e, n_=n, w_=w, b_=b, d=sc:
                               e.tensor_copy(d[:, n_:n_ + w_],
                                             self.pA[:, 512 * b_:512 * b_ + w_]),
                               reads=[pak], writes=[sck], tag="l0a_sev")
                    if last:
                        P.emit("dve", lambda e, K_=K, w_=w, b_=b, d=sc:
                               e.tensor_add(d[:, K_ - 128:K_],
                                            self.pA[:, 512 * b_ + w_ - 128:512 * b_ + w_],
                                            S["c_causal"][:, :]),
                               reads=[pak, "c_causal"], writes=[sck], tag="l0a_sevd")

            def stageBC(u):
                h, qt = units[u]
                K = 128 * (qt + 1)
                ms = u % 4
                sck = "sc0" if u % 2 == 0 else "sc1"
                sc = S[sck]
                prk = f"sb3_{u % 2}"
                pr = S["sb3"][:, u % 2]
                P.emit("dve", lambda e, K_=K, s=sc, ms_=ms:
                       e.tensor_reduce(S["mx"][:, ms_:ms_ + 1], s[:, 0:K_], AX.X,
                                       ALU.max, negate=True),
                       reads=[sck], writes=[f"mx{ms}"], tag="l0a_max")
                P.emit("act", lambda e, K_=K, s=sc, ms_=ms:
                       e.activation(s[:, 0:K_], s[:, 0:K_], AF.Exp,
                                    bias=S["mx"][:, ms_:ms_ + 1],
                                    accum_out=S["mx"][:, 4 + ms_:5 + ms_]),
                       reads=[sck, f"mx{ms}"],
                       writes=[sck, f"mx{4 + ms}"], tag="l0a_exp")
                P.emit("dve", lambda e, ms_=ms:
                       e.reciprocal(S["mx"][:, 4 + ms_:5 + ms_],
                                    S["mx"][:, 4 + ms_:5 + ms_]),
                       reads=[f"mx{4 + ms}"], writes=[f"mx{4 + ms}"], tag="l0a_rc")
                P.emit("dve", lambda e, K_=K, s=sc, p=pr, ms_=ms:
                       e.tensor_scalar_mul(p[:, 0:K_], s[:, 0:K_],
                                           S["mx"][:, 4 + ms_:5 + ms_]),
                       reads=[sck, f"mx{4 + ms}"], writes=[prk], tag="l0a_nrm")

            def stageDE(u):
                h, qt = units[u]
                hb = 64 * h
                nk_ = qt + 1
                prk = "sb3" if u % 2 == 0 else "sb2"
                pr = S["sb3"][:, u % 2]
                pvb = 2 + u % 2
                pvk = f"pB{pvb}"
                pvview = self.pB[0:64, 512 * pvb:512 * pvb + 128]
                for t0 in range(0, nk_, 4):
                    ntile = min(4, nk_ - t0)
                    g = t0 // 4
                    pb = g % 2
                    pbk = f"pB{pb}"
                    pslot = 4 * (g % 2)
                    pbview = self.pB[:, 512 * pb:512 * (pb + 1)].bitcast(BF16)
                    for i in range(ntile):
                        P.emit("pe", lambda e, i_=i, t_=t0 + i, v=pbview, p=pr:
                               e.transpose(v[:, 128 * i_:128 * (i_ + 1)],
                                           p[:, 128 * t_:128 * (t_ + 1)],
                                           S["c_idn"][:, :]),
                               reads=[prk, "c_idn"], writes=[pbk], tag="l0a_ptr")
                    P.emit("dve", lambda e, ps_=pslot, n_=ntile, v=pbview:
                           e.tensor_copy(S["prT"][:, ps_:ps_ + n_, :]
                                         .rearrange("p a b -> p (a b)"),
                                         v[:, 0:128 * n_]),
                           reads=[pbk], writes=[f"prT{g % 2}"], tag="l0a_pev")
                    for i in range(ntile):
                        kt = t0 + i
                        P.emit("pe", lambda e, i_=i, kt_=kt, ps_=pslot, hb_=hb,
                               v=pvview, vt=S["vtm"][:, pj],
                               f=(kt == 0), s=(kt == nk_ - 1):
                               e.matmul(v, vt[:, kt_, hb_:hb_ + 64],
                                        S["prT"][:, ps_ + i_, :], start=f, stop=s),
                               reads=[f"vtm{pj}", f"prT{g % 2}"], writes=[pvk],
                               tag="l0a_pv")
                jj = units[u]
                P.emit("act", lambda e, j_=j, hb_=hb, qt_=qt, v=pvview:
                       e.activation(S["ab"][hb_:hb_ + 64, j_, 128 * qt_:128 * (qt_ + 1)],
                                    v, AF.Copy),
                       reads=[pvk], writes=[f"ab{j}"], tag="l0a_oev")

            for u in range(nu + 1):
                if u < nu:
                    stageA(u)
                    stageBC(u)
                if u >= 1:
                    stageDE(u - 1)

        rhs2 = lambda k, n, w: S["ab"][:, k, n:n + w]
        rhsk2 = lambda k: [f"ab{k}"]

        def evac(og, ps, which, pkeys, m):
            P.emit("dve", lambda e, og_=og, p=ps:
                   e.scalar_tensor_tensor(S["xs"][:, og_, 1:T + 1], p[:, 0:T],
                                          S["cols"][:, 24 + og_:25 + og_],
                                          S["xs"][:, og_, 1:T + 1],
                                          ALU.add, ALU.add),
                   reads=pkeys + [f"xs{og}", "colsA"], writes=[f"xs{og}"],
                   tag="l0a_wo_ev")

        self.proj(self.WB["w0oT"], (0, 1024), rhs2, rhsk2, evac, tag="l0a_wo",
                  src_key="b_w0oT")
        self.ln(lambda c: S["xs"][:, c, 1:T + 1], lambda c: f"xs{c}", T,
                out_f32_of_c=lambda c: S["xs"][:, c, 1:T + 1],
                outkeys_f32=lambda c: f"xs{c}", tag="l0a_ln")

    def l0_ffn(self):
        T = self.Tcur
        P, S, D = self.P, self.S, self.D

        def build_mix(mu_off, om_off, tag):
            for c in range(8):
                P.emit("dve", lambda e, c_=c, o_=om_off: e.tensor_scalar_mul(
                    S["sc0"][:, 0:T], S["xs"][:, c_, 0:T],
                    S["cols"][:, o_ + c_:o_ + c_ + 1]),
                    reads=[f"xs{c}", "colsA"], writes=["sc0"], tag=tag + "_p1")
                P.emit("dve", lambda e, c_=c, m_=mu_off: e.scalar_tensor_tensor(
                    S["xb"][:, c_, 0:T], S["xs"][:, c_, 1:T + 1],
                    S["cols"][:, m_ + c_:m_ + c_ + 1],
                    S["sc0"][:, 0:T], ALU.mult, ALU.add),
                    reads=[f"xs{c}", "colsA", "sc0"], writes=[f"xb{c}"],
                    tag=tag + "_p2")

        rhs = lambda k, n, w: S["xb"][:, k, n:n + w]
        rhsk = lambda k: [f"xb{k}"]

        build_mix(40, 56, "l0f_xk")

        def evac_k(og, ps, which, pkeys, m):
            P.emit("dve", lambda e, p=ps: e.tensor_scalar_max(
                S["sc1"][:, 0:T], p[:, 0:T], 0.0),
                reads=pkeys, writes=["sc1"], tag="l0f_kre")
            P.emit("act", lambda e, og_=og: e.activation(
                S["ab"][:, og_, 0:T], S["sc1"][:, 0:T], AF.Square),
                reads=["sc1"], writes=[f"ab{og}"], tag="l0f_ksq")

        self.proj(self.WB["w0kT"], (0, 1024), rhs, rhsk, evac_k, tag="l0f_k",
                  src_key="b_w0kT")

        build_mix(32, 48, "l0f_xr")
        rhs2 = lambda k, n, w: S["ab"][:, k, n:n + w]
        rhsk2 = lambda k: [f"ab{k}"]
        for og in range(8):
            buf = og % 2
            self.wload(buf, self.WB["w0rT"], 0, [128] * 8, 128 * og, 128,
                       tag="l0f_wr", src_key="b_w0rT")
            pk1 = self.mm(self.pA, "A", rhs,
                          lambda k, b=buf: S["wk"][:, b, k, 0:128], nk=8, T=T,
                          rhs_keys=rhsk, lhsT_keys=lambda k, b=buf: [f"wk{b}"],
                          tag="l0f_r")
            P.emit("act", lambda e: e.activation(S["sc0"][:, 0:T],
                                                 self.pA[:, 0:T], AF.Sigmoid),
                   reads=pk1, writes=["sc0"], tag="l0f_sig")
            self.wload(buf, self.WB["w0vT"], 0, [128] * 8, 128 * og, 128,
                       tag="l0f_wv", src_key="b_w0vT")
            pk2 = self.mm(self.pB, "B", rhs2,
                          lambda k, b=buf: S["wk"][:, b, k, 0:128], nk=8, T=T,
                          rhs_keys=rhsk2, lhsT_keys=lambda k, b=buf: [f"wk{b}"],
                          tag="l0f_v")
            P.emit("dve", lambda e: e.tensor_mul(S["sc1"][:, 0:T],
                                                 self.pB[:, 0:T], S["sc0"][:, 0:T]),
                   reads=pk2 + ["sc0"], writes=["sc1"], tag="l0f_rv")
            P.emit("dve", lambda e, og_=og: e.tensor_add(
                S["xs"][:, og_, 1:T + 1], S["xs"][:, og_, 1:T + 1], S["sc1"][:, 0:T]),
                reads=["sc1", f"xs{og}"], writes=[f"xs{og}"], tag="l0f_res")
        self.ln(lambda c: S["xs"][:, c, 1:T + 1], lambda c: f"xs{c}", T,
                out_f32_of_c=lambda c: S["xs"][:, c, 1:T + 1],
                outkeys_f32=lambda c: f"xs{c}", tag="l0f_ln")

    def l1_attn(self):
        T = self.Tcur
        P, S, D = self.P, self.S, self.D
        ns = T // 128
        g = int(math.ceil(math.sqrt(T)))
        nfull = T // g
        tail = T - nfull * g
        assert nfull % 2 == 0, "tail block must be even-indexed"
        pstep_xs = 8 * (T + 1)
        pstep_xb = 8 * T
        neven = nfull // 2
        for c in range(8):
            base_s = c * (T + 1) + 1
            base_d = c * T
            P.emit("pool", lambda e, bs=base_s, bd=base_d, g_=g, ne=neven,
                   pb=pstep_xb, px=pstep_xs:
                   e.tensor_copy(bass.AP(S["xb"], bd, [[pb, 128], [2 * g_, ne], [1, g_]]),
                                 bass.AP(S["xs"], bs, [[px, 128], [2 * g_, ne], [1, g_]])),
                   reads=[f"xs{c}"], writes=[f"xb{c}"], tag="l1a_pme")
            P.emit("pool", lambda e, bs=base_s, bd=base_d, g_=g, ne=neven,
                   pb=pstep_xb, px=pstep_xs:
                   e.tensor_copy(bass.AP(S["xb"], bd + g_, [[pb, 128], [2 * g_, ne], [1, g_]]),
                                 bass.AP(S["xs"], bs + 2 * g_ - 1, [[px, 128], [2 * g_, ne], [-1, g_]])),
                   reads=[f"xs{c}"], writes=[f"xb{c}"], tag="l1a_pmo")
            if tail:
                P.emit("pool", lambda e, c_=c, g_=g, nf=nfull, tl=tail:
                       e.tensor_copy(S["xb"][:, c_, nf * g_:nf * g_ + tl],
                                     S["xs"][:, c_, 1 + nf * g_:1 + nf * g_ + tl]),
                       reads=[f"xs{c}"], writes=[f"xb{c}"], tag="l1a_pmt")
        rhs = lambda k, n, w: S["xb"][:, k, n:n + w]
        rhsk = lambda k: [f"xb{k}"]

        for j in range(8):
            pj = j % 2
            role_dst = {"sb0": (S["sb0"][:, 0:T], "sb0"),
                        "sb1": (S["sb1"][:, 0:T], "sb1"),
                        "sb2": (S["sb2"][:, pj, 0:T], f"sb2_{pj}")}
            for role, colbase, dstname in (("q", 128 * j, "sb0"),
                                           ("k", 1024 + 128 * j, "sb1"),
                                           ("v", 2048 + 128 * j, "sb2")):
                buf = 0 if role != "k" else 1
                self.wload(buf, self.WB["w1qkvT"], 0, [128] * 8, colbase, 128,
                           tag="l1a_w" + role, src_key="b_w1qkvT")
                which = "A" if role != "k" else "B"
                ps = self.pA if which == "A" else self.pB
                pkeys = self.mm(ps, which, rhs, lambda k, b=buf: S["wk"][:, b, k, 0:128],
                                nk=8, T=T, rhs_keys=rhsk,
                                lhsT_keys=lambda k, b=buf: [f"wk{b}"], tag="l1a_qkv")
                P.emit("act", lambda e, d=role_dst[dstname][0], p=ps[:, 0:T],
                       rs=(0.125 if role == "q" else 1.0):
                       e.activation(d, p, AF.Identity, scale=rs),
                       reads=pkeys, writes=[role_dst[dstname][1]],
                       tag="l1a_ev" + role)
            for s4 in range(0, ns, 4):
                nseg = min(4, ns - s4)
                for i in range(nseg):
                    P.emit("pe", lambda e, i_=i, s_=s4 + i, v2=S["sb2"][:, pj]:
                           e.transpose(self.pA[:, 0:512].bitcast(BF16)[0:64, 128 * i_:128 * (i_ + 1)],
                                       v2[:, 128 * s_:128 * (s_ + 1):2],
                                       S["c_idn"][:, :]),
                           reads=[f"sb2_{pj}", "c_idn"], writes=["pA0"], tag="l1a_vtr")
                P.emit("dve", lambda e, s4_=s4, n_=nseg, vt=S["vtm"][:, pj]:
                       e.tensor_copy(vt[0:64, s4_:s4_ + n_, :].rearrange("p a b -> p (a b)"),
                                     self.pA[:, 0:512].bitcast(BF16)[0:64, 0:128 * n_]),
                       reads=["pA0"], writes=[f"vtm{pj}"], tag="l1a_vev")
            units = [(h, s) for h in (0, 1) for s in range(ns)]
            nu = len(units)

            def stageA(u):
                h, s = units[u]
                hb = 64 * h
                sbank = 512 * (u % 4)
                pak = f"pB{u % 4}"
                P.emit("pe", lambda e, s_=s, hb_=hb, n_=sbank:
                       e.matmul(self.pB[:, n_:n_ + 64],
                                S["sb0"][hb_:hb_ + 64, 128 * s_:128 * (s_ + 1)],
                                S["sb1"][hb_:hb_ + 64, 128 * s_:128 * (s_ + 1):2],
                                start=True, stop=True),
                       reads=["sb0", "sb1"], writes=[pak], tag="l1a_sc")

            def stageBC(u):
                ms = u % 4
                u3 = u % 3
                sbank = 512 * ms
                pak = f"pB{ms}"
                prk = f"pr2_{u3}"
                P.emit("dve", lambda e, n_=sbank, ms_=ms:
                       e.tensor_reduce(S["mx"][:, ms_:ms_ + 1],
                                       self.pB[:, n_:n_ + 64], AX.X,
                                       ALU.max, negate=True),
                       reads=[pak], writes=[f"mx{ms}"], tag="l1a_max")
                P.emit("act", lambda e, n_=sbank, ms_=ms, b_=u3:
                       e.activation(S["pr2"][:, b_, 0:64], self.pB[:, n_:n_ + 64],
                                    AF.Exp, bias=S["mx"][:, ms_:ms_ + 1],
                                    accum_out=S["mx"][:, 4 + ms_:5 + ms_]),
                       reads=[pak, f"mx{ms}"],
                       writes=[prk, f"mx{4 + ms}"], tag="l1a_exp")
                P.emit("dve", lambda e, ms_=ms:
                       e.reciprocal(S["mx"][:, 4 + ms_:5 + ms_],
                                    S["mx"][:, 4 + ms_:5 + ms_]),
                       reads=[f"mx{4 + ms}"], writes=[f"mx{4 + ms}"], tag="l1a_rc")
                P.emit("dve", lambda e, ms_=ms, b_=u3:
                       e.tensor_scalar_mul(S["pr2"][:, b_, 0:64],
                                           S["pr2"][:, b_, 0:64],
                                           S["mx"][:, 4 + ms_:5 + ms_]),
                       reads=[prk, f"mx{4 + ms}"], writes=[prk], tag="l1a_nrm")

            def stageD(u):
                ms = u % 4
                u3 = u % 3
                prk = f"pr2_{u3}"
                pak = f"pB{ms}"
                pbview = self.pB[:, 512 * ms:512 * (ms + 1)].bitcast(BF16)
                P.emit("pe", lambda e, b_=u3, v=pbview:
                       e.transpose(v[0:64, 0:128],
                                   S["pr2"][:, b_, 0:64], S["c_idn"][:, :]),
                       reads=[prk, "c_idn"], writes=[pak], tag="l1a_ptr")
                P.emit("dve", lambda e, b_=u3, v=pbview:
                       e.tensor_copy(S["prT"][0:64, b_, :], v[0:64, 0:128]),
                       reads=[pak], writes=[f"prT{u3}"], tag="l1a_pev")

            def stageE(u):
                h, s = units[u]
                hb = 64 * h
                ms = u % 4
                u3 = u % 3
                pak = f"pB{ms}"
                pvview = self.pB[0:64, 512 * ms + 384:512 * ms + 512]
                P.emit("pe", lambda e, s_=s, hb_=hb, b_=u3, v=pvview,
                       vt=S["vtm"][:, pj]:
                       e.matmul(v, vt[0:64, s_, hb_:hb_ + 64],
                                S["prT"][0:64, b_, :], start=True, stop=True),
                       reads=[f"vtm{pj}", f"prT{u3}"], writes=[pak], tag="l1a_pv")
                P.emit("act", lambda e, j_=j, hb_=hb, s_=s, v=pvview:
                       e.activation(S["ab"][hb_:hb_ + 64, j_, 128 * s_:128 * (s_ + 1)],
                                    v, AF.Copy),
                       reads=[pak], writes=[f"ab{j}"], tag="l1a_oev")

            for u in range(nu + 4):
                if u >= 4:
                    stageE(u - 4)
                if u < nu:
                    stageA(u)
                    stageBC(u)
                if u >= 2 and u - 2 < nu:
                    stageD(u - 2)
        rhs2 = lambda k, n, w: S["ab"][:, k, n:n + w]
        rhsk2 = lambda k: [f"ab{k}"]

        def evac(og, ps, which, pkeys, m):
            P.emit("dve", lambda e, og_=og, p=ps:
                   e.tensor_add(S["xs"][:, og_, 1:T + 1], S["xs"][:, og_, 1:T + 1],
                                p[:, 0:T]),
                   reads=pkeys + [f"xs{og}"], writes=[f"xs{og}"], tag="l1a_wo_ev")

        self.proj(self.WB["w1oT"], (0, 1024), rhs2, rhsk2, evac, tag="l1a_wo",
                  src_key="b_w1oT")
        self.ln(lambda c: S["xs"][:, c, 1:T + 1], lambda c: f"xs{c}", T,
                out_f32_of_c=lambda c: S["xs"][:, c, 1:T + 1],
                outkeys_f32=lambda c: f"xs{c}", tag="l1a_ln")

    def l1_ffn(self):
        T = self.Tcur
        P, S, D = self.P, self.S, self.D
        nt = T // 128
        P.dma("sp", lambda e: e.dma_start(out=D["xpark"][:, :, 0:T],
                                          in_=S["xs"][:, :, 1:T + 1]),
              reads=[f"xs{c}" for c in range(8)], writes=["xpark"], tag="park")
        for l in range(L1F_DEPTH):
            self._l1f_layer(l)
        if L1F_STOP is not None:
            return
        for c in range(8):
            P.dma("sp", lambda e, c_=c: e.dma_start(out=S["sc0"][:, 0:T],
                                                    in_=D["xpark"][:, c_, 0:T]),
                  reads=["xpark"], writes=["sc0"], tag="unpark")
            P.emit("dve", lambda e, c_=c: e.tensor_add(
                S["xs"][:, c_, 1:T + 1], S["xs"][:, c_, 1:T + 1], S["sc0"][:, 0:T]),
                reads=["sc0", f"xs{c}"], writes=[f"xs{c}"], tag="unpark_add")
        self.ln(lambda c: S["xs"][:, c, 1:T + 1], lambda c: f"xs{c}", T,
                out_f32_of_c=lambda c: S["xs"][:, c, 1:T + 1],
                outkeys_f32=lambda c: f"xs{c}", tag="l1f_lnF")

    def _l1f_layer(self, l):
        T = self.Tcur
        P, S, D = self.P, self.S, self.D
        nt = T // 128
        P.dma("sp", lambda e, l_=l: e.dma_start(out=S["cols"][:, 76:84], in_=D["flng"][l_]),
              writes=["colsL"], tag="ld_lng")
        P.dma("sp", lambda e, l_=l: e.dma_start(out=S["cols"][:, 84:92], in_=D["flnb"][l_]),
              writes=["colsL"], tag="ld_lnb")
        P.dma("sp", lambda e, l_=l: e.dma_start(out=S["cols"][:, 92:100], in_=D["fb2"][l_]),
              writes=["colsL"], tag="ld_b2")
        P.dma("sp", lambda e, l_=l: e.dma_start(out=S["cols"][:, 100:122], in_=D["fb1a"][l_]),
              writes=["colsB"], tag="ld_b1a")
        P.dma("sp", lambda e, l_=l: e.dma_start(out=S["cols"][:, 122:144], in_=D["fb1g"][l_]),
              writes=["colsB"], tag="ld_b1g")
        if "attn" not in L1F_PARTS:
            self._l1f_ffn_part(l)
            return
        for c in range(8):
            src = (lambda c_: S["xs"][:, c_, 1:T + 1]) if c < 4 else \
                  (lambda c_: S["xs"][:, c_, 0:T])
            P.emit("pool", lambda e, c_=c, s=src(c): e.tensor_copy(
                S["xb"][:, c_, 0:T], s),
                reads=[f"xs{c}"], writes=[f"xb{c}"], tag="l1f_sh")
        rhs = lambda k, n, w: S["xb"][:, k, n:n + w]
        rhsk = lambda k: [f"xb{k}"]
        wq = self.WB["fWqkv"][l]
        wqk = f"b_fWqkv{l}"

        def prep_gen(j):
            pj = j % 2
            for role in ("q", "k", "v"):
                colbase = {"q": 128 * j, "k": 1024 + 128 * j,
                           "v": 2048 + 128 * j}[role]
                buf = 0 if role in ("q", "v") else 1
                self.wload(buf, wq, 0, [128] * 8, colbase, 128,
                           tag="l1f_w" + role, src_key=wqk)
                yield
                pkeys = self.mm(self.pA, "A", rhs,
                                lambda k, b=buf: S["wk"][:, b, k, 0:128],
                                nk=8, T=T, rhs_keys=rhsk,
                                lhsT_keys=lambda k, b=buf: [f"wk{b}"], tag="l1f_qkv")
                yield
                if role == "v":
                    P.emit("act", lambda e: e.activation(S["sb0"][:, 0:T],
                                                         self.pA[:, 0:T], AF.Copy),
                           reads=pkeys, writes=["sb0"], tag="l1f_evv")
                    yield
                    for s4 in range(0, nt, 4):
                        nseg = min(4, nt - s4)
                        for i in range(nseg):
                            P.emit("pe", lambda e, i_=i, t_=s4 + i:
                                   e.transpose(self.pA[:, 0:512].bitcast(BF16)
                                               [:, 128 * i_:128 * (i_ + 1)],
                                               S["sb0"][:, 128 * t_:128 * (t_ + 1)],
                                               S["c_idn"][:, :]),
                                   reads=["sb0", "c_idn"], writes=["pA0"],
                                   tag="l1f_vtr")
                        P.emit("dve", lambda e, s4_=s4, n_=nseg, vt=S["vtm"][:, pj]:
                               e.tensor_copy(vt[:, s4_:s4_ + n_, :]
                                             .rearrange("p a b -> p (a b)"),
                                             self.pA[:, 0:512].bitcast(BF16)[:, 0:128 * n_]),
                               reads=["pA0"], writes=[f"vtm{pj}"], tag="l1f_vev")
                        yield
                    continue
                pkA = self.pk("A", 0, T)
                scol = S["cols"][:, 64 + l:65 + l] if role == "q" else \
                    S["cols"][:, 70 + l:71 + l]
                sstab = S["c_qss"] if role == "q" else S["c_kss"]
                cstab = S["c_qcs"] if role == "q" else S["c_kcs"]
                sskey = "c_qss" if role == "q" else "c_kss"
                cskey = "c_qcs" if role == "q" else "c_kcs"
                P.emit("act", lambda e: e.activation(S["sb0"][:, 0:T],
                                                     self.pA[:, 0:T], AF.Square),
                       reads=pkeys, writes=["sb0"], tag="l1f_sq")
                yield
                P.emit("act", lambda e, sc=scol: e.activation(
                    S["sc0"][:, 0:T], self.pA[:, 0:T], AF.Identity, scale=sc),
                    reads=pkeys + ["colsA"], writes=["sc0"], tag="l1f_evqk")
                yield
                for n in range(0, T, 512):
                    w = min(512, T - n)
                    P.emit("pe", lambda e, n_=n, w_=w:
                           e.matmul(self.pA[:, n_:n_ + w_], S["c_rot32"][:, :],
                                    S["sc0"][:, n_:n_ + w_], start=True, stop=True),
                           reads=["sc0", "c_rot32"], writes=[pkA[n // 512]],
                           tag="l1f_rot")
                yield
                P.emit("dve", lambda e, st=sstab: e.tensor_mul(
                    S["sc1"][:, 0:T], self.pA[:, 0:T], st[:, 0:T]),
                    reads=pkA + [sskey], writes=["sc1"], tag="l1f_t5")
                yield
                P.emit("pool", lambda e, ct=cstab: e.tensor_mul(
                    S["sc0"][:, 0:T], S["sc0"][:, 0:T], ct[:, 0:T]),
                    reads=["sc0", cskey], writes=["sc0"], tag="l1f_t6")
                yield
                P.emit("pool", lambda e: e.tensor_add(
                    S["sc0"][:, 0:T], S["sc0"][:, 0:T], S["sc1"][:, 0:T]),
                    reads=["sc0", "sc1"], writes=["sc0"], tag="l1f_u")
                yield
                for n in range(0, T, 512):
                    w = min(512, T - n)
                    P.emit("pe", lambda e, n_=n, w_=w:
                           e.matmul(self.pA[:, n_:n_ + w_], S["c_blockones"][:, :],
                                    S["sb0"][:, n_:n_ + w_], start=True, stop=True),
                           reads=["sb0", "c_blockones"], writes=[pkA[n // 512]],
                           tag="l1f_nsum")
                yield
                P.emit("act", lambda e: e.activation(S["sc1"][:, 0:T],
                                                     self.pA[:, 0:T], AF.Ln,
                                                     bias=S["cols"][:, 145:146]),
                       reads=pkA + ["colsE"], writes=["sc1"], tag="l1f_nln")
                P.emit("act", lambda e: e.activation(S["sc1"][:, 0:T],
                                                     S["sc1"][:, 0:T], AF.Exp,
                                                     scale=-0.5),
                       reads=["sc1"], writes=["sc1"], tag="l1f_nexp")
                yield
                dst = "sb2" if role == "q" else "sb3"
                P.emit("dve", lambda e, d=S[dst][:, pj]: e.tensor_mul(
                    d[:, 0:T], S["sc0"][:, 0:T], S["sc1"][:, 0:T]),
                    reads=["sc0", "sc1"], writes=[f"{dst}_{pj}"], tag="l1f_" + dst)
                yield

        def attention(j, gen):
            pj = j % 2
            units = [(h, qt) for h in (0, 1) for qt in range(nt)]
            nu = len(units)
            qb = S["sb2"][:, pj]
            kb = S["sb3"][:, pj]
            qbk, kbk = f"sb2_{pj}", f"sb3_{pj}"
            vt = S["vtm"][:, pj]
            vtk = f"vtm{pj}"

            def stageA(u):
                h, qt = units[u]
                hb = 64 * h
                k0 = max(0, 128 * qt - 256)
                kw = 128 * (qt + 1) - k0
                sbank = 512 * (u % 4)
                pak = f"pB{u % 4}"
                mask = {128: S["c_causal"], 256: S["c_band384"][:, 128:384],
                        384: S["c_band384"]}[kw]
                maskk = {128: "c_causal", 256: "c_band384", 384: "c_band384"}[kw]
                P.emit("pe", lambda e, qt_=qt, hb_=hb, n_=sbank, k0_=k0, kw_=kw,
                       q_=qb, k_=kb:
                       e.matmul(self.pB[:, n_:n_ + kw_],
                                q_[hb_:hb_ + 64, 128 * qt_:128 * (qt_ + 1)],
                                k_[hb_:hb_ + 64, k0_:k0_ + kw_],
                                start=True, stop=False),
                       reads=[qbk, kbk], writes=[pak], tag="l1f_sc")
                P.emit("pe", lambda e, n_=sbank, kw_=kw, m=mask:
                       e.matmul(self.pB[:, n_:n_ + kw_], S["c_idn"][:, :],
                                m[:, 0:kw_], start=False, stop=True),
                       reads=["c_idn", maskk], writes=[pak], tag="l1f_scm")

            def stageBC(u):
                h, qt = units[u]
                k0 = max(0, 128 * qt - 256)
                kw = 128 * (qt + 1) - k0
                ms = u % 4
                u3 = u % 3
                sbank = 512 * ms
                pak = f"pB{ms}"
                prk = f"pr2_{u3}"
                P.emit("dve", lambda e, n_=sbank, kw_=kw, ms_=ms:
                       e.tensor_reduce(S["mx"][:, ms_:ms_ + 1],
                                       self.pB[:, n_:n_ + kw_], AX.X,
                                       ALU.max, negate=True),
                       reads=[pak], writes=[f"mx{ms}"], tag="l1f_max")
                P.emit("act", lambda e, n_=sbank, kw_=kw, b_=u3, ms_=ms:
                       e.activation(S["pr2"][:, b_, 0:kw_],
                                    self.pB[:, n_:n_ + kw_], AF.Exp,
                                    bias=S["mx"][:, ms_:ms_ + 1],
                                    accum_out=S["mx"][:, 4 + ms_:5 + ms_]),
                       reads=[pak, f"mx{ms}"],
                       writes=[prk, f"mx{4 + ms}"], tag="l1f_exp")
                P.emit("dve", lambda e, ms_=ms:
                       e.reciprocal(S["mx"][:, 4 + ms_:5 + ms_],
                                    S["mx"][:, 4 + ms_:5 + ms_]),
                       reads=[f"mx{4 + ms}"], writes=[f"mx{4 + ms}"], tag="l1f_rc")
                P.emit("dve", lambda e, kw_=kw, ms_=ms, b_=u3:
                       e.tensor_scalar_mul(S["pr2"][:, b_, 0:kw_],
                                           S["pr2"][:, b_, 0:kw_],
                                           S["mx"][:, 4 + ms_:5 + ms_]),
                       reads=[prk, f"mx{4 + ms}"], writes=[prk], tag="l1f_nrm")

            def stageD(u):
                h, qt = units[u]
                k0 = max(0, 128 * qt - 256)
                kw = 128 * (qt + 1) - k0
                nkc = kw // 128
                u4 = u % 4
                u3 = u % 3
                prk = f"pr2_{u3}"
                pak = f"pB{u4}"
                pbview = self.pB[:, 512 * u4:512 * (u4 + 1)].bitcast(BF16)
                for i in range(nkc):
                    P.emit("pe", lambda e, i_=i, b_=u3, v=pbview:
                           e.transpose(v[:, 128 * i_:128 * (i_ + 1)],
                                       S["pr2"][:, b_, 128 * i_:128 * (i_ + 1)],
                                       S["c_idn"][:, :]),
                           reads=[prk, "c_idn"], writes=[pak], tag="l1f_ptr")
                P.emit("dve", lambda e, n_=nkc, b_=u3, v=pbview:
                       e.tensor_copy(S["prT"][:, 3 * b_:3 * b_ + n_, :]
                                     .rearrange("p a b -> p (a b)"),
                                     v[:, 0:128 * n_]),
                       reads=[pak], writes=[f"prT{u3}"], tag="l1f_pev")

            def stageE(u):
                h, qt = units[u]
                hb = 64 * h
                k0 = max(0, 128 * qt - 256)
                kw = 128 * (qt + 1) - k0
                nkc = kw // 128
                u4 = u % 4
                u3 = u % 3
                pak = f"pB{u4}"
                pvview = self.pB[0:64, 512 * u4 + 384:512 * u4 + 512]
                for i in range(nkc):
                    ktg = k0 // 128 + i
                    P.emit("pe", lambda e, i_=i, ktg_=ktg, hb_=hb, b_=u3,
                           v=pvview, vt_=vt, f=(i == 0), s=(i == nkc - 1):
                           e.matmul(v, vt_[:, ktg_, hb_:hb_ + 64],
                                    S["prT"][:, 3 * b_ + i_, :], start=f, stop=s),
                           reads=[vtk, f"prT{u3}"], writes=[pak], tag="l1f_pv")
                P.emit("act", lambda e, j_=j, hb_=hb, qt_=qt, v=pvview:
                       e.activation(S["ab"][hb_:hb_ + 64, j_, 128 * qt_:128 * (qt_ + 1)],
                                    v, AF.Copy),
                       reads=[pak], writes=[f"ab{j}"], tag="l1f_oev")

            for u in range(nu + 4):
                if u >= 4:
                    stageE(u - 4)
                if u < nu:
                    stageA(u)
                    stageBC(u)
                if u >= 2 and u - 2 < nu:
                    stageD(u - 2)
                next(gen, None)
                next(gen, None)
            for _ in gen:
                pass

        for _ in prep_gen(0):
            pass
        if L1F_STOP == "qk":
            return
        for j in range(8):
            gen = prep_gen(j + 1) if j < 7 else iter(())
            attention(j, gen)

        rhs2 = lambda k, n, w: S["ab"][:, k, n:n + w]
        rhsk2 = lambda k: [f"ab{k}"]

        def evac(og, ps, which, pkeys, m):
            P.emit("dve", lambda e, og_=og, p=ps:
                   e.tensor_add(S["xs"][:, og_, 1:T + 1], S["xs"][:, og_, 1:T + 1],
                                p[:, 0:T]),
                   reads=pkeys + [f"xs{og}"], writes=[f"xs{og}"], tag="l1f_wo_ev")

        self.proj(self.WB["fWo"][l], (0, 1024), rhs2, rhsk2, evac, tag="l1f_wo",
                  src_key=f"b_fWo{l}")

        if "ffn" not in L1F_PARTS:
            return
        self._l1f_ffn_part(l)

    def _l1f_ffn_part(self, l):
        T = self.Tcur
        P, S, D = self.P, self.S, self.D
        for th in (1, 0):
            t0 = th * (T // 2)
            Th = T // 2
            zs = lambda c: (S["xs"][:, c, 1 + t0:1 + t0 + Th] if c < 4
                            else S["xs"][:, c, t0:t0 + Th])
            self.ln(zs, lambda c: f"xs{c}", Th,
                    out_bf16_of_c=lambda c: S["xb"][:, c, 0:Th],
                    outkeys_bf16=lambda c: f"xb{c}",
                    g_col=lambda c: S["cols"][:, 76 + c:77 + c],
                    b_col=lambda c: S["cols"][:, 84 + c:85 + c],
                    colkeys=["colsL"], tag="l1f_ln")
            if L1F_STOP == "xn":
                return
            rhs3 = lambda k, n, w: S["xb"][:, k, n:n + w]
            rhsk3 = lambda k: [f"xb{k}"]

            def hslot(i):
                if i < 16:
                    return S["ab"][:, i // 2, (i % 2) * Th:(i % 2) * Th + Th], f"ab{i // 2}"
                c = i - 16
                return S["xb"][:, c, T - Th:T], f"xb{c}"

            for i in range(22):
                m = 128 if i < 21 else 42
                self.wload(0, self.WB["fW1"][l], 0, [128] * 8, 128 * i, m,
                           tag="l1f_w1a", src_key=f"b_fW1{l}")
                self.wload(1, self.WB["fW1"][l], 0, [128] * 8, 2730 + 128 * i, m,
                           tag="l1f_w1g", src_key=f"b_fW1{l}")
                pp = self.pA if i % 2 == 0 else self.pB
                which = "A" if i % 2 == 0 else "B"
                gk = "sc0" if i % 2 == 0 else "sc1"
                pkA = self.mm(pp, which, rhs3,
                              lambda k: S["wk"][:, 0, k, 0:m], nk=8, T=Th,
                              rhs_keys=rhsk3, lhsT_keys=lambda k: ["wk0"],
                              tag="l1f_a", m=m, n0=0)
                pkB = self.mm(pp, which, rhs3,
                              lambda k: S["wk"][:, 1, k, 0:m], nk=8, T=Th,
                              rhs_keys=rhsk3, lhsT_keys=lambda k: ["wk1"],
                              tag="l1f_g", m=m, n0=Th)
                P.emit("act", lambda e, m_=m, i_=i, p=pp, g=S[gk]:
                       e.activation(g[0:m_, 0:Th], p[0:m_, Th:2 * Th],
                                    AF.Gelu, bias=S["cols"][0:m_, 122 + i_:123 + i_]),
                       reads=pkB + ["colsB"], writes=[gk], tag="l1f_gelu")
                hap, hkey = hslot(i)
                P.emit("dve", lambda e, m_=m, i_=i, d=hap, p=pp, g=S[gk]:
                       e.scalar_tensor_tensor(d[0:m_, :], p[0:m_, 0:Th],
                                              S["cols"][0:m_, 100 + i_:101 + i_],
                                              g[0:m_, 0:Th],
                                              ALU.add, ALU.mult),
                       reads=pkA + ["colsB", gk], writes=[hkey], tag="l1f_h")
            if L1F_STOP == "h":
                return
            for og in range(8):
                which = "A" if og % 2 == 0 else "B"
                ps = self.pA if which == "A" else self.pB

                def rhs4(k, n, w):
                    hap, _ = hslot(k)
                    rows = 128 if k < 21 else 42
                    return hap[0:rows, n:n + w]

                def rhsk4(k):
                    return [hslot(k)[1]]

                for half, (k0h, nch, buf) in enumerate(
                        ((0, 8, 0), (8, 8, 1), (16, 6, 0))):
                    rows = [128] * nch if k0h + nch <= 21 else [128] * (nch - 1) + [42]
                    self.wload(buf, self.WB["fW2"][l], 128 * k0h, rows,
                               128 * og, 128, tag="l1f_w2", src_key=f"b_fW2{l}")
                    pkeys = self.mm(ps, which,
                                    lambda k, n, w, k0h_=k0h: rhs4(k0h_ + k, n, w),
                                    lambda k, b_=buf, r_=rows: S["wk"][0:r_[k], b_, k, 0:128],
                                    nk=nch, T=Th,
                                    rhs_keys=lambda k, k0h_=k0h: rhsk4(k0h_ + k),
                                    lhsT_keys=lambda k, b_=buf: [f"wk{b_}"],
                                    tag="l1f_w2mm", open_group=(half == 0),
                                    close_group=(half == 2))
                P.emit("dve", lambda e, og_=og, p=ps, t0_=t0:
                       e.scalar_tensor_tensor(S["xs"][:, og_, 1 + t0_:1 + t0_ + Th],
                                              p[:, 0:Th],
                                              S["cols"][:, 92 + og_:93 + og_],
                                              S["xs"][:, og_, 1 + t0_:1 + t0_ + Th],
                                              ALU.add, ALU.add),
                       reads=pkeys + [f"xs{og}", "colsL"], writes=[f"xs{og}"],
                       tag="l1f_w2ev")
            if L1F_STOP == "half":
                return

    def build(self):
        self.alloc()
        P, S, D = self.P, self.S, self.D
        T = self.T
        self.Tcur = T
        self.emit_precast()
        for n in ("idn", "causal", "band384", "ones_ln",
                  "blockones", "qcs", "qss", "kcs", "kss", "blockones32",
                  "rot32"):
            self.load_const(n)
        for src, sl in (("b0qkv", (0, 24)), ("b0o", (24, 32)), ("mur", (32, 40)),
                        ("muk", (40, 48)), ("fqs", (64, 70)), ("fks", (70, 76))):
            P.dma("sp", lambda e, s=src, a=sl: e.dma_start(
                out=S["cols"][:, a[0]:a[1]], in_=D[s]),
                writes=["colsA"], tag=f"ld_{src}")
        P.emit("dve", lambda e: e.tensor_scalar_mul(S["cols"][:, 146:154],
                                                    S["cols"][:, 0:8], 0.125),
               reads=["colsA"], writes=["colsA"], tag="qbias8")
        P.emit("dve", lambda e: e.memset(S["cols"][:, 144:145], 1e-5),
               writes=["colsE"], tag="eps1")
        P.emit("dve", lambda e: e.memset(S["cols"][:, 145:146], 1e-24),
               writes=["colsE"], tag="eps2")
        P.emit("dve", lambda e: e.tensor_scalar(S["cols"][:, 48:56],
                                                S["cols"][:, 32:40], -1.0, 1.0,
                                                ALU.mult, ALU.add),
               reads=["colsA"], writes=["colsA"], tag="omr")
        P.emit("dve", lambda e: e.tensor_scalar(S["cols"][:, 56:64],
                                                S["cols"][:, 40:48], -1.0, 1.0,
                                                ALU.mult, ALU.add),
               reads=["colsA"], writes=["colsA"], tag="omk")
        P.emit("dve", lambda e: e.memset(S["xs"][:, :, 0:1], 0.0),
               writes=[f"xs{c}" for c in range(8)], tag="zerocol")
        for s in range(self.nseq):
            P.dma("sp", lambda e, s_=s: e.dma_start(
                out=S["xs"][:, :, 1:T + 1],
                in_=D["x_in"][s_].rearrange("(c p) t -> p c t", p=128)),
                reads=[], writes=[f"xs{c}" for c in range(8)], tag="ldx")
            for b in self.blocks:
                (self.l0_attn, self.l0_ffn, self.l1_attn, self.l1_ffn)[b]()
            P.dma("sp", lambda e, s_=s: e.dma_start(
                out=D["x_out"][s_].rearrange("(c p) t -> p c t", p=128),
                in_=S["xs"][:, :, 1:T + 1]),
                reads=[f"xs{c}" for c in range(8)], writes=[], tag="stx",
                is_output=True)
        self.P.materialize_and_run(self.stack)
        self.stack.close()
        return self.nc



def host_inputs(full, T=2048):
    h = {}
    h["w0qkvT"] = np.ascontiguousarray(full["L0_attn_W_qkv"].T)
    h["b0qkv"] = np.ascontiguousarray(full["L0_attn_b_qkv"].reshape(24, 128).T)
    h["w0oT"] = np.ascontiguousarray(full["L0_attn_W_o"].T)
    h["b0o"] = np.ascontiguousarray(full["L0_attn_b_o"].reshape(8, 128).T)
    h["w0rT"] = np.ascontiguousarray(full["L0_ffn_W_r"].T)
    h["w0kT"] = np.ascontiguousarray(full["L0_ffn_W_k"].T)
    h["w0vT"] = np.ascontiguousarray(full["L0_ffn_W_v"].T)
    h["mur"] = np.ascontiguousarray(full["L0_ffn_mu_r"].reshape(8, 128).T)
    h["muk"] = np.ascontiguousarray(full["L0_ffn_mu_k"].reshape(8, 128).T)
    h["w1qkvT"] = np.ascontiguousarray(full["L1_attn_w_qkv"].T)
    h["w1oT"] = np.ascontiguousarray(full["L1_attn_w_out"].T)
    h["fWqkv"] = np.ascontiguousarray(full["L1_ffn_Wqkv"])
    h["fqs"] = np.ascontiguousarray(
        np.concatenate([full["L1_ffn_q_scale"].T, full["L1_ffn_q_scale"].T], 0))
    h["fks"] = np.ascontiguousarray(
        np.concatenate([full["L1_ffn_k_scale"].T, full["L1_ffn_k_scale"].T], 0))
    h["fWo"] = np.ascontiguousarray(full["L1_ffn_Wo"])
    h["flng"] = np.ascontiguousarray(
        full["L1_ffn_ln_g"].reshape(6, 8, 128).transpose(0, 2, 1))
    h["flnb"] = np.ascontiguousarray(
        full["L1_ffn_ln_b"].reshape(6, 8, 128).transpose(0, 2, 1))
    h["fW1"] = np.ascontiguousarray(full["L1_ffn_W1"])
    b1 = full["L1_ffn_b1"]

    def colize22(v):
        out = np.zeros((6, 128, 22), dtype=np.float32)
        for l in range(6):
            x = np.zeros(22 * 128, dtype=np.float32)
            x[:2730] = v[l]
            out[l] = x.reshape(22, 128).T
        return out

    h["fb1a"] = colize22(b1[:, :2730])
    h["fb1g"] = colize22(b1[:, 2730:])
    h["fW2"] = np.ascontiguousarray(full["L1_ffn_W2"])
    h["fb2"] = np.ascontiguousarray(
        full["L1_ffn_b2"].reshape(6, 8, 128).transpose(0, 2, 1))
    for k, v in host_consts(T).items():
        h[k] = v
    for k in h:
        if h[k].dtype == np.float64:
            raise ValueError(k)
    return h


def kernel(**inputs):
    T, nseq, ncores = 2048, 4, 8
    shared = host_inputs(inputs, T)
    x = np.ascontiguousarray(
        np.asarray(inputs["L0_attn_x"]).transpose(0, 2, 1))
    core = Core(T, nseq, blocks=(0, 1, 2, 3))
    nc = core.build()
    in_maps = []
    for c in range(ncores):
        m = dict(shared)
        m["x_in"] = x[c * nseq:(c + 1) * nseq]
        in_maps.append(m)
    res = run_bass_kernel_spmd(nc, in_maps, core_ids=list(range(ncores)))
    out = np.empty((32, T, 1024), dtype=np.float32)
    for c in range(ncores):
        o = np.asarray(res.results[c]["x_out"])
        out[c * nseq:(c + 1) * nseq] = o.transpose(0, 2, 1)
    return out


# version 12
# speedup vs baseline: 1.1323x; 1.0144x over previous
import math
from contextlib import ExitStack

import numpy as np
import ml_dtypes
import concourse.bass as bass
import concourse.mybir as mybir
from concourse.bass_utils import run_bass_kernel_spmd

F32 = mybir.dt.float32
BF16 = mybir.dt.bfloat16
AF = mybir.ActivationFunctionType
ALU = mybir.AluOpType
AX = mybir.AxisListType
BF = ml_dtypes.bfloat16

ENGINES = ("pe", "dve", "act", "pool", "sp")
L1F_DEPTH = 6
L1F_PARTS = ("attn", "ffn")
L1F_STOP = None
SEM_WINDOW = 30000
NEG = -1.0e9



class Op:
    __slots__ = ("idx", "eng", "fn", "deps", "is_dma", "inc_needed", "done_key",
                 "done_val", "tag", "waits", "dma_sem_i")

    def __init__(self, idx, eng, fn, is_dma, tag):
        self.idx = idx
        self.eng = eng
        self.fn = fn
        self.deps = []
        self.is_dma = is_dma
        self.inc_needed = is_dma
        self.done_key = None
        self.done_val = None
        self.tag = tag
        self.waits = []
        self.dma_sem_i = -1


class Prog:
    def __init__(self, nc, n_dma_sems=48):
        self.nc = nc
        self.ops = []
        self.last_write = {}
        self.readers = {}
        self.n_dma_sems = n_dma_sems
        self.dma_rr = 0
        self.out_dma_ops = []

    ALLOWED_FREE = frozenset({"self", "S", "D", "P", "T", "nc", "Th"})

    def _check_fn(self, fn, tag):
        code = getattr(fn, "__code__", None)
        if code is None:
            return
        bad = [v for v in code.co_freevars if v not in self.ALLOWED_FREE]
        if bad:
            raise RuntimeError(
                f"op '{tag}': lambda captures loop variables by reference: {bad}; "
                "bind them as default arguments")

    def emit(self, eng, fn, reads=(), writes=(), tag=""):
        self._check_fn(fn, tag)
        op = Op(len(self.ops), eng, fn, False, tag)
        self._track(op, reads, writes)
        return op

    def dma(self, queue, fn, reads=(), writes=(), tag="", is_output=False):
        self._check_fn(fn, tag)
        op = Op(len(self.ops), queue, fn, True, tag)
        if queue == "pool":
            half = self.n_dma_sems // 2
            op.dma_sem_i = self.dma_rr % half
        else:
            half = self.n_dma_sems // 2
            op.dma_sem_i = half + self.dma_rr % half
        self.dma_rr += 1
        self._track(op, reads, writes)
        if is_output:
            self.out_dma_ops.append(op)
        return op

    ALIASES = {
        **{f"xs{c}": (f"xs{c}a", f"xs{c}b") for c in range(8)},
        "sb2": ("sb2_0", "sb2_1"),
        "sb3": ("sb3_0", "sb3_1"),
        "vtm": ("vtm0", "vtm1"),
        "sc0": ("scS0", "scS1", "scS2", "scS3"),
        "pr2": ("pr2_0", "pr2_1", "pr2_2"),
        "prT": ("prT0", "prT1", "prT2"),
    }

    def _expand(self, keys):
        out = list(keys)
        for k in keys:
            out.extend(self.ALIASES.get(k, ()))
        return out

    def _track(self, op, reads, writes):
        reads = self._expand(reads)
        writes = self._expand(writes)
        lw = self.last_write
        rd = self.readers
        deps = {}
        for r in reads:
            d = lw.get(r)
            if d is not None:
                deps[d.idx] = d
        for w in writes:
            d = lw.get(w)
            if d is not None:
                deps[d.idx] = d
            for d in rd.get(w, ()):
                deps[d.idx] = d
        op.deps = [d for d in deps.values()
                   if not (op.eng == "pe" and d.eng == "pe" and not d.is_dma)]
        for r in reads:
            rd.setdefault(r, []).append(op)
        for w in writes:
            lw[w] = op
            rd[w] = [op]
        self.ops.append(op)

    def materialize_and_run(self, stack):
        nc = self.nc
        for op in self.ops:
            for d in op.deps:
                d.inc_needed = True
        eng_inc = dict.fromkeys(ENGINES, 0)
        dma_targets = [0] * self.n_dma_sems
        for op in self.ops:
            if op.is_dma:
                dma_targets[op.dma_sem_i] += 16
                op.done_key = ("dma", op.dma_sem_i)
                op.done_val = dma_targets[op.dma_sem_i]
            elif op.inc_needed:
                eng_inc[op.eng] += 1
                i = eng_inc[op.eng]
                w = (i - 1) // SEM_WINDOW
                op.done_key = ("eng", op.eng, w)
                op.done_val = i - w * SEM_WINDOW
        self.sems = {}
        for eng in ENGINES:
            n = eng_inc[eng]
            for w in range((n + SEM_WINDOW - 1) // SEM_WINDOW):
                self.sems[("eng", eng, w)] = stack.enter_context(
                    nc.semaphore(f"s_{eng}_{w}"))
        for i in range(self.n_dma_sems):
            if dma_targets[i]:
                self.sems[("dma", i)] = stack.enter_context(nc.semaphore(f"d_{i}"))
        clocks = {eng: {} for eng in ENGINES}
        dma_prev = [0] * self.n_dma_sems
        for op in self.ops:
            clk = clocks[op.eng]
            waits = {}
            if op.is_dma:
                prev = dma_prev[op.dma_sem_i]
                if prev and clk.get(("dma", op.dma_sem_i), 0) < prev:
                    waits[("dma", op.dma_sem_i)] = prev
                dma_prev[op.dma_sem_i] = op.done_val
            for d in op.deps:
                k, v = d.done_key, d.done_val
                if clk.get(k, 0) < v and waits.get(k, 0) < v:
                    waits[k] = v
            for k, v in waits.items():
                clk[k] = max(clk.get(k, 0), v)
            op.waits = sorted(waits.items())
        fin = {}
        for i in range(self.n_dma_sems):
            if dma_targets[i]:
                fin[("dma", i)] = dma_targets[i]
        final_waits = sorted(fin.items())

        block = stack.enter_context(nc.Block())
        per_eng = {e: [] for e in ENGINES}
        for op in self.ops:
            per_eng[op.eng].append(op)
        sems = self.sems

        def runner(ops_list, is_sp):
            def run(eng):
                for op in ops_list:
                    for k, v in op.waits:
                        eng.wait_ge(sems[k], v)
                    inst = op.fn(eng)
                    if op.is_dma:
                        inst.then_inc(sems[("dma", op.dma_sem_i)], 16)
                    elif op.inc_needed:
                        inst.then_inc(sems[op.done_key], 1)
                if is_sp:
                    for k, v in final_waits:
                        eng.wait_ge(sems[k], v)
            return run

        block.tensor(runner(per_eng["pe"], False))
        block.vector(runner(per_eng["dve"], False))
        block.scalar(runner(per_eng["act"], False))
        block.gpsimd(runner(per_eng["pool"], False))
        block.sync(runner(per_eng["sp"], True))



def host_consts(T):
    c = {}
    c["idn"] = np.eye(128, dtype=BF)
    i = np.arange(128)[:, None]
    c["causal"] = np.where(i >= np.arange(128)[None, :], 0.0, NEG).astype(BF)
    jr = np.arange(384)[None, :]
    c["band384"] = np.where((jr >= i) & (jr <= i + 256), 0.0, NEG).astype(BF)
    jr = np.arange(256)[None, :]
    c["band256"] = np.where(jr <= i + 128, 0.0, NEG).astype(BF)
    c["ones_ln"] = np.full((128, 128), 1.0 / 1024.0, dtype=BF)
    bd = np.zeros((128, 128), dtype=np.float32)
    bd[:64, :64] = 1.0
    bd[64:, 64:] = 1.0
    c["blockones"] = bd.astype(BF)
    R = np.zeros((128, 128), dtype=np.float32)
    for h0 in (0, 64):
        for m in range(32):
            R[h0 + m + 32, h0 + m] = -1.0
            R[h0 + m, h0 + m + 32] = 1.0
    c["rot"] = R.astype(BF)
    c["rot32"] = R.astype(np.float32)
    c["blockones32"] = bd.astype(np.float32)
    DIM_HEAD, WINDOW = 64, 256
    half = np.arange(0, DIM_HEAD, 2, dtype=np.float32)
    inv_freq = (1.0 / 10000.0 ** (half / DIM_HEAD)).astype(np.float32)
    t = np.arange(T, dtype=np.float32)
    freqs = t[:, None] * inv_freq[None, :]
    freqs = np.concatenate([freqs, freqs], -1)
    base_scale = ((half + 0.4 * DIM_HEAD) / (1.4 * DIM_HEAD)).astype(np.float32)
    power = (t - T // 2) / (WINDOW // 2)
    scale = base_scale[None, :] ** power[:, None]
    scale = np.concatenate([scale, scale], -1)
    cos, sin = np.cos(freqs), np.sin(freqs)
    qcs = (8.0 * cos * scale).T
    qss = (8.0 * sin * scale).T
    kcs = (cos / scale).T
    kss = (sin / scale).T
    for name, a in (("qcs", qcs), ("qss", qss), ("kcs", kcs), ("kss", kss)):
        c[name] = np.concatenate([a, a], 0).astype(BF)
    return c


def hilbert_runs(M):
    g = int(math.ceil(math.sqrt(M)))
    runs = []
    pos = 0
    r = 0
    while pos < M:
        n = min(g, M - pos)
        full_vals = list(range(r * g, r * g + g))
        if r % 2 == 1:
            full_vals = full_vals[::-1]
        vals = [v for v in full_vals if v < M][:n]
        step = 0 if len(vals) < 2 else vals[1] - vals[0]
        runs.append((vals[0], len(vals), step < 0))
        pos += len(vals)
        r += 1
    return runs



class Core:

    def __init__(self, T, nseq, blocks=(0, 1, 2, 3)):
        self.T = T
        self.nseq = nseq
        self.blocks = blocks
        self.nc = bass.Bass("TRN2", target_bir_lowering=False)
        self.P = Prog(self.nc)
        self.wq_rr = 0
        self.stack = ExitStack()
        self._declare_dram()

    def _declare_dram(self):
        nc, T, nseq = self.nc, self.T, self.nseq
        D = {}

        def inp(name, shape, dt=F32):
            D[name] = nc.dram_tensor(name, shape, dt, kind="ExternalInput").ap()

        inp("x_in", [nseq, 1024, T])
        inp("w0qkvT", [1024, 3072])
        inp("b0qkv", [128, 24])
        inp("w0oT", [1024, 1024])
        inp("b0o", [128, 8])
        inp("w0rT", [1024, 1024])
        inp("w0kT", [1024, 1024])
        inp("w0vT", [1024, 1024])
        inp("mur", [128, 8])
        inp("muk", [128, 8])
        inp("w1qkvT", [1024, 3072])
        inp("w1oT", [1024, 1024])
        inp("fWqkv", [6, 1024, 3072])
        inp("fqs", [128, 6])
        inp("fks", [128, 6])
        inp("fWo", [6, 1024, 1024])
        inp("flng", [6, 128, 8])
        inp("flnb", [6, 128, 8])
        inp("fW1", [6, 1024, 5460])
        inp("fb1a", [6, 128, 22])
        inp("fb1g", [6, 128, 22])
        inp("fW2", [6, 2730, 1024])
        inp("fb2", [6, 128, 8])
        inp("blockones32", [128, 128])
        inp("rot32", [128, 128])
        for n, shape, dt in (
            ("idn", [128, 128], BF16), ("causal", [128, 128], BF16),
            ("band384", [128, 384], BF16), ("band256", [128, 256], BF16),
            ("ones_ln", [128, 128], BF16), ("blockones", [128, 128], BF16),
            ("rot", [128, 128], BF16),
            ("qcs", [128, T], BF16), ("qss", [128, T], BF16),
            ("kcs", [128, T], BF16), ("kss", [128, T], BF16),
        ):
            inp(n, shape, dt)
        D["x_out"] = nc.dram_tensor("x_out", [nseq, 1024, T], F32,
                                    kind="ExternalOutput").ap()
        self.WB = {}
        for n in ("w0qkvT", "w0oT", "w0rT", "w0kT", "w0vT", "w1qkvT", "w1oT",
                  "fWqkv", "fWo", "fW1", "fW2"):
            self.WB[n] = nc.dram_tensor("b_" + n, list(D[n].shape), BF16).ap()
        D["xpark"] = nc.dram_tensor("xpark", [128, 8, T], F32).ap()
        self.D = D

    def alloc(self):
        nc, T = self.nc, self.T
        st = self.stack
        S = {}

        def sb(name, shape, dt):
            S[name] = st.enter_context(nc.sbuf_tensor(name, shape, dt))

        sb("xs", [128, 8, T + 1], F32)
        sb("xb", [128, 8, T], BF16)
        sb("ab", [128, 8, T], BF16)
        sb("sc0", [128, T], F32)
        sb("sc1", [128, T], F32)
        sb("sb0", [128, T], BF16)
        sb("sb1", [128, T], BF16)
        sb("sb2", [128, 2, T], BF16)
        sb("sb3", [128, 2, T], BF16)
        sb("vtm", [128, 2, T // 128, 128], BF16)
        sb("prT", [128, 9, 128], BF16)
        sb("pr2", [128, 3, 384], BF16)
        sb("wk", [128, 2, 8, 128], BF16)
        sb("mx", [128, 8], F32)
        sb("cols", [128, 160], F32)
        for n in ("idn", "causal", "band384", "ones_ln",
                  "blockones", "qcs", "qss", "kcs", "kss"):
            sb("c_" + n, list(self.D[n].shape), BF16)
        for n in ("blockones32", "rot32"):
            sb("c_" + n, list(self.D[n].shape), F32)
        self.S = S
        self.pA = st.enter_context(nc.psum_tensor("pA", [128, 512 * 4], F32))
        self.pB = st.enter_context(nc.psum_tensor("pB", [128, 512 * 4], F32))

    def pk(self, which, n0, n1):
        return [f"p{which}{i}" for i in range(n0 // 512, (n1 + 511) // 512)]

    def load_const(self, name):
        P, S, D = self.P, self.S, self.D
        dst = S["c_" + name]
        P.dma("sp", lambda e, d=dst, s=D[name]: e.dma_start(out=d[:, :] if len(d.shape) == 2 else d[:, :, :], in_=s),
              writes=[f"c_{name}"], tag=f"ld_{name}")

    def emit_precast(self):
        P, D = self.P, self.D
        singles = ["w0qkvT", "w0oT", "w0rT", "w0kT", "w0vT", "w1qkvT", "w1oT"]
        for n in singles:
            P.dma("pool", lambda e, s=D[n], d=self.WB[n]: e.dma_start(out=d, in_=s),
                  writes=["b_" + n], tag="precast_" + n)
        for l in range(6):
            for n in ("fWqkv", "fWo", "fW1", "fW2"):
                P.dma("pool", lambda e, s=D[n][l], d=self.WB[n][l]:
                      e.dma_start(out=d, in_=s),
                      writes=[f"b_{n}{l}"], tag=f"precast_{n}{l}")

    def load_cols(self, dst_sl, src_ap, key, tag=""):
        P, S = self.P, self.S
        P.dma("sp", lambda e, s=src_ap, d=S["cols"], sl=dst_sl:
              e.dma_start(out=d[:, sl], in_=s),
              writes=[key], tag=tag or f"cols_{key}")

    def wload(self, buf, w_hbm, rows0, nrows, col0, ncols, tag="", flat=False,
              src_key=None):
        P, S = self.P, self.S
        q = "sp"
        self.wq_rr += 1
        rk = [src_key] if src_key else []
        r0 = rows0
        fv = S["wk"][:, :, :, :].rearrange("p a b c -> p (a b c)") if flat else None
        nfull = 0
        while nfull < len(nrows) and nrows[nfull] == 128:
            nfull += 1
        if nfull > 1 and rows0 % 128 == 0:
            src3 = w_hbm[rows0:rows0 + 128 * nfull, col0:col0 + ncols].rearrange(
                "(k p) o -> p k o", p=128)
            if flat:
                dst3 = bass.AP(fv.tensor, fv.offset,
                               [list(fv.ap[0]), [128, nfull], [1, ncols]])                     if False else None
                dst3 = fv[:, 0:128 * nfull].rearrange("p (k o) -> p k o", o=128)[:, :, 0:ncols]
            else:
                dst3 = S["wk"][:, buf, 0:nfull, 0:ncols]
            P.dma(q, lambda e, s=src3, d=dst3: e.dma_start(out=d, in_=s),
                  reads=rk, writes=(["wk0", "wk1"] if flat else [f"wk{buf}"]),
                  tag=tag or "wload")
            r0 = rows0 + 128 * nfull
            start_k = nfull
        else:
            start_k = 0
        for k in range(start_k, len(nrows)):
            rc = nrows[k]
            if flat:
                dst = fv[0:rc, 128 * k:128 * k + ncols]
            else:
                dst = S["wk"][0:rc, buf, k, 0:ncols]
            P.dma(q, lambda e, s=w_hbm[r0:r0 + rc, col0:col0 + ncols],
                  d=dst: e.dma_start(out=d, in_=s),
                  reads=rk, writes=(["wk0", "wk1"] if flat else [f"wk{buf}"]),
                  tag=tag or "wload")
            r0 += rc

    def mm(self, psum, which, rhs_of_k, lhsT_of_k, nk, T, rhs_keys, lhsT_keys,
           tag="mm", m=128, n0=0, open_group=True, close_group=True):
        P = self.P
        pkeys = self.pk(which, n0, n0 + T)
        for k in range(nk):
            first = (k == 0) and open_group
            last = (k == nk - 1) and close_group
            for n in range(0, T, 512):
                w = min(512, T - n)
                P.emit("pe", lambda e, a=psum[0:m, n0 + n:n0 + n + w],
                       l=lhsT_of_k(k), r=rhs_of_k(k, n, w), f=first, s=last:
                       e.matmul(a, l, r, start=f, stop=s),
                       reads=list(rhs_keys(k)) + list(lhsT_keys(k)),
                       writes=[pkeys[(n0 + n) // 512 - n0 // 512]], tag=tag)
        return pkeys

    def ln(self, z_of_c, zkey_of_c, T, out_f32_of_c=None, out_bf16_of_c=None,
           outkeys_f32=None, outkeys_bf16=None, g_col=None, b_col=None,
           colkeys=(), tag="ln"):
        P, S = self.P, self.S
        pkA = self.pk("A", 0, T)
        pkB = self.pk("B", 0, T)
        for c in range(8):
            zb = S["sb0" if c % 2 == 0 else "sb1"]
            zq = S["sb2"][:, 0] if c % 2 == 0 else S["sb3"][:, 0]
            zbk = "sb0" if c % 2 == 0 else "sb1"
            zqk = "sb2" if c % 2 == 0 else "sb3"
            P.emit("pool", lambda e, d=zb[:, 0:T], s=z_of_c(c): e.tensor_copy(d, s),
                   reads=[zkey_of_c(c)], writes=[zbk], tag=tag + "_zb")
            P.emit("act", lambda e, d=zq[:, 0:T], s=z_of_c(c):
                   e.activation(d, s, AF.Square),
                   reads=[zkey_of_c(c)], writes=[zqk], tag=tag + "_zq")
            for n in range(0, T, 512):
                w = min(512, T - n)
                P.emit("pe", lambda e, a=self.pA[:, n:n + w], l=S["c_ones_ln"][:, :],
                       r=zb[:, n:n + w], f=(c == 0), s=(c == 7):
                       e.matmul(a, l, r, start=f, stop=s),
                       reads=[zbk, "c_ones_ln"], writes=[pkA[n // 512]], tag=tag + "_sm")
                P.emit("pe", lambda e, a=self.pB[:, n:n + w], l=S["c_ones_ln"][:, :],
                       r=zq[:, n:n + w], f=(c == 0), s=(c == 7):
                       e.matmul(a, l, r, start=f, stop=s),
                       reads=[zqk, "c_ones_ln"], writes=[pkB[n // 512]], tag=tag + "_sq")
        P.emit("act", lambda e: e.activation(S["sc1"][:, 0:T], self.pA[:, 0:T],
                                             AF.Square),
               reads=pkA, writes=["sc1"], tag=tag + "_m2")
        P.emit("dve", lambda e: e.tensor_sub(S["sc1"][:, 0:T], self.pB[:, 0:T],
                                             S["sc1"][:, 0:T]),
               reads=pkB + ["sc1"], writes=["sc1"], tag=tag + "_var")
        P.emit("act", lambda e: e.activation(S["sc1"][:, 0:T], S["sc1"][:, 0:T],
                                             AF.Ln, bias=S["cols"][:, 144:145]),
               reads=["sc1", "colsE"], writes=["sc1"], tag=tag + "_ln")
        P.emit("act", lambda e: e.activation(S["sc1"][:, 0:T], S["sc1"][:, 0:T],
                                             AF.Exp, scale=-0.5),
               reads=["sc1"], writes=["sc1"], tag=tag + "_exp")
        for c in range(8):
            P.emit("dve", lambda e, s=z_of_c(c): e.tensor_sub(S["sc0"][:, 0:T], s,
                                                              self.pA[:, 0:T]),
                   reads=[zkey_of_c(c)] + pkA, writes=["sc0"], tag=tag + "_a1")
            P.emit("dve", lambda e: e.tensor_mul(S["sc0"][:, 0:T], S["sc0"][:, 0:T],
                                                 S["sc1"][:, 0:T]),
                   reads=["sc0", "sc1"], writes=["sc0"], tag=tag + "_a2")
            if out_f32_of_c is not None:
                P.emit("pool", lambda e, d=out_f32_of_c(c):
                       e.tensor_copy(d, S["sc0"][:, 0:T]),
                       reads=["sc0"], writes=[outkeys_f32(c)], tag=tag + "_of")
            if out_bf16_of_c is not None:
                if g_col is not None:
                    P.emit("dve", lambda e, d=out_bf16_of_c(c), g=g_col(c), b=b_col(c):
                           e.tensor_scalar(d, S["sc0"][:, 0:T], g, b, ALU.mult, ALU.add),
                           reads=["sc0"] + list(colkeys), writes=[outkeys_bf16(c)],
                           tag=tag + "_ob")
                else:
                    P.emit("pool", lambda e, d=out_bf16_of_c(c):
                           e.tensor_copy(d, S["sc0"][:, 0:T]),
                           reads=["sc0"], writes=[outkeys_bf16(c)], tag=tag + "_ob")

    def proj(self, w_hbm, wcols, rhs_chunk, rhs_key, evac, nk=8, krows=None,
             piece=128, tag="proj", src_key=None):
        T = self.Tcur
        P, S = self.P, self.S
        if krows is None:
            krows = [128] * nk
        c0, c1 = wcols
        ncols = c1 - c0
        npieces = (ncols + piece - 1) // piece
        for pi in range(npieces):
            pc0 = c0 + pi * piece
            pw = min(piece, c1 - pc0)
            buf = pi % 2
            self.wload(buf, w_hbm, 0, krows, pc0, pw, tag=tag + "_w",
                       src_key=src_key)
            for oc in range(0, pw, 128):
                m = min(128, pw - oc)
                og = (pi * piece + oc) // 128
                which = "A" if og % 2 == 0 else "B"
                ps = self.pA if which == "A" else self.pB
                pkeys = self.mm(
                    ps, which,
                    rhs_of_k=rhs_chunk,
                    lhsT_of_k=lambda k, b=buf, o=oc, mm_=m:
                        S["wk"][0:krows[k], b, k, o:o + mm_],
                    nk=nk, T=T,
                    rhs_keys=rhs_key,
                    lhsT_keys=lambda k, b=buf: [f"wk{b}"],
                    tag=tag, m=m)
                evac(og, ps, which, pkeys, m)

    def l0_attn(self):
        T = self.Tcur
        P, S, D = self.P, self.S, self.D
        nt = T // 128
        for c in range(8):
            P.emit("pool", lambda e, c_=c: e.tensor_copy(
                S["xb"][:, c_, 0:T], S["xs"][:, c_, 1:T + 1]),
                reads=[f"xs{c}"], writes=[f"xb{c}"], tag="l0a_cast")
        rhs = lambda k, n, w: S["xb"][:, k, n:n + w]
        rhsk = lambda k: [f"xb{k}"]

        for j in range(8):
            pj = j % 2
            role_dst = {"q": (S["sb0"][:, 0:T], "sb0"),
                        "k": (S["sb1"][:, 0:T], "sb1"),
                        "v": (S["sb2"][:, pj, 0:T], f"sb2_{pj}")}
            for role, colbase, rname, biascol, rscale in (
                    ("q", 128 * j, "q", 146 + j, 0.125),
                    ("k", 1024 + 128 * j, "k", 8 + j, 1.0),
                    ("v", 2048 + 128 * j, "v", 16 + j, 1.0)):
                buf = 0 if role != "k" else 1
                self.wload(buf, self.WB["w0qkvT"], 0, [128] * 8, colbase, 128,
                           tag="l0a_w" + role, src_key="b_w0qkvT")
                which = "A" if role != "k" else "B"
                ps = self.pA if which == "A" else self.pB
                pkeys = self.mm(ps, which, rhs, lambda k, b=buf: S["wk"][:, b, k, 0:128],
                                nk=8, T=T, rhs_keys=rhsk,
                                lhsT_keys=lambda k, b=buf: [f"wk{b}"], tag="l0a_qkv")
                P.emit("act", lambda e, d=role_dst[rname][0], p=ps[:, 0:T],
                       bc=S["cols"][:, biascol:biascol + 1], rs=rscale:
                       e.activation(d, p, AF.Identity, bias=bc, scale=rs),
                       reads=pkeys + ["colsA"], writes=[role_dst[rname][1]],
                       tag="l0a_ev" + role)
            for t0 in range(0, nt, 4):
                ntile = min(4, nt - t0)
                for i in range(ntile):
                    P.emit("pe", lambda e, i_=i, t_=t0 + i, v2=S["sb2"][:, pj],
                           o=self.pA[:, 0:512].bitcast(BF16)[:, 128 * i:128 * (i + 1)]:
                           e.transpose(o, v2[:, 128 * t_:128 * (t_ + 1)],
                                       S["c_idn"][:, :]),
                           reads=[f"sb2_{pj}", "c_idn"], writes=["pA0"], tag="l0a_vtr")
                P.emit("dve", lambda e, t0_=t0, n_=ntile, vt=S["vtm"][:, pj]:
                       e.tensor_copy(vt[:, t0_:t0_ + n_, :].rearrange("p a b -> p (a b)"),
                                     self.pA[:, 0:512].bitcast(BF16)[:, 0:128 * n_]),
                       reads=["pA0"], writes=[f"vtm{pj}"], tag="l0a_vev")
            units = [(h, qt) for h in (0, 1) for qt in range(nt)]
            nu = len(units)
            rr = [0]

            def stageA(u):
                h, qt = units[u]
                hb = 64 * h
                K = 128 * (qt + 1)
                sck = "sc0" if u % 2 == 0 else "sc1"
                sc = S[sck]
                for n in range(0, K, 512):
                    w = min(512, K - n)
                    b = rr[0] % 2
                    rr[0] += 1
                    pak = f"pB{b}"
                    P.emit("pe", lambda e, n_=n, w_=w, qt_=qt, hb_=hb, b_=b:
                           e.matmul(self.pB[:, 512 * b_:512 * b_ + w_],
                                    S["sb0"][hb_:hb_ + 64, 128 * qt_:128 * (qt_ + 1)],
                                    S["sb1"][hb_:hb_ + 64, n_:n_ + w_],
                                    start=True, stop=True),
                           reads=["sb0", "sb1"], writes=[pak], tag="l0a_sc")
                    last = (n + w == K)
                    if last and w > 128:
                        P.emit("act", lambda e, n_=n, w_=w, b_=b, d=sc:
                               e.activation(d[:, n_:n_ + w_ - 128],
                                            self.pB[:, 512 * b_:512 * b_ + w_ - 128],
                                            AF.Copy),
                               reads=[pak], writes=[sck], tag="l0a_sev")
                    elif not last:
                        P.emit("act", lambda e, n_=n, w_=w, b_=b, d=sc:
                               e.activation(d[:, n_:n_ + w_],
                                            self.pB[:, 512 * b_:512 * b_ + w_],
                                            AF.Copy),
                               reads=[pak], writes=[sck], tag="l0a_sev")
                    if last:
                        P.emit("dve", lambda e, K_=K, w_=w, b_=b, d=sc:
                               e.tensor_add(d[:, K_ - 128:K_],
                                            self.pB[:, 512 * b_ + w_ - 128:512 * b_ + w_],
                                            S["c_causal"][:, :]),
                               reads=[pak, "c_causal"], writes=[sck], tag="l0a_sevd")

            def stageBC(u):
                h, qt = units[u]
                K = 128 * (qt + 1)
                ms = u % 4
                sck = "sc0" if u % 2 == 0 else "sc1"
                sc = S[sck]
                prk = f"sb3_{u % 2}"
                pr = S["sb3"][:, u % 2]
                P.emit("dve", lambda e, K_=K, s=sc, ms_=ms:
                       e.tensor_reduce(S["mx"][:, ms_:ms_ + 1], s[:, 0:K_], AX.X,
                                       ALU.max, negate=True),
                       reads=[sck], writes=[f"mx{ms}"], tag="l0a_max")
                P.emit("act", lambda e, K_=K, s=sc, ms_=ms:
                       e.activation(s[:, 0:K_], s[:, 0:K_], AF.Exp,
                                    bias=S["mx"][:, ms_:ms_ + 1],
                                    accum_out=S["mx"][:, 4 + ms_:5 + ms_]),
                       reads=[sck, f"mx{ms}"],
                       writes=[sck, f"mx{4 + ms}"], tag="l0a_exp")
                P.emit("dve", lambda e, ms_=ms:
                       e.reciprocal(S["mx"][:, 4 + ms_:5 + ms_],
                                    S["mx"][:, 4 + ms_:5 + ms_]),
                       reads=[f"mx{4 + ms}"], writes=[f"mx{4 + ms}"], tag="l0a_rc")
                P.emit("dve", lambda e, K_=K, s=sc, p=pr, ms_=ms:
                       e.tensor_scalar_mul(p[:, 0:K_], s[:, 0:K_],
                                           S["mx"][:, 4 + ms_:5 + ms_]),
                       reads=[sck, f"mx{4 + ms}"], writes=[prk], tag="l0a_nrm")

            def stageDE(u):
                h, qt = units[u]
                hb = 64 * h
                nk_ = qt + 1
                prk = "sb3" if u % 2 == 0 else "sb2"
                pr = S["sb3"][:, u % 2]
                pvk = "pB3"
                pvview = self.pB[0:64, 1536:1664]
                for t0 in range(0, nk_, 4):
                    ntile = min(4, nk_ - t0)
                    g = t0 // 4
                    pbk = "pB2"
                    pslot = 4 * (g % 2)
                    pbview = self.pB[:, 1024:1536].bitcast(BF16)
                    for i in range(ntile):
                        P.emit("pe", lambda e, i_=i, t_=t0 + i, v=pbview, p=pr:
                               e.transpose(v[:, 128 * i_:128 * (i_ + 1)],
                                           p[:, 128 * t_:128 * (t_ + 1)],
                                           S["c_idn"][:, :]),
                               reads=[prk, "c_idn"], writes=[pbk], tag="l0a_ptr")
                    P.emit("dve", lambda e, ps_=pslot, n_=ntile, v=pbview:
                           e.tensor_copy(S["prT"][:, ps_:ps_ + n_, :]
                                         .rearrange("p a b -> p (a b)"),
                                         v[:, 0:128 * n_]),
                           reads=[pbk], writes=[f"prT{g % 2}"], tag="l0a_pev")
                    for i in range(ntile):
                        kt = t0 + i
                        P.emit("pe", lambda e, i_=i, kt_=kt, ps_=pslot, hb_=hb,
                               v=pvview, vt=S["vtm"][:, pj],
                               f=(kt == 0), s=(kt == nk_ - 1):
                               e.matmul(v, vt[:, kt_, hb_:hb_ + 64],
                                        S["prT"][:, ps_ + i_, :], start=f, stop=s),
                               reads=[f"vtm{pj}", f"prT{g % 2}"], writes=[pvk],
                               tag="l0a_pv")
                jj = units[u]
                P.emit("act", lambda e, j_=j, hb_=hb, qt_=qt, v=pvview:
                       e.activation(S["ab"][hb_:hb_ + 64, j_, 128 * qt_:128 * (qt_ + 1)],
                                    v, AF.Copy),
                       reads=[pvk], writes=[f"ab{j}"], tag="l0a_oev")

            for u in range(nu + 1):
                if u < nu:
                    stageA(u)
                    stageBC(u)
                if u >= 1:
                    stageDE(u - 1)

        rhs2 = lambda k, n, w: S["ab"][:, k, n:n + w]
        rhsk2 = lambda k: [f"ab{k}"]

        def evac(og, ps, which, pkeys, m):
            P.emit("dve", lambda e, og_=og, p=ps:
                   e.scalar_tensor_tensor(S["xs"][:, og_, 1:T + 1], p[:, 0:T],
                                          S["cols"][:, 24 + og_:25 + og_],
                                          S["xs"][:, og_, 1:T + 1],
                                          ALU.add, ALU.add),
                   reads=pkeys + [f"xs{og}", "colsA"], writes=[f"xs{og}"],
                   tag="l0a_wo_ev")

        self.proj(self.WB["w0oT"], (0, 1024), rhs2, rhsk2, evac, tag="l0a_wo",
                  src_key="b_w0oT")
        self.ln(lambda c: S["xs"][:, c, 1:T + 1], lambda c: f"xs{c}", T,
                out_f32_of_c=lambda c: S["xs"][:, c, 1:T + 1],
                outkeys_f32=lambda c: f"xs{c}", tag="l0a_ln")

    def l0_ffn(self):
        T = self.Tcur
        P, S, D = self.P, self.S, self.D

        def build_mix(mu_off, om_off, tag):
            for c in range(8):
                P.emit("dve", lambda e, c_=c, o_=om_off: e.tensor_scalar_mul(
                    S["sc0"][:, 0:T], S["xs"][:, c_, 0:T],
                    S["cols"][:, o_ + c_:o_ + c_ + 1]),
                    reads=[f"xs{c}", "colsA"], writes=["sc0"], tag=tag + "_p1")
                P.emit("dve", lambda e, c_=c, m_=mu_off: e.scalar_tensor_tensor(
                    S["xb"][:, c_, 0:T], S["xs"][:, c_, 1:T + 1],
                    S["cols"][:, m_ + c_:m_ + c_ + 1],
                    S["sc0"][:, 0:T], ALU.mult, ALU.add),
                    reads=[f"xs{c}", "colsA", "sc0"], writes=[f"xb{c}"],
                    tag=tag + "_p2")

        rhs = lambda k, n, w: S["xb"][:, k, n:n + w]
        rhsk = lambda k: [f"xb{k}"]

        build_mix(40, 56, "l0f_xk")

        def evac_k(og, ps, which, pkeys, m):
            P.emit("dve", lambda e, p=ps: e.tensor_scalar_max(
                S["sc1"][:, 0:T], p[:, 0:T], 0.0),
                reads=pkeys, writes=["sc1"], tag="l0f_kre")
            P.emit("act", lambda e, og_=og: e.activation(
                S["ab"][:, og_, 0:T], S["sc1"][:, 0:T], AF.Square),
                reads=["sc1"], writes=[f"ab{og}"], tag="l0f_ksq")

        self.proj(self.WB["w0kT"], (0, 1024), rhs, rhsk, evac_k, tag="l0f_k",
                  src_key="b_w0kT")

        build_mix(32, 48, "l0f_xr")
        rhs2 = lambda k, n, w: S["ab"][:, k, n:n + w]
        rhsk2 = lambda k: [f"ab{k}"]
        for og in range(8):
            buf = og % 2
            self.wload(buf, self.WB["w0rT"], 0, [128] * 8, 128 * og, 128,
                       tag="l0f_wr", src_key="b_w0rT")
            pk1 = self.mm(self.pA, "A", rhs,
                          lambda k, b=buf: S["wk"][:, b, k, 0:128], nk=8, T=T,
                          rhs_keys=rhsk, lhsT_keys=lambda k, b=buf: [f"wk{b}"],
                          tag="l0f_r")
            P.emit("act", lambda e: e.activation(S["sc0"][:, 0:T],
                                                 self.pA[:, 0:T], AF.Sigmoid),
                   reads=pk1, writes=["sc0"], tag="l0f_sig")
            self.wload(buf, self.WB["w0vT"], 0, [128] * 8, 128 * og, 128,
                       tag="l0f_wv", src_key="b_w0vT")
            pk2 = self.mm(self.pB, "B", rhs2,
                          lambda k, b=buf: S["wk"][:, b, k, 0:128], nk=8, T=T,
                          rhs_keys=rhsk2, lhsT_keys=lambda k, b=buf: [f"wk{b}"],
                          tag="l0f_v")
            P.emit("dve", lambda e: e.tensor_mul(S["sc1"][:, 0:T],
                                                 self.pB[:, 0:T], S["sc0"][:, 0:T]),
                   reads=pk2 + ["sc0"], writes=["sc1"], tag="l0f_rv")
            P.emit("dve", lambda e, og_=og: e.tensor_add(
                S["xs"][:, og_, 1:T + 1], S["xs"][:, og_, 1:T + 1], S["sc1"][:, 0:T]),
                reads=["sc1", f"xs{og}"], writes=[f"xs{og}"], tag="l0f_res")
        self.ln(lambda c: S["xs"][:, c, 1:T + 1], lambda c: f"xs{c}", T,
                out_f32_of_c=lambda c: S["xs"][:, c, 1:T + 1],
                outkeys_f32=lambda c: f"xs{c}", tag="l0f_ln")

    def l1_attn(self):
        T = self.Tcur
        P, S, D = self.P, self.S, self.D
        ns = T // 128
        g = int(math.ceil(math.sqrt(T)))
        nfull = T // g
        tail = T - nfull * g
        assert nfull % 2 == 0, "tail block must be even-indexed"
        pstep_xs = 8 * (T + 1)
        pstep_xb = 8 * T
        neven = nfull // 2
        for c in range(8):
            base_s = c * (T + 1) + 1
            base_d = c * T
            P.emit("pool", lambda e, bs=base_s, bd=base_d, g_=g, ne=neven,
                   pb=pstep_xb, px=pstep_xs:
                   e.tensor_copy(bass.AP(S["xb"], bd, [[pb, 128], [2 * g_, ne], [1, g_]]),
                                 bass.AP(S["xs"], bs, [[px, 128], [2 * g_, ne], [1, g_]])),
                   reads=[f"xs{c}"], writes=[f"xb{c}"], tag="l1a_pme")
            P.emit("pool", lambda e, bs=base_s, bd=base_d, g_=g, ne=neven,
                   pb=pstep_xb, px=pstep_xs:
                   e.tensor_copy(bass.AP(S["xb"], bd + g_, [[pb, 128], [2 * g_, ne], [1, g_]]),
                                 bass.AP(S["xs"], bs + 2 * g_ - 1, [[px, 128], [2 * g_, ne], [-1, g_]])),
                   reads=[f"xs{c}"], writes=[f"xb{c}"], tag="l1a_pmo")
            if tail:
                P.emit("pool", lambda e, c_=c, g_=g, nf=nfull, tl=tail:
                       e.tensor_copy(S["xb"][:, c_, nf * g_:nf * g_ + tl],
                                     S["xs"][:, c_, 1 + nf * g_:1 + nf * g_ + tl]),
                       reads=[f"xs{c}"], writes=[f"xb{c}"], tag="l1a_pmt")
        rhs = lambda k, n, w: S["xb"][:, k, n:n + w]
        rhsk = lambda k: [f"xb{k}"]

        for j in range(8):
            pj = j % 2
            role_dst = {"sb0": (S["sb0"][:, 0:T], "sb0"),
                        "sb1": (S["sb1"][:, 0:T], "sb1"),
                        "sb2": (S["sb2"][:, pj, 0:T], f"sb2_{pj}")}
            for role, colbase, dstname in (("q", 128 * j, "sb0"),
                                           ("k", 1024 + 128 * j, "sb1"),
                                           ("v", 2048 + 128 * j, "sb2")):
                buf = 0 if role != "k" else 1
                self.wload(buf, self.WB["w1qkvT"], 0, [128] * 8, colbase, 128,
                           tag="l1a_w" + role, src_key="b_w1qkvT")
                which = "A" if role != "k" else "B"
                ps = self.pA if which == "A" else self.pB
                pkeys = self.mm(ps, which, rhs, lambda k, b=buf: S["wk"][:, b, k, 0:128],
                                nk=8, T=T, rhs_keys=rhsk,
                                lhsT_keys=lambda k, b=buf: [f"wk{b}"], tag="l1a_qkv")
                P.emit("act", lambda e, d=role_dst[dstname][0], p=ps[:, 0:T],
                       rs=(0.125 if role == "q" else 1.0):
                       e.activation(d, p, AF.Identity, scale=rs),
                       reads=pkeys, writes=[role_dst[dstname][1]],
                       tag="l1a_ev" + role)
            for s4 in range(0, ns, 4):
                nseg = min(4, ns - s4)
                for i in range(nseg):
                    P.emit("pe", lambda e, i_=i, s_=s4 + i, v2=S["sb2"][:, pj]:
                           e.transpose(self.pA[:, 0:512].bitcast(BF16)[0:64, 128 * i_:128 * (i_ + 1)],
                                       v2[:, 128 * s_:128 * (s_ + 1):2],
                                       S["c_idn"][:, :]),
                           reads=[f"sb2_{pj}", "c_idn"], writes=["pA0"], tag="l1a_vtr")
                P.emit("dve", lambda e, s4_=s4, n_=nseg, vt=S["vtm"][:, pj]:
                       e.tensor_copy(vt[0:64, s4_:s4_ + n_, :].rearrange("p a b -> p (a b)"),
                                     self.pA[:, 0:512].bitcast(BF16)[0:64, 0:128 * n_]),
                       reads=["pA0"], writes=[f"vtm{pj}"], tag="l1a_vev")
            units = [(h, s) for h in (0, 1) for s in range(ns)]
            nu = len(units)

            def stageA(u):
                h, s = units[u]
                hb = 64 * h
                sbank = 512 * (u % 4)
                pak = f"pB{u % 4}"
                P.emit("pe", lambda e, s_=s, hb_=hb, n_=sbank:
                       e.matmul(self.pB[:, n_:n_ + 64],
                                S["sb0"][hb_:hb_ + 64, 128 * s_:128 * (s_ + 1)],
                                S["sb1"][hb_:hb_ + 64, 128 * s_:128 * (s_ + 1):2],
                                start=True, stop=True),
                       reads=["sb0", "sb1"], writes=[pak], tag="l1a_sc")

            def stageBC(u):
                ms = u % 4
                u3 = u % 3
                sbank = 512 * ms
                pak = f"pB{ms}"
                prk = f"pr2_{u3}"
                P.emit("dve", lambda e, n_=sbank, ms_=ms:
                       e.tensor_reduce(S["mx"][:, ms_:ms_ + 1],
                                       self.pB[:, n_:n_ + 64], AX.X,
                                       ALU.max, negate=True),
                       reads=[pak], writes=[f"mx{ms}"], tag="l1a_max")
                P.emit("act", lambda e, n_=sbank, ms_=ms, b_=u3:
                       e.activation(S["pr2"][:, b_, 0:64], self.pB[:, n_:n_ + 64],
                                    AF.Exp, bias=S["mx"][:, ms_:ms_ + 1],
                                    accum_out=S["mx"][:, 4 + ms_:5 + ms_]),
                       reads=[pak, f"mx{ms}"],
                       writes=[prk, f"mx{4 + ms}"], tag="l1a_exp")
                P.emit("dve", lambda e, ms_=ms:
                       e.reciprocal(S["mx"][:, 4 + ms_:5 + ms_],
                                    S["mx"][:, 4 + ms_:5 + ms_]),
                       reads=[f"mx{4 + ms}"], writes=[f"mx{4 + ms}"], tag="l1a_rc")
                P.emit("dve", lambda e, ms_=ms, b_=u3:
                       e.tensor_scalar_mul(S["pr2"][:, b_, 0:64],
                                           S["pr2"][:, b_, 0:64],
                                           S["mx"][:, 4 + ms_:5 + ms_]),
                       reads=[prk, f"mx{4 + ms}"], writes=[prk], tag="l1a_nrm")

            def stageD(u):
                ms = u % 4
                u3 = u % 3
                prk = f"pr2_{u3}"
                pak = f"pB{ms}"
                pbview = self.pB[:, 512 * ms:512 * (ms + 1)].bitcast(BF16)
                P.emit("pe", lambda e, b_=u3, v=pbview:
                       e.transpose(v[0:64, 0:128],
                                   S["pr2"][:, b_, 0:64], S["c_idn"][:, :]),
                       reads=[prk, "c_idn"], writes=[pak], tag="l1a_ptr")
                P.emit("dve", lambda e, b_=u3, v=pbview:
                       e.tensor_copy(S["prT"][0:64, b_, :], v[0:64, 0:128]),
                       reads=[pak], writes=[f"prT{u3}"], tag="l1a_pev")

            def stageE(u):
                h, s = units[u]
                hb = 64 * h
                ms = u % 4
                u3 = u % 3
                pak = f"pB{ms}"
                pvview = self.pB[0:64, 512 * ms + 384:512 * ms + 512]
                P.emit("pe", lambda e, s_=s, hb_=hb, b_=u3, v=pvview,
                       vt=S["vtm"][:, pj]:
                       e.matmul(v, vt[0:64, s_, hb_:hb_ + 64],
                                S["prT"][0:64, b_, :], start=True, stop=True),
                       reads=[f"vtm{pj}", f"prT{u3}"], writes=[pak], tag="l1a_pv")
                P.emit("act", lambda e, j_=j, hb_=hb, s_=s, v=pvview:
                       e.activation(S["ab"][hb_:hb_ + 64, j_, 128 * s_:128 * (s_ + 1)],
                                    v, AF.Copy),
                       reads=[pak], writes=[f"ab{j}"], tag="l1a_oev")

            for u in range(nu + 4):
                if u >= 4:
                    stageE(u - 4)
                if u < nu:
                    stageA(u)
                    stageBC(u)
                if u >= 2 and u - 2 < nu:
                    stageD(u - 2)
        rhs2 = lambda k, n, w: S["ab"][:, k, n:n + w]
        rhsk2 = lambda k: [f"ab{k}"]

        def evac(og, ps, which, pkeys, m):
            P.emit("dve", lambda e, og_=og, p=ps:
                   e.tensor_add(S["xs"][:, og_, 1:T + 1], S["xs"][:, og_, 1:T + 1],
                                p[:, 0:T]),
                   reads=pkeys + [f"xs{og}"], writes=[f"xs{og}"], tag="l1a_wo_ev")

        self.proj(self.WB["w1oT"], (0, 1024), rhs2, rhsk2, evac, tag="l1a_wo",
                  src_key="b_w1oT")
        self.ln(lambda c: S["xs"][:, c, 1:T + 1], lambda c: f"xs{c}", T,
                out_f32_of_c=lambda c: S["xs"][:, c, 1:T + 1],
                outkeys_f32=lambda c: f"xs{c}", tag="l1a_ln")

    def l1_ffn(self):
        T = self.Tcur
        P, S, D = self.P, self.S, self.D
        nt = T // 128
        P.dma("sp", lambda e: e.dma_start(out=D["xpark"][:, :, 0:T],
                                          in_=S["xs"][:, :, 1:T + 1]),
              reads=[f"xs{c}" for c in range(8)], writes=["xpark"], tag="park")
        for l in range(L1F_DEPTH):
            self._l1f_layer(l)
        if L1F_STOP is not None:
            return
        for c in range(8):
            P.dma("sp", lambda e, c_=c: e.dma_start(out=S["sc0"][:, 0:T],
                                                    in_=D["xpark"][:, c_, 0:T]),
                  reads=["xpark"], writes=["sc0"], tag="unpark")
            P.emit("dve", lambda e, c_=c: e.tensor_add(
                S["xs"][:, c_, 1:T + 1], S["xs"][:, c_, 1:T + 1], S["sc0"][:, 0:T]),
                reads=["sc0", f"xs{c}"], writes=[f"xs{c}"], tag="unpark_add")
        self.ln(lambda c: S["xs"][:, c, 1:T + 1], lambda c: f"xs{c}", T,
                out_f32_of_c=lambda c: S["xs"][:, c, 1:T + 1],
                outkeys_f32=lambda c: f"xs{c}", tag="l1f_lnF")

    def _l1f_layer(self, l):
        T = self.Tcur
        P, S, D = self.P, self.S, self.D
        nt = T // 128
        P.dma("sp", lambda e, l_=l: e.dma_start(out=S["cols"][:, 76:84], in_=D["flng"][l_]),
              writes=["colsL"], tag="ld_lng")
        P.dma("sp", lambda e, l_=l: e.dma_start(out=S["cols"][:, 84:92], in_=D["flnb"][l_]),
              writes=["colsL"], tag="ld_lnb")
        P.dma("sp", lambda e, l_=l: e.dma_start(out=S["cols"][:, 92:100], in_=D["fb2"][l_]),
              writes=["colsL"], tag="ld_b2")
        P.dma("sp", lambda e, l_=l: e.dma_start(out=S["cols"][:, 100:122], in_=D["fb1a"][l_]),
              writes=["colsB"], tag="ld_b1a")
        P.dma("sp", lambda e, l_=l: e.dma_start(out=S["cols"][:, 122:144], in_=D["fb1g"][l_]),
              writes=["colsB"], tag="ld_b1g")
        if "attn" not in L1F_PARTS:
            self._l1f_ffn_part(l)
            return
        for c in range(8):
            src = (lambda c_: S["xs"][:, c_, 1:T + 1]) if c < 4 else \
                  (lambda c_: S["xs"][:, c_, 0:T])
            P.emit("pool", lambda e, c_=c, s=src(c): e.tensor_copy(
                S["xb"][:, c_, 0:T], s),
                reads=[f"xs{c}"], writes=[f"xb{c}"], tag="l1f_sh")
        rhs = lambda k, n, w: S["xb"][:, k, n:n + w]
        rhsk = lambda k: [f"xb{k}"]
        wq = self.WB["fWqkv"][l]
        wqk = f"b_fWqkv{l}"

        def prep_gen(j):
            pj = j % 2
            for role in ("q", "k", "v"):
                colbase = {"q": 128 * j, "k": 1024 + 128 * j,
                           "v": 2048 + 128 * j}[role]
                buf = 0 if role in ("q", "v") else 1
                self.wload(buf, wq, 0, [128] * 8, colbase, 128,
                           tag="l1f_w" + role, src_key=wqk)
                yield
                pkeys = self.mm(self.pA, "A", rhs,
                                lambda k, b=buf: S["wk"][:, b, k, 0:128],
                                nk=8, T=T, rhs_keys=rhsk,
                                lhsT_keys=lambda k, b=buf: [f"wk{b}"], tag="l1f_qkv")
                yield
                if role == "v":
                    P.emit("act", lambda e: e.activation(S["sb0"][:, 0:T],
                                                         self.pA[:, 0:T], AF.Copy),
                           reads=pkeys, writes=["sb0"], tag="l1f_evv")
                    yield
                    for s4 in range(0, nt, 4):
                        nseg = min(4, nt - s4)
                        for i in range(nseg):
                            P.emit("pe", lambda e, i_=i, t_=s4 + i:
                                   e.transpose(self.pA[:, 0:512].bitcast(BF16)
                                               [:, 128 * i_:128 * (i_ + 1)],
                                               S["sb0"][:, 128 * t_:128 * (t_ + 1)],
                                               S["c_idn"][:, :]),
                                   reads=["sb0", "c_idn"], writes=["pA0"],
                                   tag="l1f_vtr")
                        P.emit("dve", lambda e, s4_=s4, n_=nseg, vt=S["vtm"][:, pj]:
                               e.tensor_copy(vt[:, s4_:s4_ + n_, :]
                                             .rearrange("p a b -> p (a b)"),
                                             self.pA[:, 0:512].bitcast(BF16)[:, 0:128 * n_]),
                               reads=["pA0"], writes=[f"vtm{pj}"], tag="l1f_vev")
                        yield
                    continue
                pkA = self.pk("A", 0, T)
                scol = S["cols"][:, 64 + l:65 + l] if role == "q" else \
                    S["cols"][:, 70 + l:71 + l]
                sstab = S["c_qss"] if role == "q" else S["c_kss"]
                cstab = S["c_qcs"] if role == "q" else S["c_kcs"]
                sskey = "c_qss" if role == "q" else "c_kss"
                cskey = "c_qcs" if role == "q" else "c_kcs"
                P.emit("act", lambda e: e.activation(S["sb0"][:, 0:T],
                                                     self.pA[:, 0:T], AF.Square),
                       reads=pkeys, writes=["sb0"], tag="l1f_sq")
                yield
                P.emit("act", lambda e, sc=scol: e.activation(
                    S["sc0"][:, 0:T], self.pA[:, 0:T], AF.Identity, scale=sc),
                    reads=pkeys + ["colsA"], writes=["sc0"], tag="l1f_evqk")
                yield
                for n in range(0, T, 512):
                    w = min(512, T - n)
                    P.emit("pe", lambda e, n_=n, w_=w:
                           e.matmul(self.pA[:, n_:n_ + w_], S["c_rot32"][:, :],
                                    S["sc0"][:, n_:n_ + w_], start=True, stop=True),
                           reads=["sc0", "c_rot32"], writes=[pkA[n // 512]],
                           tag="l1f_rot")
                yield
                P.emit("dve", lambda e, st=sstab: e.tensor_mul(
                    S["sc1"][:, 0:T], self.pA[:, 0:T], st[:, 0:T]),
                    reads=pkA + [sskey], writes=["sc1"], tag="l1f_t5")
                yield
                P.emit("pool", lambda e, ct=cstab: e.tensor_mul(
                    S["sc0"][:, 0:T], S["sc0"][:, 0:T], ct[:, 0:T]),
                    reads=["sc0", cskey], writes=["sc0"], tag="l1f_t6")
                yield
                P.emit("pool", lambda e: e.tensor_add(
                    S["sc0"][:, 0:T], S["sc0"][:, 0:T], S["sc1"][:, 0:T]),
                    reads=["sc0", "sc1"], writes=["sc0"], tag="l1f_u")
                yield
                for n in range(0, T, 512):
                    w = min(512, T - n)
                    P.emit("pe", lambda e, n_=n, w_=w:
                           e.matmul(self.pA[:, n_:n_ + w_], S["c_blockones"][:, :],
                                    S["sb0"][:, n_:n_ + w_], start=True, stop=True),
                           reads=["sb0", "c_blockones"], writes=[pkA[n // 512]],
                           tag="l1f_nsum")
                yield
                P.emit("act", lambda e: e.activation(S["sc1"][:, 0:T],
                                                     self.pA[:, 0:T], AF.Ln,
                                                     bias=S["cols"][:, 145:146]),
                       reads=pkA + ["colsE"], writes=["sc1"], tag="l1f_nln")
                P.emit("act", lambda e: e.activation(S["sc1"][:, 0:T],
                                                     S["sc1"][:, 0:T], AF.Exp,
                                                     scale=-0.5),
                       reads=["sc1"], writes=["sc1"], tag="l1f_nexp")
                yield
                dst = "sb2" if role == "q" else "sb3"
                P.emit("dve", lambda e, d=S[dst][:, pj]: e.tensor_mul(
                    d[:, 0:T], S["sc0"][:, 0:T], S["sc1"][:, 0:T]),
                    reads=["sc0", "sc1"], writes=[f"{dst}_{pj}"], tag="l1f_" + dst)
                yield

        def attention(j, gen):
            pj = j % 2
            units = [(h, qt) for h in (0, 1) for qt in range(nt)]
            nu = len(units)
            qb = S["sb2"][:, pj]
            kb = S["sb3"][:, pj]
            qbk, kbk = f"sb2_{pj}", f"sb3_{pj}"
            vt = S["vtm"][:, pj]
            vtk = f"vtm{pj}"

            def stageA(u):
                h, qt = units[u]
                hb = 64 * h
                k0 = max(0, 128 * qt - 256)
                kw = 128 * (qt + 1) - k0
                sbank = 512 * (u % 4)
                pak = f"pB{u % 4}"
                mask = {128: S["c_causal"], 256: S["c_band384"][:, 128:384],
                        384: S["c_band384"]}[kw]
                maskk = {128: "c_causal", 256: "c_band384", 384: "c_band384"}[kw]
                P.emit("pe", lambda e, qt_=qt, hb_=hb, n_=sbank, k0_=k0, kw_=kw,
                       q_=qb, k_=kb:
                       e.matmul(self.pB[:, n_:n_ + kw_],
                                q_[hb_:hb_ + 64, 128 * qt_:128 * (qt_ + 1)],
                                k_[hb_:hb_ + 64, k0_:k0_ + kw_],
                                start=True, stop=False),
                       reads=[qbk, kbk], writes=[pak], tag="l1f_sc")
                P.emit("pe", lambda e, n_=sbank, kw_=kw, m=mask:
                       e.matmul(self.pB[:, n_:n_ + kw_], S["c_idn"][:, :],
                                m[:, 0:kw_], start=False, stop=True),
                       reads=["c_idn", maskk], writes=[pak], tag="l1f_scm")

            def stageBC(u):
                h, qt = units[u]
                k0 = max(0, 128 * qt - 256)
                kw = 128 * (qt + 1) - k0
                ms = u % 4
                u3 = u % 3
                sbank = 512 * ms
                pak = f"pB{ms}"
                prk = f"pr2_{u3}"
                P.emit("dve", lambda e, n_=sbank, kw_=kw, ms_=ms:
                       e.tensor_reduce(S["mx"][:, ms_:ms_ + 1],
                                       self.pB[:, n_:n_ + kw_], AX.X,
                                       ALU.max, negate=True),
                       reads=[pak], writes=[f"mx{ms}"], tag="l1f_max")
                P.emit("act", lambda e, n_=sbank, kw_=kw, b_=u3, ms_=ms:
                       e.activation(S["pr2"][:, b_, 0:kw_],
                                    self.pB[:, n_:n_ + kw_], AF.Exp,
                                    bias=S["mx"][:, ms_:ms_ + 1],
                                    accum_out=S["mx"][:, 4 + ms_:5 + ms_]),
                       reads=[pak, f"mx{ms}"],
                       writes=[prk, f"mx{4 + ms}"], tag="l1f_exp")
                P.emit("dve", lambda e, ms_=ms:
                       e.reciprocal(S["mx"][:, 4 + ms_:5 + ms_],
                                    S["mx"][:, 4 + ms_:5 + ms_]),
                       reads=[f"mx{4 + ms}"], writes=[f"mx{4 + ms}"], tag="l1f_rc")
                P.emit("dve", lambda e, kw_=kw, ms_=ms, b_=u3:
                       e.tensor_scalar_mul(S["pr2"][:, b_, 0:kw_],
                                           S["pr2"][:, b_, 0:kw_],
                                           S["mx"][:, 4 + ms_:5 + ms_]),
                       reads=[prk, f"mx{4 + ms}"], writes=[prk], tag="l1f_nrm")

            def stageD(u):
                h, qt = units[u]
                k0 = max(0, 128 * qt - 256)
                kw = 128 * (qt + 1) - k0
                nkc = kw // 128
                u4 = u % 4
                u3 = u % 3
                prk = f"pr2_{u3}"
                pak = f"pB{u4}"
                pbview = self.pB[:, 512 * u4:512 * (u4 + 1)].bitcast(BF16)
                for i in range(nkc):
                    P.emit("pe", lambda e, i_=i, b_=u3, v=pbview:
                           e.transpose(v[:, 128 * i_:128 * (i_ + 1)],
                                       S["pr2"][:, b_, 128 * i_:128 * (i_ + 1)],
                                       S["c_idn"][:, :]),
                           reads=[prk, "c_idn"], writes=[pak], tag="l1f_ptr")
                P.emit("dve", lambda e, n_=nkc, b_=u3, v=pbview:
                       e.tensor_copy(S["prT"][:, 3 * b_:3 * b_ + n_, :]
                                     .rearrange("p a b -> p (a b)"),
                                     v[:, 0:128 * n_]),
                       reads=[pak], writes=[f"prT{u3}"], tag="l1f_pev")

            def stageE(u):
                h, qt = units[u]
                hb = 64 * h
                k0 = max(0, 128 * qt - 256)
                kw = 128 * (qt + 1) - k0
                nkc = kw // 128
                u4 = u % 4
                u3 = u % 3
                pak = f"pB{u4}"
                pvview = self.pB[0:64, 512 * u4 + 384:512 * u4 + 512]
                for i in range(nkc):
                    ktg = k0 // 128 + i
                    P.emit("pe", lambda e, i_=i, ktg_=ktg, hb_=hb, b_=u3,
                           v=pvview, vt_=vt, f=(i == 0), s=(i == nkc - 1):
                           e.matmul(v, vt_[:, ktg_, hb_:hb_ + 64],
                                    S["prT"][:, 3 * b_ + i_, :], start=f, stop=s),
                           reads=[vtk, f"prT{u3}"], writes=[pak], tag="l1f_pv")
                P.emit("act", lambda e, j_=j, hb_=hb, qt_=qt, v=pvview:
                       e.activation(S["ab"][hb_:hb_ + 64, j_, 128 * qt_:128 * (qt_ + 1)],
                                    v, AF.Copy),
                       reads=[pak], writes=[f"ab{j}"], tag="l1f_oev")

            for u in range(nu + 4):
                if u >= 4:
                    stageE(u - 4)
                if u < nu:
                    stageA(u)
                    stageBC(u)
                if u >= 2 and u - 2 < nu:
                    stageD(u - 2)
                next(gen, None)
                next(gen, None)
            for _ in gen:
                pass

        for _ in prep_gen(0):
            pass
        if L1F_STOP == "qk":
            return
        for j in range(8):
            gen = prep_gen(j + 1) if j < 7 else iter(())
            attention(j, gen)

        rhs2 = lambda k, n, w: S["ab"][:, k, n:n + w]
        rhsk2 = lambda k: [f"ab{k}"]

        def evac(og, ps, which, pkeys, m):
            P.emit("dve", lambda e, og_=og, p=ps:
                   e.tensor_add(S["xs"][:, og_, 1:T + 1], S["xs"][:, og_, 1:T + 1],
                                p[:, 0:T]),
                   reads=pkeys + [f"xs{og}"], writes=[f"xs{og}"], tag="l1f_wo_ev")

        self.proj(self.WB["fWo"][l], (0, 1024), rhs2, rhsk2, evac, tag="l1f_wo",
                  src_key=f"b_fWo{l}")

        if "ffn" not in L1F_PARTS:
            return
        self._l1f_ffn_part(l)

    def _l1f_ffn_part(self, l):
        T = self.Tcur
        P, S, D = self.P, self.S, self.D
        for th in (1, 0):
            t0 = th * (T // 2)
            Th = T // 2
            zs = lambda c: (S["xs"][:, c, 1 + t0:1 + t0 + Th] if c < 4
                            else S["xs"][:, c, t0:t0 + Th])
            zk = (lambda c: f"xs{c}a") if th == 0 else (lambda c: f"xs{c}")
            self.ln(zs, zk, Th,
                    out_bf16_of_c=lambda c: S["xb"][:, c, 0:Th],
                    outkeys_bf16=lambda c: f"xb{c}",
                    g_col=lambda c: S["cols"][:, 76 + c:77 + c],
                    b_col=lambda c: S["cols"][:, 84 + c:85 + c],
                    colkeys=["colsL"], tag="l1f_ln")
            if L1F_STOP == "xn":
                return
            rhs3 = lambda k, n, w: S["xb"][:, k, n:n + w]
            rhsk3 = lambda k: [f"xb{k}"]

            def hslot(i):
                if i < 16:
                    return S["ab"][:, i // 2, (i % 2) * Th:(i % 2) * Th + Th], f"ab{i // 2}"
                c = i - 16
                return S["xb"][:, c, T - Th:T], f"xb{c}"

            for i in range(22):
                m = 128 if i < 21 else 42
                self.wload(0, self.WB["fW1"][l], 0, [128] * 8, 128 * i, m,
                           tag="l1f_w1a", src_key=f"b_fW1{l}")
                self.wload(1, self.WB["fW1"][l], 0, [128] * 8, 2730 + 128 * i, m,
                           tag="l1f_w1g", src_key=f"b_fW1{l}")
                pp = self.pA if i % 2 == 0 else self.pB
                which = "A" if i % 2 == 0 else "B"
                gk = "sc0" if i % 2 == 0 else "sc1"
                pkA = self.mm(pp, which, rhs3,
                              lambda k: S["wk"][:, 0, k, 0:m], nk=8, T=Th,
                              rhs_keys=rhsk3, lhsT_keys=lambda k: ["wk0"],
                              tag="l1f_a", m=m, n0=0)
                pkB = self.mm(pp, which, rhs3,
                              lambda k: S["wk"][:, 1, k, 0:m], nk=8, T=Th,
                              rhs_keys=rhsk3, lhsT_keys=lambda k: ["wk1"],
                              tag="l1f_g", m=m, n0=Th)
                P.emit("act", lambda e, m_=m, i_=i, p=pp, g=S[gk]:
                       e.activation(g[0:m_, 0:Th], p[0:m_, Th:2 * Th],
                                    AF.Gelu, bias=S["cols"][0:m_, 122 + i_:123 + i_]),
                       reads=pkB + ["colsB"], writes=[gk], tag="l1f_gelu")
                hap, hkey = hslot(i)
                P.emit("dve", lambda e, m_=m, i_=i, d=hap, p=pp, g=S[gk]:
                       e.scalar_tensor_tensor(d[0:m_, :], p[0:m_, 0:Th],
                                              S["cols"][0:m_, 100 + i_:101 + i_],
                                              g[0:m_, 0:Th],
                                              ALU.add, ALU.mult),
                       reads=pkA + ["colsB", gk], writes=[hkey], tag="l1f_h")
            if L1F_STOP == "h":
                return
            for og in range(8):
                which = "A" if og % 2 == 0 else "B"
                ps = self.pA if which == "A" else self.pB

                def rhs4(k, n, w):
                    hap, _ = hslot(k)
                    rows = 128 if k < 21 else 42
                    return hap[0:rows, n:n + w]

                def rhsk4(k):
                    return [hslot(k)[1]]

                for half, (k0h, nch, buf) in enumerate(
                        ((0, 8, 0), (8, 8, 1), (16, 6, 0))):
                    rows = [128] * nch if k0h + nch <= 21 else [128] * (nch - 1) + [42]
                    self.wload(buf, self.WB["fW2"][l], 128 * k0h, rows,
                               128 * og, 128, tag="l1f_w2", src_key=f"b_fW2{l}")
                    pkeys = self.mm(ps, which,
                                    lambda k, n, w, k0h_=k0h: rhs4(k0h_ + k, n, w),
                                    lambda k, b_=buf, r_=rows: S["wk"][0:r_[k], b_, k, 0:128],
                                    nk=nch, T=Th, n0=Th,
                                    rhs_keys=lambda k, k0h_=k0h: rhsk4(k0h_ + k),
                                    lhsT_keys=lambda k, b_=buf: [f"wk{b_}"],
                                    tag="l1f_w2mm", open_group=(half == 0),
                                    close_group=(half == 2))
                hk = "a" if th == 0 else "b"
                P.emit("dve", lambda e, og_=og, p=ps, t0_=t0:
                       e.scalar_tensor_tensor(S["xs"][:, og_, 1 + t0_:1 + t0_ + Th],
                                              p[:, Th:2 * Th],
                                              S["cols"][:, 92 + og_:93 + og_],
                                              S["xs"][:, og_, 1 + t0_:1 + t0_ + Th],
                                              ALU.add, ALU.add),
                       reads=pkeys + [f"xs{og}{hk}", "colsL"],
                       writes=[f"xs{og}{hk}"], tag="l1f_w2ev")
            if L1F_STOP == "half":
                return

    def build(self):
        self.alloc()
        P, S, D = self.P, self.S, self.D
        T = self.T
        self.Tcur = T
        self.emit_precast()
        for n in ("idn", "causal", "band384", "ones_ln",
                  "blockones", "qcs", "qss", "kcs", "kss", "blockones32",
                  "rot32"):
            self.load_const(n)
        for src, sl in (("b0qkv", (0, 24)), ("b0o", (24, 32)), ("mur", (32, 40)),
                        ("muk", (40, 48)), ("fqs", (64, 70)), ("fks", (70, 76))):
            P.dma("sp", lambda e, s=src, a=sl: e.dma_start(
                out=S["cols"][:, a[0]:a[1]], in_=D[s]),
                writes=["colsA"], tag=f"ld_{src}")
        P.emit("dve", lambda e: e.tensor_scalar_mul(S["cols"][:, 146:154],
                                                    S["cols"][:, 0:8], 0.125),
               reads=["colsA"], writes=["colsA"], tag="qbias8")
        P.emit("dve", lambda e: e.memset(S["cols"][:, 144:145], 1e-5),
               writes=["colsE"], tag="eps1")
        P.emit("dve", lambda e: e.memset(S["cols"][:, 145:146], 1e-24),
               writes=["colsE"], tag="eps2")
        P.emit("dve", lambda e: e.tensor_scalar(S["cols"][:, 48:56],
                                                S["cols"][:, 32:40], -1.0, 1.0,
                                                ALU.mult, ALU.add),
               reads=["colsA"], writes=["colsA"], tag="omr")
        P.emit("dve", lambda e: e.tensor_scalar(S["cols"][:, 56:64],
                                                S["cols"][:, 40:48], -1.0, 1.0,
                                                ALU.mult, ALU.add),
               reads=["colsA"], writes=["colsA"], tag="omk")
        P.emit("dve", lambda e: e.memset(S["xs"][:, :, 0:1], 0.0),
               writes=[f"xs{c}" for c in range(8)], tag="zerocol")
        for s in range(self.nseq):
            P.dma("sp", lambda e, s_=s: e.dma_start(
                out=S["xs"][:, :, 1:T + 1],
                in_=D["x_in"][s_].rearrange("(c p) t -> p c t", p=128)),
                reads=[], writes=[f"xs{c}" for c in range(8)], tag="ldx")
            for b in self.blocks:
                (self.l0_attn, self.l0_ffn, self.l1_attn, self.l1_ffn)[b]()
            P.dma("sp", lambda e, s_=s: e.dma_start(
                out=D["x_out"][s_].rearrange("(c p) t -> p c t", p=128),
                in_=S["xs"][:, :, 1:T + 1]),
                reads=[f"xs{c}" for c in range(8)], writes=[], tag="stx",
                is_output=True)
        self.P.materialize_and_run(self.stack)
        self.stack.close()
        return self.nc



def host_inputs(full, T=2048):
    h = {}
    h["w0qkvT"] = np.ascontiguousarray(full["L0_attn_W_qkv"].T)
    h["b0qkv"] = np.ascontiguousarray(full["L0_attn_b_qkv"].reshape(24, 128).T)
    h["w0oT"] = np.ascontiguousarray(full["L0_attn_W_o"].T)
    h["b0o"] = np.ascontiguousarray(full["L0_attn_b_o"].reshape(8, 128).T)
    h["w0rT"] = np.ascontiguousarray(full["L0_ffn_W_r"].T)
    h["w0kT"] = np.ascontiguousarray(full["L0_ffn_W_k"].T)
    h["w0vT"] = np.ascontiguousarray(full["L0_ffn_W_v"].T)
    h["mur"] = np.ascontiguousarray(full["L0_ffn_mu_r"].reshape(8, 128).T)
    h["muk"] = np.ascontiguousarray(full["L0_ffn_mu_k"].reshape(8, 128).T)
    h["w1qkvT"] = np.ascontiguousarray(full["L1_attn_w_qkv"].T)
    h["w1oT"] = np.ascontiguousarray(full["L1_attn_w_out"].T)
    h["fWqkv"] = np.ascontiguousarray(full["L1_ffn_Wqkv"])
    h["fqs"] = np.ascontiguousarray(
        np.concatenate([full["L1_ffn_q_scale"].T, full["L1_ffn_q_scale"].T], 0))
    h["fks"] = np.ascontiguousarray(
        np.concatenate([full["L1_ffn_k_scale"].T, full["L1_ffn_k_scale"].T], 0))
    h["fWo"] = np.ascontiguousarray(full["L1_ffn_Wo"])
    h["flng"] = np.ascontiguousarray(
        full["L1_ffn_ln_g"].reshape(6, 8, 128).transpose(0, 2, 1))
    h["flnb"] = np.ascontiguousarray(
        full["L1_ffn_ln_b"].reshape(6, 8, 128).transpose(0, 2, 1))
    h["fW1"] = np.ascontiguousarray(full["L1_ffn_W1"])
    b1 = full["L1_ffn_b1"]

    def colize22(v):
        out = np.zeros((6, 128, 22), dtype=np.float32)
        for l in range(6):
            x = np.zeros(22 * 128, dtype=np.float32)
            x[:2730] = v[l]
            out[l] = x.reshape(22, 128).T
        return out

    h["fb1a"] = colize22(b1[:, :2730])
    h["fb1g"] = colize22(b1[:, 2730:])
    h["fW2"] = np.ascontiguousarray(full["L1_ffn_W2"])
    h["fb2"] = np.ascontiguousarray(
        full["L1_ffn_b2"].reshape(6, 8, 128).transpose(0, 2, 1))
    for k, v in host_consts(T).items():
        h[k] = v
    for k in h:
        if h[k].dtype == np.float64:
            raise ValueError(k)
    return h


def kernel(**inputs):
    T, nseq, ncores = 2048, 4, 8
    shared = host_inputs(inputs, T)
    x = np.ascontiguousarray(
        np.asarray(inputs["L0_attn_x"]).transpose(0, 2, 1))
    core = Core(T, nseq, blocks=(0, 1, 2, 3))
    nc = core.build()
    in_maps = []
    for c in range(ncores):
        m = dict(shared)
        m["x_in"] = x[c * nseq:(c + 1) * nseq]
        in_maps.append(m)
    res = run_bass_kernel_spmd(nc, in_maps, core_ids=list(range(ncores)))
    out = np.empty((32, T, 1024), dtype=np.float32)
    for c in range(ncores):
        o = np.asarray(res.results[c]["x_out"])
        out[c * nseq:(c + 1) * nseq] = o.transpose(0, 2, 1)
    return out
